# version 02
# baseline (speedup 1.0000x reference)
import numpy as np
import ml_dtypes
import concourse.bass as bass
import concourse.mybir as mybir
from concourse.bass_utils import run_bass_kernel_spmd
from concourse.replica_groups import maybe_share_collective_output_space
from contextlib import ExitStack

F32 = mybir.dt.float32
BF16 = mybir.dt.bfloat16
I8 = mybir.dt.int8
AF = mybir.ActivationFunctionType
ALU = mybir.AluOpType
AXX = mybir.AxisListType.X

NCORES = 8
T = 1024
D = 2048
DC = 16
SEQ = 8192
INV_HD0 = 0.08838834764831845
INV_HD1 = 0.03125
LN_EPS = 1e-5

STAGES = 99
TAPS = ()


class Res:
    __slots__ = ("w", "rs")

    def __init__(self):
        self.w = None
        self.rs = []

    def r_waits(self):
        return [self.w] if self.w else []

    def w_waits(self):
        out = list(self.rs)
        if self.w:
            out.append(self.w)
        return out

    def note_read(self, evt):
        if evt:
            self.rs.append(evt)

    def note_write(self, evt):
        self.w = evt
        self.rs = []


def rwait(*rr):
    out = []
    for r in rr:
        out += r.r_waits()
    return out


def wwait(*rr):
    out = []
    for r in rr:
        out += r.w_waits()
    return out


class Arena:
    def __init__(self):
        self.regions = []

    def region(self, s, e):
        r = Res()
        pend, keep = [], []
        for (s0, e0, r0) in self.regions:
            if s0 < e and s < e0:
                pend += r0.w_waits()
            else:
                keep.append((s0, e0, r0))
        r.rs = pend
        self.regions = keep + [(s, e, r)]
        return r


def dedup(waits):
    m = {}
    for w in waits:
        if w is None:
            continue
        s, v = w
        if s not in m or v > m[s]:
            m[s] = v
    return tuple(sorted(m.items()))


class Prog:
    def __init__(self):
        self.ops = {k: [] for k in "TSVGY"}
        self.cnt = {}

    def bump(self, s, by=1):
        self.cnt[s] = self.cnt.get(s, 0) + by
        return self.cnt[s]

    def emit(self, ek, fn, waits, inc, by):
        self.ops[ek].append((dedup(waits), fn, inc, by))


def _rope_tables():
    scale = (np.arange(0, 128, 2).astype(np.float32) / np.float32(128.0)).astype(np.float32)
    inv_freq = (np.float32(1.0) / (np.float32(10000.0) ** scale)).astype(np.float32)
    t = np.arange(SEQ, dtype=np.float32)
    ang = (t[:, None] * inv_freq[None, :]).astype(np.float32)
    c = np.cos(ang).astype(np.float32)
    s = np.sin(ang).astype(np.float32)
    C = np.empty((128, SEQ), np.float32)
    S2 = np.empty((128, SEQ), np.float32)
    for i in range(64):
        C[2 * i] = c[:, i]
        C[2 * i + 1] = c[:, i]
        S2[2 * i] = -s[:, i]
        S2[2 * i + 1] = s[:, i]
    return C, S2


def _col128(v, dtype=np.float32):
    v = np.asarray(v).astype(dtype)
    n = v.shape[0] // 128
    return np.ascontiguousarray(v.reshape(n, 128).T)


def _tile_go(wT, nsl, W):
    c = wT.shape[0] // 128
    return np.ascontiguousarray(wT.reshape(c, 128, nsl, W).transpose(2, 1, 0, 3))


def _tile_dn(wT, nsl, cc):
    O = wT.shape[1]
    return np.ascontiguousarray(wT.reshape(nsl, cc, 128, O).transpose(0, 2, 1, 3))


def _prep(inputs):
    f32 = np.float32
    bf16 = ml_dtypes.bfloat16
    x = np.asarray(inputs["L0_attn_x"])
    sh = {}
    sh["wqT"] = _tile_go(np.asarray(inputs["L0_attn_wq"]).T, 4, 512)
    sh["wkT"] = _tile_go(np.asarray(inputs["L0_attn_wk"]).T, 4, 512)
    sh["wvT"] = _tile_go(np.asarray(inputs["L0_attn_wv"]).T, 4, 512)
    sh["woT"] = _tile_go(np.asarray(inputs["L0_attn_wo"]).T, 4, 512)
    sh["wg8"] = _tile_go(np.asarray(inputs["L0_ffn_w_gate"]), 16, 512)
    sh["wu8"] = _tile_go(np.asarray(inputs["L0_ffn_w_up"]), 16, 512)
    sh["wd8"] = _tile_dn(np.asarray(inputs["L0_ffn_w_down"]), 16, 4)
    sh["sgate"] = _col128(inputs["L0_ffn_s_gate"])
    sh["sup"] = _col128(inputs["L0_ffn_s_up"])
    sh["sdown"] = _col128(inputs["L0_ffn_s_down"])
    for nm, key in (("w1qT", "L1_attn_Wq"), ("w1kT", "L1_attn_Wk"),
                    ("w1vT", "L1_attn_Wv"), ("w1oT", "L1_attn_Wo")):
        sh[nm] = _tile_go(np.asarray(inputs[key]).T, 4, 512)
    for nm, key in (("b1q", "L1_attn_bq"), ("b1k", "L1_attn_bk"),
                    ("b1v", "L1_attn_bv"), ("b1o", "L1_attn_bo")):
        sh[nm] = _col128(np.asarray(inputs[key]).astype(f32))
    sh["gw"] = np.ascontiguousarray(np.asarray(inputs["L1_ffn_gate_weight"]).T)
    sh["esb"] = np.asarray(inputs["L1_ffn_e_score_correction_bias"]).astype(f32).reshape(1, 16)
    mgT = np.asarray(inputs["L1_ffn_gate_w"]).transpose(0, 2, 1)
    muT = np.asarray(inputs["L1_ffn_up_w"]).transpose(0, 2, 1)
    mdT = np.asarray(inputs["L1_ffn_down_w"]).transpose(0, 2, 1)
    sh["mgT"] = np.ascontiguousarray(mgT.reshape(16, 16, 128, 4, 256).transpose(0, 3, 2, 1, 4))
    sh["muT"] = np.ascontiguousarray(muT.reshape(16, 16, 128, 4, 256).transpose(0, 3, 2, 1, 4))
    sh["mdT"] = np.ascontiguousarray(mdT.reshape(16, 8, 128, 8, 256).transpose(0, 3, 2, 1, 4))
    sh["shgT"] = _tile_go(np.asarray(inputs["L1_ffn_sh_gate_w"]).T, 4, 512)
    sh["shuT"] = _tile_go(np.asarray(inputs["L1_ffn_sh_up_w"]).T, 4, 512)
    sh["shdT"] = _tile_dn(np.asarray(inputs["L1_ffn_sh_down_w"]).T, 4, 4)
    sh["identb"] = np.eye(128, dtype=np.float32).astype(bf16)
    sh["identf"] = np.eye(128, dtype=np.float32)
    perm = np.zeros((128, 128), np.float32)
    for i in range(128):
        perm[i, i ^ 1] = 1.0
    sh["permf"] = perm
    sh["onescolb"] = np.ones((128, 1), np.float32).astype(bf16)
    sh["onescolf"] = np.ones((128, 1), np.float32)
    sh["onesrowf"] = np.ones((1, 128), np.float32)
    sh["neg20"] = np.full((128, 1), -20.0, np.float32)

    C, S2 = _rope_tables()
    maps = []
    for c in range(NCORES):
        m = dict(sh)
        m["x_tm"] = np.ascontiguousarray(x[0, c * T:(c + 1) * T, :]).astype(f32)
        m["ropec"] = np.ascontiguousarray(C[:, c * T:(c + 1) * T])
        m["ropes"] = np.ascontiguousarray(S2[:, c * T:(c + 1) * T])
        maps.append(m)
    return maps


_NC_CACHE = {}


def build_nc(stages=None, taps=None):
    stages = STAGES if stages is None else stages
    taps = TAPS if taps is None else taps
    key = (stages, tuple(taps))
    if key in _NC_CACHE:
        return _NC_CACHE[key]

    nc = bass.Bass("TRN2", target_bir_lowering=False)
    P = Prog()

    def din(name, shape, dt):
        return nc.dram_tensor(name, shape, dt, kind="ExternalInput").ap()

    x_tm = din("x_tm", [T, D], F32)
    wqT = din("wqT", [4, 128, 16, 512], F32)
    wkT = din("wkT", [4, 128, 16, 512], F32)
    wvT = din("wvT", [4, 128, 16, 512], F32)
    woT = din("woT", [4, 128, 16, 512], F32)
    wg8 = din("wg8", [16, 128, 16, 512], I8)
    wu8 = din("wu8", [16, 128, 16, 512], I8)
    wd8 = din("wd8", [16, 128, 4, 2048], I8)
    sgate = din("sgate", [128, 64], F32)
    sup = din("sup", [128, 64], F32)
    sdown = din("sdown", [128, 16], F32)
    w1qT = din("w1qT", [4, 128, 16, 512], BF16)
    w1kT = din("w1kT", [4, 128, 16, 512], BF16)
    w1vT = din("w1vT", [4, 128, 16, 512], BF16)
    w1oT = din("w1oT", [4, 128, 16, 512], BF16)
    b1q = din("b1q", [128, 16], F32)
    b1k = din("b1k", [128, 16], F32)
    b1v = din("b1v", [128, 16], F32)
    b1o = din("b1o", [128, 16], F32)
    gw = din("gw", [D, 16], F32)
    esb = din("esb", [1, 16], F32)
    mgT = din("mgT", [16, 4, 128, 16, 256], F32)
    muT = din("muT", [16, 4, 128, 16, 256], F32)
    mdT = din("mdT", [16, 8, 128, 8, 256], F32)
    shgT = din("shgT", [4, 128, 16, 512], F32)
    shuT = din("shuT", [4, 128, 16, 512], F32)
    shdT = din("shdT", [4, 128, 4, 2048], F32)
    ropec = din("ropec", [128, T], F32)
    ropes = din("ropes", [128, T], F32)
    identb_h = din("identb", [128, 128], BF16)
    identf_h = din("identf", [128, 128], F32)
    permf_h = din("permf", [128, 128], F32)
    onescolb_h = din("onescolb", [128, 1], BF16)
    onescolf_h = din("onescolf", [128, 1], F32)
    onesrowf_h = din("onesrowf", [1, 128], F32)
    neg20_h = din("neg20", [128, 1], F32)

    y_out = nc.dram_tensor("y", [T, D], F32, kind="ExternalOutput").ap()

    tap_outs = {}
    for tp in taps:
        if tp in ("xn", "xn1", "attn0", "attn1"):
            tap_outs[tp] = nc.dram_tensor("tap_" + tp, [128, DC, T], BF16, kind="ExternalOutput").ap()
        elif tp in ("x_ffn", "x_l0", "x_moe", "x_sh"):
            tap_outs[tp] = nc.dram_tensor("tap_" + tp, [128, DC, T], F32, kind="ExternalOutput").ap()
        elif tp in ("qsc", "qsc1"):
            tap_outs[tp] = nc.dram_tensor("tap_" + tp, [D, T], BF16, kind="ExternalOutput").ap()
        elif tp in ("kown", "vown", "kown1", "vown1"):
            tap_outs[tp] = nc.dram_tensor("tap_" + tp, [DC * 8 * 128, 128], BF16, kind="ExternalOutput").ap()
        elif tp == "wtm":
            tap_outs[tp] = nc.dram_tensor("tap_wtm", [128, 8, 16], F32, kind="ExternalOutput").ap()
        elif tp == "rsc":
            tap_outs[tp] = nc.dram_tensor("tap_rsc", [128, 8, 16], F32, kind="ExternalOutput").ap()
        elif tp == "esbb":
            tap_outs[tp] = nc.dram_tensor("tap_esbb", [128, 16], F32, kind="ExternalOutput").ap()

    kown0 = nc.dram_tensor("kown0", [DC * 8 * 128, 128], BF16, kind="Internal")
    vown0 = nc.dram_tensor("vown0", [DC * 8 * 128, 128], BF16, kind="Internal")
    kown1 = nc.dram_tensor("kown1", [DC * 8 * 128, 128], BF16, kind="Internal")
    vown1 = nc.dram_tensor("vown1", [DC * 8 * 128, 128], BF16, kind="Internal")
    qsc = nc.dram_tensor("qsc", [D, T], BF16, kind="Internal")
    qsc1 = nc.dram_tensor("qsc1", [D, T], BF16, kind="Internal")
    agspace = maybe_share_collective_output_space("AllGather", [list(range(NCORES))])
    KG0 = nc.dram_tensor("KG0", [NCORES * DC * 8 * 128, 128], BF16, kind="Internal", addr_space=agspace)
    VG0 = nc.dram_tensor("VG0", [NCORES * DC * 8 * 128, 128], BF16, kind="Internal", addr_space=agspace)
    KG1 = nc.dram_tensor("KG1", [NCORES * DC * 8 * 128, 128], BF16, kind="Internal", addr_space=agspace)
    VG1 = nc.dram_tensor("VG1", [NCORES * DC * 8 * 128, 128], BF16, kind="Internal", addr_space=agspace)

    def bview0(t_):
        return t_.ap().rearrange("(dc p tc) x -> dc p tc x", dc=DC, p=128, tc=8)

    def bview1(t_):
        return t_.ap().rearrange("(tc p dc) x -> dc p tc x", tc=8, p=128, dc=DC)

    def gview0(t_):
        return t_.ap().rearrange("(r dc p tc) x -> r dc p tc x", r=NCORES, dc=DC, p=128, tc=8)

    def gview1(t_):
        return t_.ap().rearrange("(r tc p dc) x -> r tc p dc x", r=NCORES, tc=8, p=128, dc=DC)

    kown0v = bview0(kown0)
    vown0v = bview0(vown0)
    kown1v = bview1(kown1)
    vown1v = bview1(vown1)
    KG0v = gview0(KG0)
    VG0v = gview0(VG0)
    KG1v = gview1(KG1)
    VG1v = gview1(VG1)
    qsc1v = qsc1.ap().rearrange("(c p) t -> p c t", p=128)

    with ExitStack() as stack:
        def sb(name, shape, dt):
            return stack.enter_context(nc.sbuf_tensor(name, shape, dt))

        def psmake(name, shape):
            return stack.enter_context(nc.psum_tensor(name, shape, F32))

        xres = sb("xres", [128, DC, T], F32)
        aA = sb("aA", [128, DC, T], BF16)
        aC = sb("aC", [128, 16384], BF16)
        aW = sb("aW", [128, 16384], BF16)
        eA = sb("eA", [128, 4096], F32)
        srow = sb("srow", [1, 2048], F32)
        identb = sb("identb_s", [128, 128], BF16)
        identf = sb("identf_s", [128, 128], F32)
        permf = sb("permf_s", [128, 128], F32)
        onescolb = sb("onescolb_s", [128, 1], BF16)
        onescolf = sb("onescolf_s", [128, 1], F32)
        onesrowf = sb("onesrowf_s", [1, 128], F32)
        neg20 = sb("neg20_s", [128, 1], F32)
        ropec_s = sb("ropec_s", [128, T], F32)
        ropes_s = sb("ropes_s", [128, T], F32)
        sgate_s = sb("sgate_s", [128, 64], F32)
        sup_s = sb("sup_s", [128, 64], F32)
        sdown_s = sb("sdown_s", [128, 16], F32)
        b1q_s = sb("b1q_s", [128, 16], F32)
        b1k_s = sb("b1k_s", [128, 16], F32)
        b1v_s = sb("b1v_s", [128, 16], F32)
        b1o_s = sb("b1o_s", [128, 16], F32)
        gwf = sb("gwf", [128, 16, 16], F32)
        esb_s = sb("esb_s", [1, 16], F32)
        esbb = sb("esbb", [128, 16], F32)
        r_sc = sb("r_sc", [128, 8, 16], F32)
        r_s4 = sb("r_s4", [128, 8, 16], F32)
        r_pr = sb("r_pr", [128, 8, 4, 6], F32)
        r_gt = sb("r_gt", [128, 8, 4], F32)
        r_m2 = sb("r_m2", [128, 8, 1], F32)
        r_gm = sb("r_gm", [128, 8, 16], F32)
        r_wk = sb("r_wk", [128, 8, 16], F32)
        r_ge = sb("r_ge", [128, 8, 16], F32)
        r_ti = sb("r_ti", [128, 8, 1], F32)
        r_sel = sb("r_sel", [128, 8, 16], F32)
        r_w = sb("r_w", [128, 8, 16], F32)

        psA = psmake("psA", [128, 1024])
        psB = psmake("psB", [128, 1024])
        psC = psmake("psC", [128, 1024])
        psD = psmake("psD", [128, 512])
        psE = psmake("psE", [128, 512])

        LANES_Y = ["Lcn", "Lx0", "Lx1", "Ls0", "Ls1", "Ls2", "Ls3", "Ls4", "Ls5",
                   "Lout", "Ltap"]
        LANES_G = ["Lg0", "Lg1", "Lg2", "Lg3", "Lg4", "Lg5"]
        SEMS = {}
        for nm in ["T", "S", "V", "G", "cc"] + LANES_Y + LANES_G:
            SEMS[nm] = stack.enter_context(nc.semaphore("sem" + nm))

        def eT(fn, waits=(), inc=False):
            c = P.bump("T") if inc else None
            P.emit("T", fn, waits, "T" if inc else None, 1)
            return c

        def eS(fn, waits=(), inc=True):
            c = P.bump("S") if inc else None
            P.emit("S", fn, waits, "S" if inc else None, 1)
            return c

        def eV(fn, waits=(), inc=True):
            c = P.bump("V") if inc else None
            P.emit("V", fn, waits, "V" if inc else None, 1)
            return c

        def eG(fn, waits=(), inc=True):
            c = P.bump("G") if inc else None
            P.emit("G", fn, waits, "G" if inc else None, 1)
            return c

        def dY(fn, lane, waits=()):
            c = P.bump(lane, 16)
            P.emit("Y", fn, waits, lane, 16)
            return (lane, c)

        def dG(fn, lane, waits=()):
            c = P.bump(lane, 16)
            P.emit("G", fn, waits, lane, 16)
            return (lane, c)

        def cG(fn, waits=()):
            c = P.bump("cc", 1)
            P.emit("G", fn, waits, "cc", 1)
            return c

        def yWait(waits):
            P.emit("Y", lambda e: None, waits, None, 1)

        RX = [Res() for _ in range(DC)]
        RA = [Res() for _ in range(DC)]
        RE = [Res() for _ in range(4)]
        RSR = [Res() for _ in range(4)]
        RPS = {k: Res() for k in ("A0", "A1", "B0", "B1", "C0", "C1", "D", "E")}
        arC = Arena()
        arW = Arena()
        r_kown0, r_vown0, r_kown1, r_vown1 = Res(), Res(), Res(), Res()
        r_qsc, r_qsc1 = Res(), Res()
        r_KG0, r_VG0, r_KG1, r_VG1 = Res(), Res(), Res(), Res()
        r_route = Res()
        r_wtm = Res()

        PAIR_AP = {"A": psA, "B": psB, "C": psC}

        def pair_res(p):
            return [RPS[p + "0"], RPS[p + "1"]]

        BANK_AP = {
            "A0": psA[:, 0:512], "A1": psA[:, 512:1024],
            "B0": psB[:, 0:512], "B1": psB[:, 512:1024],
            "C0": psC[:, 0:512], "C1": psC[:, 512:1024],
            "D": psD[:, :], "E": psE[:, :],
        }

        e_slot = [eA[:, i * 1024:(i + 1) * 1024] for i in range(4)]
        sr_slot = [srow[0:1, i * 512:(i + 1) * 512] for i in range(4)]

        consts = Res()
        for dst, src in ((identb, identb_h), (identf, identf_h), (permf, permf_h),
                         (onescolb, onescolb_h), (onescolf, onescolf_h),
                         (onesrowf, onesrowf_h), (neg20, neg20_h),
                         (ropec_s, ropec), (ropes_s, ropes), (sgate_s, sgate),
                         (sup_s, sup), (sdown_s, sdown), (b1q_s, b1q), (b1k_s, b1k),
                         (b1v_s, b1v), (b1o_s, b1o), (esb_s, esb)):
            dY(lambda e, d=dst, s=src: e.dma_start(out=d[...], in_=s[...]), "Lcn")
        cgw = dY(lambda e: e.dma_start(out=gwf[:, :, :], in_=gw.rearrange("(c p) n -> p c n", p=128)), "Lcn")
        consts.note_write(cgw)
        CW = consts.r_waits()

        ROT6 = ["A0", "A1", "B0", "B1", "C0", "C1"]
        rot_i = [0]

        def take_slot():
            k = ROT6[rot_i[0] % 6]
            rot_i[0] += 1
            return k

        xv = x_tm.rearrange("(tc p) d -> tc p d", p=128)
        stg = [eA[:, 0:2048], eA[:, 2048:4096]]
        for tc in range(8):
            sl = tc % 2
            ld = dY(lambda e, sl=sl, tc=tc: e.dma_start(out=stg[sl][...], in_=xv[tc]),
                    "Lx0" if sl == 0 else "Lx1",
                    waits=wwait(RE[2 * sl], RE[2 * sl + 1]))
            RE[2 * sl].note_write(ld)
            RE[2 * sl + 1].note_write(ld)
            for g in range(4):
                bk = take_slot()
                bap = BANK_AP[bk]
                waits = [ld] + wwait(RPS[bk]) + CW
                cT = None
                for j in range(4):
                    dc = g * 4 + j
                    cT = eT(lambda e, bap=bap, j=j, sl=sl, dc=dc:
                            e.transpose(bap[:, j * 128:(j + 1) * 128],
                                        stg[sl][:, dc * 128:(dc + 1) * 128], identf[...]),
                            waits=waits if j == 0 else (), inc=(j == 3))
                RPS[bk].note_write(("T", cT))
                RE[2 * sl].note_read(("T", cT))
                RE[2 * sl + 1].note_read(("T", cT))
                cV = eV(lambda e, bap=bap, g=g, tc=tc:
                        e.tensor_copy(xres[:, g * 4:(g + 1) * 4, tc * 128:(tc + 1) * 128],
                                      bap.rearrange("p (a b) -> p a b", a=4)),
                        waits=[("T", cT)] + wwait(*RX[g * 4:(g + 1) * 4]))
                RPS[bk].note_read(("V", cV))
                RE[2 * sl].note_read(("V", cV))
                RE[2 * sl + 1].note_read(("V", cV))
                for dc in range(g * 4, g * 4 + 4):
                    RX[dc].note_write(("V", cV))

        def emit_ln(layer):
            xb_reg = [arC.region(0, 1024), arC.region(1024, 2048)]
            sq_reg = [arC.region(2048, 3072), arC.region(3072, 4096)]
            xb_ap = [aC[:, 0:1024], aC[:, 1024:2048]]
            sq_ap = [aC[:, 2048:3072], aC[:, 3072:4096]]
            sum_banks = [("D", psD[0:1, :]), ("E", psE[0:1, :])]
            ssq_banks = [("C0", psC[0:1, 0:512]), ("C1", psC[0:1, 512:1024])]
            for dc in range(DC):
                sl = dc % 2
                c1 = eS(lambda e, sl=sl, dc=dc: e.copy(xb_ap[sl][...], xres[:, dc, :]),
                        waits=rwait(RX[dc]) + wwait(xb_reg[sl]))
                xb_reg[sl].note_write(("S", c1))
                RX[dc].note_read(("S", c1))
                c2 = eS(lambda e, sl=sl, dc=dc: e.activation(sq_ap[sl][...], xres[:, dc, :], AF.Square),
                        waits=rwait(RX[dc]) + wwait(sq_reg[sl]))
                sq_reg[sl].note_write(("S", c2))
                RX[dc].note_read(("S", c2))
                cT = None
                for th in range(2):
                    bk, bap = sum_banks[th]
                    w = [("S", c1)] + (wwait(RPS[bk]) if dc == 0 else [])
                    eT(lambda e, bap=bap, sl=sl, th=th, dc=dc:
                       e.matmul(bap, onescolb[...], xb_ap[sl][:, th * 512:(th + 1) * 512],
                                start=(dc == 0), stop=(dc == DC - 1)),
                       waits=w)
                    bk2, bap2 = ssq_banks[th]
                    w = [("S", c2)] + (wwait(RPS[bk2]) if dc == 0 else [])
                    cT = eT(lambda e, bap2=bap2, sl=sl, th=th, dc=dc:
                            e.matmul(bap2, onescolb[...], sq_ap[sl][:, th * 512:(th + 1) * 512],
                                     start=(dc == 0), stop=(dc == DC - 1)),
                            waits=w, inc=(th == 1))
                if dc == DC - 1:
                    for bk in ("D", "E", "C0", "C1"):
                        RPS[bk].note_write(("T", cT))
                xb_reg[sl].note_read(("T", cT))
                sq_reg[sl].note_read(("T", cT))
            cm0 = eV(lambda e: e.tensor_scalar(sr_slot[0][...], psD[0:1, :], 1.0 / D, None, ALU.mult),
                     waits=rwait(RPS["D"]) + wwait(RSR[0]))
            cm1 = eV(lambda e: e.tensor_scalar(sr_slot[1][...], psE[0:1, :], 1.0 / D, None, ALU.mult),
                     waits=rwait(RPS["E"]) + wwait(RSR[1]))
            cq = eV(lambda e: e.tensor_scalar(srow[0:1, 1024:2048], psC[0:1, 0:1024], 1.0 / D, None, ALU.mult),
                    waits=rwait(RPS["C0"], RPS["C1"]) + wwait(RSR[2], RSR[3]))
            ct0 = eV(lambda e: e.tensor_tensor(psD[0:1, :], sr_slot[0][...], sr_slot[0][...], ALU.mult),
                     waits=[("V", cm0)])
            ct1 = eV(lambda e: e.tensor_tensor(psE[0:1, :], sr_slot[1][...], sr_slot[1][...], ALU.mult),
                     waits=[("V", cm1)])
            cv0 = eV(lambda e: e.tensor_tensor(srow[0:1, 1024:1536], srow[0:1, 1024:1536], psD[0:1, :], ALU.subtract),
                     waits=[("V", ct0), ("V", cq)])
            cv1 = eV(lambda e: e.tensor_tensor(srow[0:1, 1536:2048], srow[0:1, 1536:2048], psE[0:1, :], ALU.subtract),
                     waits=[("V", ct1), ("V", cq)])
            ce = eV(lambda e: e.tensor_scalar(srow[0:1, 1024:2048], srow[0:1, 1024:2048], LN_EPS, None, ALU.add),
                    waits=[("V", cv0), ("V", cv1)])
            cr0 = eV(lambda e: e.reciprocal(psD[0:1, :], srow[0:1, 1024:1536]), waits=[("V", ce)])
            cr1 = eV(lambda e: e.reciprocal(psE[0:1, :], srow[0:1, 1536:2048]), waits=[("V", ce)])
            cs0 = eS(lambda e: e.activation(srow[0:1, 1024:1536], psD[0:1, :], AF.Sqrt), waits=[("V", cr0)])
            cs1 = eS(lambda e: e.activation(srow[0:1, 1536:2048], psE[0:1, :], AF.Sqrt), waits=[("V", cr1)])
            cmr = eV(lambda e: e.tensor_tensor(srow[0:1, 0:1024], srow[0:1, 0:1024], srow[0:1, 1024:2048], ALU.mult),
                     waits=[("S", cs0), ("S", cs1)])
            RPS["D"].note_write(("V", cr0))
            RPS["E"].note_write(("V", cr1))
            RPS["D"].note_read(("S", cs0))
            RPS["E"].note_read(("S", cs1))
            for r in (RSR[0], RSR[1]):
                r.note_write(("V", cmr))
            for r in (RSR[2], RSR[3]):
                r.note_write(("V", cmr))
            w_ab = [("S", cs1), ("V", cmr)] + wwait(RPS["A0"], RPS["A1"], RPS["B0"], RPS["B1"]) + CW
            cb2 = None
            for th in range(2):
                eT(lambda e, th=th: e.matmul(psA[:, th * 512:(th + 1) * 512], onesrowf[...],
                                             srow[0:1, 1024 + th * 512:1536 + th * 512],
                                             start=True, stop=True),
                   waits=w_ab if th == 0 else ())
                cb2 = eT(lambda e, th=th: e.matmul(psB[:, th * 512:(th + 1) * 512], onesrowf[...],
                                                   srow[0:1, th * 512:(th + 1) * 512],
                                                   start=True, stop=True),
                         inc=(th == 1))
            for k in ("A0", "A1", "B0", "B1"):
                RPS[k].note_write(("T", cb2))
            for r in RSR:
                r.note_read(("T", cb2))
            if layer == 1:
                cbe = eT(lambda e: e.matmul(psD[:, 0:16], onesrowf[...], esb_s[0:1, :], start=True, stop=True),
                         waits=wwait(RPS["D"]) + CW, inc=True)
                cse = eS(lambda e: e.copy(esbb[:, :], psD[:, 0:16]), waits=[("T", cbe)])
                RPS["D"].note_write(("T", cbe))
                RPS["D"].note_read(("S", cse))
                r_route.note_write(("S", cse))
            for dc in range(DC):
                tsl = dc % 2
                c1 = eV(lambda e, tsl=tsl, dc=dc: e.tensor_tensor(e_slot[tsl][...], xres[:, dc, :], psA[:, :], ALU.mult),
                        waits=rwait(RX[dc]) + [("T", cb2)] + wwait(RE[tsl]))
                RE[tsl].note_write(("V", c1))
                RX[dc].note_read(("V", c1))
                if layer == 0:
                    c2 = eV(lambda e, tsl=tsl, dc=dc: e.tensor_tensor(aA[:, dc, :], e_slot[tsl][...], psB[:, :], ALU.subtract),
                            waits=[("V", c1)] + wwait(RA[dc]))
                    RA[dc].note_write(("V", c2))
                    RE[tsl].note_read(("V", c2))
                else:
                    fsl = 2 + (dc % 2)
                    c2 = eV(lambda e, fsl=fsl, tsl=tsl: e.tensor_tensor(e_slot[fsl][...], e_slot[tsl][...], psB[:, :], ALU.subtract),
                            waits=[("V", c1)] + wwait(RE[fsl]))
                    RE[fsl].note_write(("V", c2))
                    RE[tsl].note_read(("V", c2))
                    c3 = eS(lambda e, fsl=fsl, dc=dc: e.copy(aA[:, dc, :], e_slot[fsl][...]),
                            waits=[("V", c2)] + wwait(RA[dc]))
                    RA[dc].note_write(("S", c3))
                    RE[fsl].note_read(("S", c3))
                    w0 = [("V", c2)] + CW
                    if dc == 0:
                        w0 += wwait(RPS["D"])
                    cr = None
                    for tc in range(8):
                        cr = eT(lambda e, fsl=fsl, tc=tc, dc=dc:
                                e.matmul(psD[:, tc * 16:(tc + 1) * 16],
                                         e_slot[fsl][:, tc * 128:(tc + 1) * 128],
                                         gwf[:, dc, :],
                                         start=(dc == 0 and tc == 0), stop=(dc == DC - 1),
                                         skip_group_check=True),
                                waits=w0 if tc == 0 else (), inc=(tc == 7))
                    RE[fsl].note_read(("T", cr))
                    if dc == DC - 1:
                        RPS["D"].note_write(("T", cr))
                for k in ("A0", "A1", "B0", "B1"):
                    RPS[k].note_read(("V", c2))

        if stages >= 1:
            emit_ln(0)

        def tap_sbuf(name, src_ap, waits):
            if name in taps:
                dY(lambda e, s=src_ap, o=tap_outs[name]: e.dma_start(out=o[...], in_=s), "Ltap", waits=waits)

        def tap_dram(name, dram, res):
            if name in taps:
                dY(lambda e, d=dram, o=tap_outs[name]: e.dma_start(out=o[...], in_=d.ap()),
                   "Ltap", waits=rwait(res))

        if stages >= 1:
            tap_sbuf("xn", aA[:, :, :], rwait(*RA))

        WSLOT_OFF = [0, 8192]

        def wslab_ap(off):
            return aW[:, off:off + 8192].rearrange("p (c o) -> p c o", c=16)

        arW_slots = [arW.region(0, 8192), arW.region(8192, 16384)]

        def emit_proj(w_hbm, epilogue):
            pair_rot = ["A", "B", "C"]
            pi = [0]
            for s in range(4):
                sslot = s % 2
                off = WSLOT_OFF[sslot]
                wreg = arW_slots[sslot]
                cl = dG(lambda e, s=s, off=off: e.dma_start(out=wslab_ap(off)[...],
                                                            in_=w_hbm[s]),
                        "Lg0" if sslot == 0 else "Lg1",
                        waits=wwait(wreg))
                wreg.note_write(cl)
                for j in range(4):
                    dc = s * 4 + j
                    pk = pair_rot[pi[0] % 3]
                    pi[0] += 1
                    pap = PAIR_AP[pk]
                    w = [cl] + wwait(*pair_res(pk)) + rwait(*RA)
                    cnt = None
                    for hc in range(DC):
                        for th in range(2):
                            cnt = eT(lambda e, pap=pap, off=off, hc=hc, j=j, th=th:
                                     e.matmul(pap[:, th * 512:(th + 1) * 512],
                                              wslab_ap(off)[:, hc, j * 128:(j + 1) * 128],
                                              aA[:, hc, th * 512:(th + 1) * 512],
                                              start=(hc == 0), stop=(hc == DC - 1)),
                                     waits=w if (hc == 0 and th == 0) else (),
                                     inc=(hc == DC - 1 and th == 1))
                    for r in pair_res(pk):
                        r.note_write(("T", cnt))
                    for r in RA:
                        r.note_read(("T", cnt))
                    wreg.note_read(("T", cnt))
                    epilogue(dc, pk, cnt)

        def kv_regions():
            return {
                "kst": ([arC.region(0, 1024), arC.region(1024, 2048)],
                        [aC[:, 0:1024], aC[:, 1024:2048]]),
                "vst": ([arC.region(2048, 3072), arC.region(3072, 4096)],
                        [aC[:, 2048:3072], aC[:, 3072:4096]]),
                "qst": ([arC.region(4096, 5120), arC.region(5120, 6144)],
                        [aC[:, 4096:5120], aC[:, 5120:6144]]),
                "vstg": ([arC.region(6144, 7168), arC.region(7168, 8192)],
                         [aC[:, 6144:7168].rearrange("p (a b) -> p a b", a=8),
                          aC[:, 7168:8192].rearrange("p (a b) -> p a b", a=8)]),
            }

        def rope_epilogue(dst_regs, dst_aps, store_fn):
            def ep(dc, pk, cnt):
                sl = dc % 2
                pap = PAIR_AP[pk]
                qf = e_slot[sl]
                c1 = eS(lambda e, qf=qf, pap=pap: e.copy(qf[...], pap[:, :]),
                        waits=[("T", cnt)] + wwait(RE[sl]))
                RE[sl].note_write(("S", c1))
                for r in pair_res(pk):
                    r.note_read(("S", c1))
                cp = None
                for th, bk in ((0, "D"), (1, "E")):
                    cp = eT(lambda e, qf=qf, th=th, bk=bk:
                            e.matmul(BANK_AP[bk], permf[...], qf[:, th * 512:(th + 1) * 512],
                                     start=True, stop=True),
                            waits=[("S", c1)] + wwait(RPS[bk]) + CW, inc=(th == 1))
                RPS["D"].note_write(("T", cp))
                RPS["E"].note_write(("T", cp))
                RE[sl].note_read(("T", cp))
                ct1a = eV(lambda e: e.tensor_tensor(e_slot[2][:, 0:512], psD[:, :], ropes_s[:, 0:512], ALU.mult),
                          waits=[("T", cp)] + wwait(RE[2]) + CW)
                ct1b = eV(lambda e: e.tensor_tensor(e_slot[2][:, 512:1024], psE[:, :], ropes_s[:, 512:1024], ALU.mult),
                          waits=[("T", cp)])
                RE[2].note_write(("V", ct1b))
                RPS["D"].note_read(("V", ct1a))
                RPS["E"].note_read(("V", ct1b))
                ct2 = eV(lambda e, qf=qf: e.tensor_tensor(e_slot[3][...], qf[...], ropec_s[:, :], ALU.mult),
                         waits=[("S", c1)] + wwait(RE[3]))
                RE[3].note_write(("V", ct2))
                RE[sl].note_read(("V", ct2))
                cfin = eV(lambda e, sl=sl: e.tensor_tensor(dst_aps[sl][...], e_slot[2][...], e_slot[3][...], ALU.add),
                          waits=[("V", ct1b), ("V", ct2)] + wwait(dst_regs[sl]))
                dst_regs[sl].note_write(("V", cfin))
                RE[2].note_read(("V", cfin))
                RE[3].note_read(("V", cfin))
                store_fn(dc, sl, ("V", cfin))
            return ep

        def blk_store(dst_regs, dst_aps, dram_view, dram_res):
            def store(dc, sl, evt):
                cdma = dY(lambda e, sl=sl, dc=dc: e.dma_start(
                    out=dram_view[dc],
                    in_=dst_aps[sl][...].rearrange("p (tc x) -> p tc x", tc=8)),
                    "Ls0" if sl == 0 else "Ls1",
                    waits=[evt])
                dst_regs[sl].note_read(cdma)
                dram_res.note_write(cdma)
            return store

        def row_store(dst_regs, dst_aps, qdram, dram_res):
            def store(dc, sl, evt):
                cdma = dY(lambda e, sl=sl, dc=dc: e.dma_start(
                    out=qdram[dc * 128:(dc + 1) * 128, :], in_=dst_aps[sl][...]),
                    "Ls4" if sl == 0 else "Ls5",
                    waits=[evt])
                dst_regs[sl].note_read(cdma)
                dram_res.note_write(cdma)
            return store

        def relu_epilogue(bias_col, dst_regs, dst_aps, store_fn):
            def ep(dc, pk, cnt):
                sl = dc % 2
                pap = PAIR_AP[pk]
                c1 = eS(lambda e, sl=sl, pap=pap, dc=dc:
                        e.activation(dst_aps[sl][...], pap[:, :], AF.Relu,
                                     bias=bias_col[:, dc:dc + 1], scale=1.0),
                        waits=[("T", cnt)] + wwait(dst_regs[sl]) + CW)
                dst_regs[sl].note_write(("S", c1))
                for r in pair_res(pk):
                    r.note_read(("S", c1))
                store_fn(dc, sl, ("S", c1))
            return ep

        def vtrans_epilogue(vst_regs, vst_aps, stg_regs, stg_aps, dram_view, dram_res,
                            relu_bias=None):
            def ep(dc, pk, cnt):
                sl = dc % 2
                pap = PAIR_AP[pk]
                if relu_bias is None:
                    c1 = eS(lambda e, sl=sl, pap=pap: e.copy(vst_aps[sl][...], pap[:, :]),
                            waits=[("T", cnt)] + wwait(vst_regs[sl]))
                else:
                    c1 = eS(lambda e, sl=sl, pap=pap, dc=dc:
                            e.activation(vst_aps[sl][...], pap[:, :], AF.Relu,
                                         bias=relu_bias[:, dc:dc + 1], scale=1.0),
                            waits=[("T", cnt)] + wwait(vst_regs[sl]) + CW)
                vst_regs[sl].note_write(("S", c1))
                for r in pair_res(pk):
                    r.note_read(("S", c1))
                ctr3 = ctr7 = None
                for tc in range(8):
                    bk = "D" if tc < 4 else "E"
                    w = [("S", c1)] + (wwait(RPS[bk]) if tc % 4 == 0 else []) + CW
                    cc_ = eT(lambda e, sl=sl, tc=tc, bk=bk:
                             e.matmul(BANK_AP[bk][:, (tc % 4) * 128:(tc % 4 + 1) * 128],
                                      vst_aps[sl][:, tc * 128:(tc + 1) * 128], identb[...],
                                      start=(tc % 4 == 0), stop=(tc % 4 == 3),
                                      skip_group_check=True),
                             waits=w, inc=(tc % 4 == 3))
                    if tc == 3:
                        ctr3 = cc_
                    if tc == 7:
                        ctr7 = cc_
                RPS["D"].note_write(("T", ctr3))
                RPS["E"].note_write(("T", ctr7))
                vst_regs[sl].note_read(("T", ctr7))
                ssl = dc % 2
                c2 = eS(lambda e, ssl=ssl: e.copy(stg_aps[ssl][:, 0:4, :],
                                                  psD[:, :].rearrange("p (a b) -> p a b", a=4)),
                        waits=[("T", ctr3)] + wwait(stg_regs[ssl]))
                c3 = eS(lambda e, ssl=ssl: e.copy(stg_aps[ssl][:, 4:8, :],
                                                  psE[:, :].rearrange("p (a b) -> p a b", a=4)),
                        waits=[("T", ctr7)])
                stg_regs[ssl].note_write(("S", c3))
                RPS["D"].note_read(("S", c2))
                RPS["E"].note_read(("S", c3))
                cdma = dY(lambda e, ssl=ssl, dc=dc: e.dma_start(
                    out=dram_view[dc], in_=stg_aps[ssl][...]),
                    "Ls2" if ssl == 0 else "Ls3",
                    waits=[("S", c3)])
                stg_regs[ssl].note_read(cdma)
                dram_res.note_write(cdma)
            return ep

        if stages >= 2:
            kv = kv_regions()
            emit_proj(wkT, rope_epilogue(kv["kst"][0], kv["kst"][1],
                                         blk_store(kv["kst"][0], kv["kst"][1], kown0v, r_kown0)))
            if stages >= 2.3:
                emit_proj(wvT, vtrans_epilogue(kv["vst"][0], kv["vst"][1],
                                               kv["vstg"][0], kv["vstg"][1], vown0v, r_vown0))
            if stages >= 2.6:
                emit_proj(wqT, rope_epilogue(kv["qst"][0], kv["qst"][1],
                                             row_store(kv["qst"][0], kv["qst"][1], qsc.ap(), r_qsc)))
            if stages >= 2.9:
                ck = cG(lambda e: e.collective_compute(
                    "AllGather", ALU.bypass, replica_groups=[list(range(NCORES))],
                    ins=[kown0.ap().opt()], outs=[KG0.ap().opt()]),
                    waits=rwait(r_kown0))
                r_KG0.note_write(("cc", ck))
                cv = cG(lambda e: e.collective_compute(
                    "AllGather", ALU.bypass, replica_groups=[list(range(NCORES))],
                    ins=[vown0.ap().opt()], outs=[VG0.ap().opt()]),
                    waits=rwait(r_vown0))
                r_VG0.note_write(("cc", cv))
            tap_dram("qsc", qsc, r_qsc)
            tap_dram("kown", kown0, r_kown0)
            tap_dram("vown", vown0, r_vown0)

        def emit_ffn(wg_h, wu_h, wd_h, inter, sg_col, su_col, sd_col):
            nslab = inter // 512
            fuse_reg = arC.region(0, 4096)
            fuse_ap = aC[:, 0:4096].rearrange("p (a b) -> p a b", a=4)
            wd_reg = arC.region(4096, 12288)
            wd_ap = aC[:, 4096:12288].rearrange("p (a b) -> p a b", a=4)
            gt_regs = [arC.region(12288, 12800), arC.region(12800, 13312)]
            gt_aps = [aC[:, 12288:12800], aC[:, 12800:13312]]
            ut_regs = [arC.region(13312, 13824), arC.region(13824, 14336)]
            ut_aps = [aC[:, 13312:13824], aC[:, 13824:14336]]
            wg_reg = arW_slots[0]
            wu_reg = arW_slots[1]
            for i in range(nslab):
                clg = dG(lambda e, i=i: e.dma_start(out=wslab_ap(0)[...],
                                                    in_=wg_h[i]),
                         "Lg0", waits=wwait(wg_reg))
                wg_reg.note_write(clg)
                clu = dG(lambda e, i=i: e.dma_start(out=wslab_ap(8192)[...],
                                                    in_=wu_h[i]),
                         "Lg1", waits=wwait(wu_reg))
                wu_reg.note_write(clu)
                cld = dG(lambda e, i=i: e.dma_start(out=wd_ap[...],
                                                    in_=wd_h[i]),
                         "Lg2", waits=wwait(wd_reg))
                wd_reg.note_write(cld)
                for oc in range(4):
                    og = i * 4 + oc
                    for th in range(2):
                        gsl = th
                        gk = take_slot()
                        w = [clg] + wwait(RPS[gk]) + rwait(*RA)
                        cg = None
                        for hc in range(DC):
                            cg = eT(lambda e, gk=gk, hc=hc, oc=oc, th=th:
                                    e.matmul(BANK_AP[gk], wslab_ap(0)[:, hc, oc * 128:(oc + 1) * 128],
                                             aA[:, hc, th * 512:(th + 1) * 512],
                                             start=(hc == 0), stop=(hc == DC - 1)),
                                    waits=w if hc == 0 else (), inc=(hc == DC - 1))
                        RPS[gk].note_write(("T", cg))
                        for r in RA:
                            r.note_read(("T", cg))
                        wg_reg.note_read(("T", cg))
                        if sg_col is not None:
                            cgs = eS(lambda e, gk=gk, gsl=gsl, og=og:
                                     e.activation(gt_aps[gsl][...], BANK_AP[gk], AF.Silu,
                                                  scale=sg_col[:, og:og + 1]),
                                     waits=[("T", cg)] + wwait(gt_regs[gsl]) + CW)
                        else:
                            cgs = eS(lambda e, gk=gk, gsl=gsl:
                                     e.activation(gt_aps[gsl][...], BANK_AP[gk], AF.Silu),
                                     waits=[("T", cg)] + wwait(gt_regs[gsl]))
                        gt_regs[gsl].note_write(("S", cgs))
                        RPS[gk].note_read(("S", cgs))
                        uk = take_slot()
                        w = [clu] + wwait(RPS[uk])
                        cu = None
                        for hc in range(DC):
                            cu = eT(lambda e, uk=uk, hc=hc, oc=oc, th=th:
                                    e.matmul(BANK_AP[uk], wslab_ap(8192)[:, hc, oc * 128:(oc + 1) * 128],
                                             aA[:, hc, th * 512:(th + 1) * 512],
                                             start=(hc == 0), stop=(hc == DC - 1)),
                                    waits=w if hc == 0 else (), inc=(hc == DC - 1))
                        RPS[uk].note_write(("T", cu))
                        for r in RA:
                            r.note_read(("T", cu))
                        wu_reg.note_read(("T", cu))
                        if su_col is not None:
                            cus = eV(lambda e, uk=uk, gsl=gsl, og=og:
                                     e.tensor_scalar(ut_aps[gsl][...], BANK_AP[uk],
                                                     su_col[:, og:og + 1], None, ALU.mult),
                                     waits=[("T", cu)] + wwait(ut_regs[gsl]) + CW)
                        else:
                            cus = eV(lambda e, uk=uk, gsl=gsl:
                                     e.tensor_copy(ut_aps[gsl][...], BANK_AP[uk]),
                                     waits=[("T", cu)] + wwait(ut_regs[gsl]))
                        ut_regs[gsl].note_write(("V", cus))
                        RPS[uk].note_read(("V", cus))
                        cf = eV(lambda e, gsl=gsl, oc=oc, th=th:
                                e.tensor_tensor(fuse_ap[:, oc, th * 512:(th + 1) * 512],
                                                gt_aps[gsl][...], ut_aps[gsl][...], ALU.mult),
                                waits=[("S", cgs), ("V", cus)] +
                                      (wwait(fuse_reg) if (oc == 0 and th == 0) else []))
                        gt_regs[gsl].note_read(("V", cf))
                        ut_regs[gsl].note_read(("V", cf))
                        if oc == 3 and th == 1:
                            fuse_reg.note_write(("V", cf))
                fuse_done = P.cnt["V"]
                for hc in range(DC):
                    for th in range(2):
                        dk = take_slot()
                        w = [cld, ("V", fuse_done)] + wwait(RPS[dk])
                        cd = None
                        for oc in range(4):
                            cd = eT(lambda e, dk=dk, oc=oc, hc=hc, th=th:
                                    e.matmul(BANK_AP[dk], wd_ap[:, oc, hc * 128:(hc + 1) * 128],
                                             fuse_ap[:, oc, th * 512:(th + 1) * 512],
                                             start=(oc == 0), stop=(oc == 3)),
                                    waits=w if oc == 0 else (), inc=(oc == 3))
                        RPS[dk].note_write(("T", cd))
                        wd_reg.note_read(("T", cd))
                        fuse_reg.note_read(("T", cd))
                        if sd_col is not None:
                            ca = eV(lambda e, dk=dk, hc=hc, th=th:
                                    e.scalar_tensor_tensor(xres[:, hc, th * 512:(th + 1) * 512],
                                                           BANK_AP[dk], sd_col[:, hc:hc + 1],
                                                           xres[:, hc, th * 512:(th + 1) * 512],
                                                           ALU.mult, ALU.add),
                                    waits=[("T", cd)] + wwait(RX[hc]) + CW)
                        else:
                            ca = eV(lambda e, dk=dk, hc=hc, th=th:
                                    e.tensor_tensor(xres[:, hc, th * 512:(th + 1) * 512],
                                                    BANK_AP[dk],
                                                    xres[:, hc, th * 512:(th + 1) * 512], ALU.add),
                                    waits=[("T", cd)] + wwait(RX[hc]))
                        RX[hc].note_write(("V", ca))
                        RPS[dk].note_read(("V", ca))

        if stages >= 3:
            emit_ffn(wg8, wu8, wd8, 8192, sgate_s, sup_s, sdown_s)
            tap_sbuf("x_ffn", xres[:, :, :], rwait(*RX))

        def emit_attn0():
            qh_regs = [arC.region(0, 1024), arC.region(1024, 2048)]
            qh_aps = [aC[:, 0:1024], aC[:, 1024:2048]]
            ks_regs = [arC.region(2048 + i * 1024, 3072 + i * 1024) for i in range(3)]
            ks_aps = [aC[:, 2048 + i * 1024:3072 + i * 1024].rearrange("p (a b) -> p a b", a=8)
                      for i in range(3)]
            vs_regs = [arC.region(5120 + i * 1024, 6144 + i * 1024) for i in range(3)]
            vs_aps = [aC[:, 5120 + i * 1024:6144 + i * 1024].rearrange("p (a b) -> p a b", a=8)
                      for i in range(3)]
            pt_regs = [arC.region(8192 + i * 1024, 9216 + i * 1024) for i in range(4)]
            pt_aps = [aC[:, 8192 + i * 1024:9216 + i * 1024] for i in range(4)]
            STP = {0: (psA, pair_res("A")), 1: (psB, pair_res("B"))}
            for h in range(16):
                qsl = h % 2
                cq = dY(lambda e, h=h, qsl=qsl: e.dma_start(out=qh_aps[qsl][...],
                                                            in_=qsc.ap()[h * 128:(h + 1) * 128, :]),
                        "Lx0" if qsl == 0 else "Lx1",
                        waits=wwait(qh_regs[qsl]) + rwait(r_qsc))
                qh_regs[qsl].note_write(cq)
                if h % 2 == 0:
                    pv_res = [RPS["C0"], RPS["C1"]]
                    pv_ap = [psC[:, 0:512], psC[:, 512:1024]]
                else:
                    pv_res = [RPS["D"], RPS["E"]]
                    pv_ap = [psD[:, :], psE[:, :]]
                lastV = lastG = None
                for r in range(8):
                    ssl = r % 3
                    ck = dY(lambda e, r=r, h=h, ssl=ssl: e.dma_start(
                        out=ks_aps[ssl][...], in_=KG0v[r, h]),
                        "Ls0" if r % 2 == 0 else "Ls1",
                        waits=wwait(ks_regs[ssl]) + rwait(r_KG0))
                    ks_regs[ssl].note_write(ck)
                    cvl = dY(lambda e, r=r, h=h, ssl=ssl: e.dma_start(
                        out=vs_aps[ssl][...], in_=VG0v[r, h]),
                        "Ls2" if r % 2 == 0 else "Ls3",
                        waits=wwait(vs_regs[ssl]) + rwait(r_VG0))
                    vs_regs[ssl].note_write(cvl)
                    for kc in range(8):
                        kcg = r * 8 + kc
                        stp, stres = STP[kcg % 2]
                        w = [ck, cq] + wwait(*stres)
                        eT(lambda e, stp=stp, ssl=ssl, kc=kc, qsl=qsl:
                           e.matmul(stp[:, 0:512], ks_aps[ssl][:, kc, :], qh_aps[qsl][:, 0:512],
                                    start=True, stop=True),
                           waits=w)
                        cst = eT(lambda e, stp=stp, ssl=ssl, kc=kc, qsl=qsl:
                                 e.matmul(stp[:, 512:1024], ks_aps[ssl][:, kc, :], qh_aps[qsl][:, 512:1024],
                                          start=True, stop=True), inc=True)
                        for rr in stres:
                            rr.note_write(("T", cst))
                        ks_regs[ssl].note_read(("T", cst))
                        qh_regs[qsl].note_read(("T", cst))
                        psl = kcg % 4
                        cexp = eS(lambda e, stp=stp, psl=psl:
                                  e.activation(pt_aps[psl][...], stp[:, :], AF.Exp,
                                               bias=neg20[:, :], scale=INV_HD0),
                                  waits=[("T", cst)] + wwait(pt_regs[psl]) + CW)
                        pt_regs[psl].note_write(("S", cexp))
                        for rr in stres:
                            rr.note_read(("S", cexp))
                        if kcg % 2 == 0:
                            zi, zres, em = 0, RE[0], eV
                        else:
                            zi, zres, em = 1, RE[1], eG
                        if kcg < 2:
                            cz = em(lambda e, psl=psl, zi=zi: e.tensor_copy(e_slot[zi][...], pt_aps[psl][...]),
                                    waits=[("S", cexp)] + wwait(zres))
                        else:
                            cz = em(lambda e, psl=psl, zi=zi: e.tensor_tensor(
                                e_slot[zi][...], e_slot[zi][...], pt_aps[psl][...], ALU.add),
                                waits=[("S", cexp)] + wwait(zres))
                        zres.note_write(("V" if kcg % 2 == 0 else "G", cz))
                        pt_regs[psl].note_read(("V" if kcg % 2 == 0 else "G", cz))
                        if kcg % 2 == 0:
                            lastV = cz
                        else:
                            lastG = cz
                        wz = [("S", cexp), cvl]
                        if kcg == 0:
                            wz += wwait(*pv_res)
                        eT(lambda e, pv_ap=pv_ap, psl=psl, kcg=kcg, ssl=ssl, kc=kc:
                           e.matmul(pv_ap[0], vs_aps[ssl][:, kc, :], pt_aps[psl][:, 0:512],
                                    start=(kcg == 0), stop=(kcg == 63)),
                           waits=wz)
                        cpv = eT(lambda e, pv_ap=pv_ap, psl=psl, kcg=kcg, ssl=ssl, kc=kc:
                                 e.matmul(pv_ap[1], vs_aps[ssl][:, kc, :], pt_aps[psl][:, 512:1024],
                                          start=(kcg == 0), stop=(kcg == 63)), inc=True)
                        pt_regs[psl].note_read(("T", cpv))
                        vs_regs[ssl].note_read(("T", cpv))
                cpv63 = P.cnt["T"]
                for rr in pv_res:
                    rr.note_write(("T", cpv63))
                wzf = [("V", lastV), ("G", lastG)] + wwait(RPS["A0"], RPS["A1"])
                czf = None
                for th in range(2):
                    eT(lambda e, th=th: e.matmul(psA[0:1, th * 512:(th + 1) * 512], onescolf[...],
                                                 e_slot[0][:, th * 512:(th + 1) * 512],
                                                 start=True, stop=False, skip_group_check=True),
                       waits=wzf if th == 0 else ())
                    czf = eT(lambda e, th=th: e.matmul(psA[0:1, th * 512:(th + 1) * 512], onescolf[...],
                                                       e_slot[1][:, th * 512:(th + 1) * 512],
                                                       start=False, stop=True, skip_group_check=True),
                             inc=(th == 1))
                RPS["A0"].note_write(("T", czf))
                RPS["A1"].note_write(("T", czf))
                RE[0].note_read(("T", czf))
                RE[1].note_read(("T", czf))
                cr0 = eS(lambda e: e.copy(sr_slot[0][...], psA[0:1, 0:512]),
                         waits=[("T", czf)] + wwait(RSR[0]))
                cr1 = eS(lambda e: e.copy(sr_slot[1][...], psA[0:1, 512:1024]),
                         waits=wwait(RSR[1]))
                RSR[0].note_write(("S", cr0))
                RSR[1].note_write(("S", cr1))
                RPS["A0"].note_read(("S", cr0))
                RPS["A1"].note_read(("S", cr1))
                cb = None
                wb = [("S", cr1)] + wwait(RPS["B0"], RPS["B1"])
                for th in range(2):
                    cb = eT(lambda e, th=th: e.matmul(psB[:, th * 512:(th + 1) * 512], onesrowf[...],
                                                      sr_slot[th][...], start=True, stop=True),
                            waits=wb if th == 0 else (), inc=(th == 1))
                RPS["B0"].note_write(("T", cb))
                RPS["B1"].note_write(("T", cb))
                RSR[0].note_read(("T", cb))
                RSR[1].note_read(("T", cb))
                crc = eV(lambda e: e.reciprocal(e_slot[2][...], psB[:, :]),
                         waits=[("T", cb)] + wwait(RE[2]))
                RE[2].note_write(("V", crc))
                RPS["B0"].note_read(("V", crc))
                RPS["B1"].note_read(("V", crc))
                cn = eV(lambda e, h=h, pv_ap=pv_ap: e.tensor_tensor(
                    aA[:, h, 0:512], pv_ap[0], e_slot[2][:, 0:512], ALU.mult),
                    waits=[("T", cpv63), ("V", crc)] + wwait(RA[h]))
                cn2 = eV(lambda e, h=h, pv_ap=pv_ap: e.tensor_tensor(
                    aA[:, h, 512:1024], pv_ap[1], e_slot[2][:, 512:1024], ALU.mult))
                RA[h].note_write(("V", cn2))
                RE[2].note_read(("V", cn2))
                pv_res[0].note_read(("V", cn))
                pv_res[1].note_read(("V", cn2))

        if stages >= 4:
            emit_attn0()
            tap_sbuf("attn0", aA[:, :, :], rwait(*RA))

        def emit_oproj(w_hbm, relu_bias=None):
            def ep(dc, pk, cnt):
                pap = PAIR_AP[pk]
                if relu_bias is None:
                    ca = eV(lambda e, pap=pap, dc=dc: e.tensor_tensor(
                        xres[:, dc, :], pap[:, :], xres[:, dc, :], ALU.add),
                        waits=[("T", cnt)] + wwait(RX[dc]))
                    RX[dc].note_write(("V", ca))
                    for r in pair_res(pk):
                        r.note_read(("V", ca))
                else:
                    esl = dc % 2
                    cs = eS(lambda e, pap=pap, dc=dc, esl=esl: e.activation(
                        e_slot[esl][...], pap[:, :], AF.Relu,
                        bias=relu_bias[:, dc:dc + 1], scale=1.0),
                        waits=[("T", cnt)] + wwait(RE[esl]) + CW)
                    RE[esl].note_write(("S", cs))
                    for r in pair_res(pk):
                        r.note_read(("S", cs))
                    ca = eV(lambda e, dc=dc, esl=esl: e.tensor_tensor(
                        xres[:, dc, :], e_slot[esl][...], xres[:, dc, :], ALU.add),
                        waits=[("S", cs)] + wwait(RX[dc]))
                    RX[dc].note_write(("V", ca))
                    RE[esl].note_read(("V", ca))
            emit_proj(w_hbm, ep)

        if stages >= 5:
            emit_oproj(woT)
            tap_sbuf("x_l0", xres[:, :, :], rwait(*RX))

        if stages >= 6:
            emit_ln(1)
            rw0 = rwait(RPS["D"]) + rwait(r_route)
            c1 = eS(lambda e: e.activation(r_sc[:, :, :],
                                           psD[:, 0:128].rearrange("p (a b) -> p a b", a=8),
                                           AF.Sigmoid), waits=rw0)
            RPS["D"].note_read(("S", c1))
            c2 = eV(lambda e: e.tensor_tensor(r_s4[:, :, :], r_sc[:, :, :],
                                              esbb[:, :].unsqueeze(1).broadcast_to((128, 8, 16)),
                                              ALU.add),
                    waits=[("S", c1)] + rwait(r_route))
            s4g = r_s4[:, :, :].rearrange("p a (g e) -> p a g e", g=4)
            c3 = eV(lambda e: e.tensor_tensor(r_pr[:, :, :, 0:3], s4g[:, :, :, 0:3], s4g[:, :, :, 1:4], ALU.add),
                    waits=[("V", c2)])
            c4 = eV(lambda e: e.tensor_tensor(r_pr[:, :, :, 3:5], s4g[:, :, :, 0:2], s4g[:, :, :, 2:4], ALU.add),
                    waits=[("V", c3)])
            c5 = eV(lambda e: e.tensor_tensor(r_pr[:, :, :, 5:6], s4g[:, :, :, 0:1], s4g[:, :, :, 3:4], ALU.add),
                    waits=[("V", c4)])
            c6 = eV(lambda e: e.tensor_reduce(r_gt[:, :, :], r_pr[:, :, :, :], axis=AXX, op=ALU.max),
                    waits=[("V", c5)])
            c7 = eV(lambda e: e.tensor_tensor(r_pr[:, :, 0, 0:3], r_gt[:, :, 0:3], r_gt[:, :, 1:4], ALU.min),
                    waits=[("V", c6)])
            c8 = eV(lambda e: e.tensor_tensor(r_pr[:, :, 0, 3:5], r_gt[:, :, 0:2], r_gt[:, :, 2:4], ALU.min),
                    waits=[("V", c7)])
            c9 = eV(lambda e: e.tensor_tensor(r_pr[:, :, 0, 5:6], r_gt[:, :, 0:1], r_gt[:, :, 3:4], ALU.min),
                    waits=[("V", c8)])
            c10 = eV(lambda e: e.tensor_reduce(r_m2[:, :, :], r_pr[:, :, 0, :], axis=AXX, op=ALU.max),
                     waits=[("V", c9)])
            c11 = eV(lambda e: e.tensor_tensor(r_gt[:, :, :], r_gt[:, :, :],
                                               r_m2[:, :, :].broadcast_to((128, 8, 4)), ALU.is_ge),
                     waits=[("V", c10)])
            c12 = eV(lambda e: e.tensor_copy(r_gm[:, :, :].rearrange("p a (g e) -> p a g e", g=4),
                                             r_gt[:, :, :].unsqueeze(3).broadcast_to((128, 8, 4, 4))),
                     waits=[("V", c11)])
            c13a = eV(lambda e: e.tensor_tensor(r_wk[:, :, :], r_s4[:, :, :], r_gm[:, :, :], ALU.mult),
                      waits=[("V", c12)])
            c13b = eV(lambda e: e.tensor_scalar(r_gm[:, :, :], r_gm[:, :, :], -1.0, None, ALU.add),
                      waits=[("V", c13a)])
            c14 = eV(lambda e: e.scalar_tensor_tensor(r_gm[:, :, :], r_gm[:, :, :], 1e30,
                                                      r_wk[:, :, :], ALU.mult, ALU.add),
                     waits=[("V", c13b)])
            prev = eV(lambda e: e.tensor_copy(r_wk[:, :, :], r_gm[:, :, :]), waits=[("V", c14)])
            for it in range(3):
                ct = eV(lambda e: e.tensor_reduce(r_ti[:, :, :], r_wk[:, :, :], axis=AXX, op=ALU.max),
                        waits=[("V", prev)])
                cg_ = eV(lambda e: e.tensor_tensor(r_ge[:, :, :], r_wk[:, :, :],
                                                   r_ti[:, :, :].broadcast_to((128, 8, 16)), ALU.is_ge),
                         waits=[("V", ct)])
                prev = eV(lambda e: e.scalar_tensor_tensor(r_wk[:, :, :], r_ge[:, :, :], -1e30,
                                                           r_wk[:, :, :], ALU.mult, ALU.add),
                          waits=[("V", cg_)])
            cti = eV(lambda e: e.tensor_reduce(r_ti[:, :, :], r_wk[:, :, :], axis=AXX, op=ALU.max),
                     waits=[("V", prev)])
            csel = eV(lambda e: e.tensor_tensor(r_sel[:, :, :], r_gm[:, :, :],
                                                r_ti[:, :, :].broadcast_to((128, 8, 16)), ALU.is_ge),
                      waits=[("V", cti)])
            cwr = eV(lambda e: e.tensor_tensor(r_w[:, :, :], r_sc[:, :, :], r_sel[:, :, :], ALU.mult),
                     waits=[("V", csel)])
            czz = eV(lambda e: e.tensor_reduce(r_ti[:, :, :], r_w[:, :, :], axis=AXX, op=ALU.add),
                     waits=[("V", cwr)])
            cz2 = eV(lambda e: e.tensor_scalar(r_ti[:, :, :], r_ti[:, :, :], 1e-20, None, ALU.add),
                     waits=[("V", czz)])
            cz3 = eV(lambda e: e.reciprocal(r_m2[:, :, :], r_ti[:, :, :]), waits=[("V", cz2)])
            cz4 = eV(lambda e: e.tensor_tensor(r_w[:, :, :], r_w[:, :, :],
                                               r_m2[:, :, :].broadcast_to((128, 8, 16)), ALU.mult),
                     waits=[("V", cz3)])
            cz5 = eV(lambda e: e.tensor_scalar(r_w[:, :, :], r_w[:, :, :], 2.5, None, ALU.mult),
                     waits=[("V", cz4)])
            r_wtm.note_write(("V", cz5))
            if "wtm" in taps:
                dY(lambda e: e.dma_start(out=tap_outs["wtm"][...], in_=r_w[:, :, :]),
                   "Ltap", waits=rwait(r_wtm))
            if "rsc" in taps:
                dY(lambda e: e.dma_start(out=tap_outs["rsc"][...], in_=r_sc[:, :, :]),
                   "Ltap", waits=rwait(r_wtm))
            if "esbb" in taps:
                dY(lambda e: e.dma_start(out=tap_outs["esbb"][...], in_=esbb[:, :]),
                   "Ltap", waits=rwait(r_wtm))
            tap_sbuf("xn1", aA[:, :, :], rwait(*RA))

        if stages >= 7:
            kv = kv_regions()
            emit_proj(w1kT, relu_epilogue(b1k_s, kv["kst"][0], kv["kst"][1],
                                          blk_store(kv["kst"][0], kv["kst"][1], kown1v, r_kown1)))
            emit_proj(w1vT, vtrans_epilogue(kv["vst"][0], kv["vst"][1],
                                            kv["vstg"][0], kv["vstg"][1], vown1v, r_vown1,
                                            relu_bias=b1v_s))
            emit_proj(w1qT, relu_epilogue(b1q_s, kv["qst"][0], kv["qst"][1],
                                          row_store(kv["qst"][0], kv["qst"][1], qsc1.ap(), r_qsc1)))
            ck = cG(lambda e: e.collective_compute(
                "AllGather", ALU.bypass, replica_groups=[list(range(NCORES))],
                ins=[kown1.ap().opt()], outs=[KG1.ap().opt()]),
                waits=rwait(r_kown1))
            r_KG1.note_write(("cc", ck))
            cv = cG(lambda e: e.collective_compute(
                "AllGather", ALU.bypass, replica_groups=[list(range(NCORES))],
                ins=[vown1.ap().opt()], outs=[VG1.ap().opt()]),
                waits=rwait(r_vown1))
            r_VG1.note_write(("cc", cv))
            tap_dram("qsc1", qsc1, r_qsc1)
            tap_dram("kown1", kown1, r_kown1)
            tap_dram("vown1", vown1, r_vown1)

        def emit_moe():
            fuse_reg = arC.region(0, 8192)
            fuse_ap = aC[:, 0:8192].rearrange("p (a b) -> p a b", a=8)
            gt_regs = [arC.region(8192, 8704), arC.region(8704, 9216)]
            gt_aps = [aC[:, 8192:8704], aC[:, 8704:9216]]
            ut_regs = [arC.region(9216, 9728), arC.region(9728, 10240)]
            ut_aps = [aC[:, 9216:9728], aC[:, 9728:10240]]
            wg_regs = [arW.region(0, 4096), arW.region(4096, 8192)]
            wg_off = [0, 4096]
            wu_regs = [arW.region(8192, 12288), arW.region(12288, 16384)]
            wu_off = [8192, 12288]
            dn_regs = [arC.region(10240, 12288), arC.region(12288, 14336)]
            dn_off = [10240, 12288]

            def wpiece(off):
                return aW[:, off:off + 4096].rearrange("p (c o) -> p c o", c=16)

            def dpiece(off):
                return aC[:, off:off + 2048].rearrange("p (c o) -> p c o", c=8)

            for n in range(16):
                wex = rwait(r_wtm) + wwait(RPS["D"], RPS["E"]) + CW
                cx = None
                for tc in range(8):
                    bk = "D" if tc < 4 else "E"
                    cx = eT(lambda e, tc=tc, n=n, bk=bk:
                            e.matmul(BANK_AP[bk][0:1, (tc % 4) * 128:(tc % 4 + 1) * 128],
                                     r_w[:, tc, n:n + 1], identf[...],
                                     start=(tc % 4 == 0), stop=(tc % 4 == 3),
                                     skip_group_check=True),
                            waits=wex if tc == 0 else (), inc=(tc == 7))
                RPS["D"].note_write(("T", cx))
                RPS["E"].note_write(("T", cx))
                r_wtm.note_read(("T", cx))
                cw1 = eS(lambda e: e.copy(sr_slot[0][...], psD[0:1, :]),
                         waits=[("T", cx)] + wwait(RSR[0]))
                cw2 = eS(lambda e: e.copy(sr_slot[1][...], psE[0:1, :]),
                         waits=[("T", cx)] + wwait(RSR[1]))
                RSR[0].note_write(("S", cw1))
                RSR[1].note_write(("S", cw2))
                RPS["D"].note_read(("S", cw1))
                RPS["E"].note_read(("S", cw2))
                cb = None
                wbc = [("S", cw1), ("S", cw2)]
                for th in range(2):
                    bk = "D" if th == 0 else "E"
                    cb = eT(lambda e, th=th, bk=bk: e.matmul(BANK_AP[bk], onesrowf[...],
                                                             sr_slot[th][...], start=True, stop=True),
                            waits=wbc if th == 0 else (), inc=(th == 1))
                RPS["D"].note_write(("T", cb))
                RPS["E"].note_write(("T", cb))
                RSR[0].note_read(("T", cb))
                RSR[1].note_read(("T", cb))
                cbw = eS(lambda e: e.copy(e_slot[3][:, 0:512], psD[:, :]),
                         waits=[("T", cb)] + wwait(RE[3]))
                cbw2 = eS(lambda e: e.copy(e_slot[3][:, 512:1024], psE[:, :]))
                RE[3].note_write(("S", cbw2))
                RPS["D"].note_read(("S", cbw))
                RPS["E"].note_read(("S", cbw2))
                for fq in range(4):
                    gsl = fq % 2
                    clg = dG(lambda e, n=n, fq=fq, gsl=gsl: e.dma_start(
                        out=wpiece(wg_off[gsl])[...], in_=mgT[n, fq]),
                        "Lg0" if gsl == 0 else "Lg1",
                        waits=wwait(wg_regs[gsl]))
                    wg_regs[gsl].note_write(clg)
                    clu = dG(lambda e, n=n, fq=fq, gsl=gsl: e.dma_start(
                        out=wpiece(wu_off[gsl])[...], in_=muT[n, fq]),
                        "Lg2" if gsl == 0 else "Lg3",
                        waits=wwait(wu_regs[gsl]))
                    wu_regs[gsl].note_write(clu)
                    for fc in range(2):
                        fcg = fq * 2 + fc
                        for th in range(2):
                            tsl = th
                            gk = take_slot()
                            w = [clg] + wwait(RPS[gk]) + rwait(*RA)
                            cg2 = None
                            for hc in range(DC):
                                cg2 = eT(lambda e, gk=gk, gsl=gsl, hc=hc, fc=fc, th=th:
                                         e.matmul(BANK_AP[gk], wpiece(wg_off[gsl])[:, hc, fc * 128:(fc + 1) * 128],
                                                  aA[:, hc, th * 512:(th + 1) * 512],
                                                  start=(hc == 0), stop=(hc == DC - 1)),
                                         waits=w if hc == 0 else (), inc=(hc == DC - 1))
                            RPS[gk].note_write(("T", cg2))
                            for r in RA:
                                r.note_read(("T", cg2))
                            wg_regs[gsl].note_read(("T", cg2))
                            cgs = eS(lambda e, gk=gk, tsl=tsl: e.activation(gt_aps[tsl][...], BANK_AP[gk], AF.Silu),
                                     waits=[("T", cg2)] + wwait(gt_regs[tsl]))
                            gt_regs[tsl].note_write(("S", cgs))
                            RPS[gk].note_read(("S", cgs))
                            uk = take_slot()
                            w = [clu] + wwait(RPS[uk])
                            cu2 = None
                            for hc in range(DC):
                                cu2 = eT(lambda e, uk=uk, gsl=gsl, hc=hc, fc=fc, th=th:
                                         e.matmul(BANK_AP[uk], wpiece(wu_off[gsl])[:, hc, fc * 128:(fc + 1) * 128],
                                                  aA[:, hc, th * 512:(th + 1) * 512],
                                                  start=(hc == 0), stop=(hc == DC - 1)),
                                         waits=w if hc == 0 else (), inc=(hc == DC - 1))
                            RPS[uk].note_write(("T", cu2))
                            for r in RA:
                                r.note_read(("T", cu2))
                            wu_regs[gsl].note_read(("T", cu2))
                            cus = eV(lambda e, uk=uk, tsl=tsl, th=th:
                                     e.tensor_tensor(ut_aps[tsl][...], BANK_AP[uk],
                                                     e_slot[3][:, th * 512:(th + 1) * 512], ALU.mult),
                                     waits=[("T", cu2), ("S", cbw2)] + wwait(ut_regs[tsl]))
                            ut_regs[tsl].note_write(("V", cus))
                            RPS[uk].note_read(("V", cus))
                            RE[3].note_read(("V", cus))
                            cf = eV(lambda e, tsl=tsl, fcg=fcg, th=th:
                                    e.tensor_tensor(fuse_ap[:, fcg, th * 512:(th + 1) * 512],
                                                    gt_aps[tsl][...], ut_aps[tsl][...], ALU.mult),
                                    waits=[("S", cgs), ("V", cus)] +
                                          (wwait(fuse_reg) if (fcg == 0 and th == 0) else []))
                            gt_regs[tsl].note_read(("V", cf))
                            ut_regs[tsl].note_read(("V", cf))
                            if fcg == 7 and th == 1:
                                fuse_reg.note_write(("V", cf))
                fuse_done = P.cnt["V"]
                for hs in range(8):
                    dsl = hs % 2
                    cld = dG(lambda e, n=n, hs=hs, dsl=dsl: e.dma_start(
                        out=dpiece(dn_off[dsl])[...], in_=mdT[n, hs]),
                        "Lg4" if dsl == 0 else "Lg5",
                        waits=wwait(dn_regs[dsl]))
                    dn_regs[dsl].note_write(cld)
                    for hcl in range(2):
                        hc = hs * 2 + hcl
                        for th in range(2):
                            dk = take_slot()
                            w = [cld, ("V", fuse_done)] + wwait(RPS[dk])
                            cd = None
                            for fc in range(8):
                                cd = eT(lambda e, dk=dk, dsl=dsl, fc=fc, hcl=hcl, th=th:
                                        e.matmul(BANK_AP[dk], dpiece(dn_off[dsl])[:, fc, hcl * 128:(hcl + 1) * 128],
                                                 fuse_ap[:, fc, th * 512:(th + 1) * 512],
                                                 start=(fc == 0), stop=(fc == 7)),
                                        waits=w if fc == 0 else (), inc=(fc == 7))
                            RPS[dk].note_write(("T", cd))
                            dn_regs[dsl].note_read(("T", cd))
                            fuse_reg.note_read(("T", cd))
                            ca = eV(lambda e, dk=dk, hc=hc, th=th:
                                    e.tensor_tensor(xres[:, hc, th * 512:(th + 1) * 512],
                                                    BANK_AP[dk],
                                                    xres[:, hc, th * 512:(th + 1) * 512], ALU.add),
                                    waits=[("T", cd)] + wwait(RX[hc]))
                            RX[hc].note_write(("V", ca))
                            RPS[dk].note_read(("V", ca))

        if stages >= 8:
            emit_moe()
            tap_sbuf("x_moe", xres[:, :, :], rwait(*RX))

        if stages >= 9:
            emit_ffn(shgT, shuT, shdT, 2048, None, None, None)
            tap_sbuf("x_sh", xres[:, :, :], rwait(*RX))

        def emit_attn1():
            q2_reg = arC.region(0, 8192)
            q2_ap = aC[:, 0:8192].rearrange("p (a b) -> p a b", a=8)
            kb_regs = [arC.region(8192 + i * 1024, 9216 + i * 1024) for i in range(3)]
            kb_aps = [aC[:, 8192 + i * 1024:9216 + i * 1024].rearrange("p (a b) -> p a b", a=8)
                      for i in range(3)]
            vb_regs = [arC.region(11264 + i * 1024, 12288 + i * 1024) for i in range(3)]
            vb_aps = [aC[:, 11264 + i * 1024:12288 + i * 1024].rearrange("p (a b) -> p a b", a=8)
                      for i in range(3)]
            pt_regs = [arC.region(14336 + i * 256, 14592 + i * 256) for i in range(4)]
            pt_aps = [aC[:, 14336 + i * 256:14592 + i * 256] for i in range(4)]
            OUT2 = {0: psA[:, 0:256], 1: psA[:, 256:512], 2: psA[:, 512:768], 3: psA[:, 768:1024],
                    4: psB[:, 0:256], 5: psB[:, 256:512], 6: psB[:, 512:768], 7: psB[:, 768:1024]}
            O2RES = {0: "A0", 1: "A0", 2: "A1", 3: "A1", 4: "B0", 5: "B0", 6: "B1", 7: "B1"}
            SC4 = [psC[:, i * 256:(i + 1) * 256] for i in range(4)]
            RQ = [Res() for _ in range(4)]
            for i, bk in ((0, "C0"), (1, "C0"), (2, "C1"), (3, "C1")):
                RQ[i].rs = list(RPS[bk].w_waits())
            zacc = e_slot[3][:, 0:256]
            zrow = srow[0:1, 1024:1280]
            for h in range(2):
                cq = dY(lambda e, h=h: e.dma_start(
                    out=q2_ap[...], in_=qsc1v[:, h * 8:(h + 1) * 8, :]),
                    "Lx0", waits=wwait(q2_reg) + rwait(r_qsc1))
                q2_reg.note_write(cq)
                for qt in range(4):
                    lastZ = None
                    for r in range(8):
                        for tc in range(8):
                            kcg = r * 8 + tc
                            bsl = kcg % 3
                            ck = dY(lambda e, r=r, h=h, tc=tc, bsl=bsl: e.dma_start(
                                out=kb_aps[bsl][...],
                                in_=KG1v[r, tc, :, h * 8:(h + 1) * 8, :]),
                                "Ls0" if kcg % 2 == 0 else "Ls1",
                                waits=wwait(kb_regs[bsl]) + rwait(r_KG1))
                            kb_regs[bsl].note_write(ck)
                            cvl = dG(lambda e, r=r, h=h, tc=tc, bsl=bsl: e.dma_start(
                                out=vb_aps[bsl][...],
                                in_=VG1v[r, tc, :, h * 8:(h + 1) * 8, :]),
                                "Lg4" if kcg % 2 == 0 else "Lg5",
                                waits=wwait(vb_regs[bsl]) + rwait(r_VG1))
                            vb_regs[bsl].note_write(cvl)
                            sq = kcg % 4
                            w = [ck, cq] + wwait(RQ[sq])
                            cst = None
                            for dcl in range(8):
                                cst = eT(lambda e, sq=sq, bsl=bsl, dcl=dcl, qt=qt:
                                         e.matmul(SC4[sq], kb_aps[bsl][:, dcl, :],
                                                  q2_ap[:, dcl, qt * 256:(qt + 1) * 256],
                                                  start=(dcl == 0), stop=(dcl == 7),
                                                  skip_group_check=True),
                                         waits=w if dcl == 0 else (), inc=(dcl == 7))
                            RQ[sq].note_write(("T", cst))
                            kb_regs[bsl].note_read(("T", cst))
                            q2_reg.note_read(("T", cst))
                            psl = kcg % 4
                            cexp = eS(lambda e, sq=sq, psl=psl:
                                      e.activation(pt_aps[psl][...], SC4[sq], AF.Exp,
                                                   bias=neg20[:, :], scale=INV_HD1),
                                      waits=[("T", cst)] + wwait(pt_regs[psl]) + CW)
                            pt_regs[psl].note_write(("S", cexp))
                            RQ[sq].note_read(("S", cexp))
                            if kcg == 0:
                                cz = eV(lambda e, psl=psl: e.tensor_copy(zacc[...], pt_aps[psl][...]),
                                        waits=[("S", cexp)] + wwait(RE[3]))
                            else:
                                cz = eV(lambda e, psl=psl: e.tensor_tensor(
                                    zacc[...], zacc[...], pt_aps[psl][...], ALU.add),
                                    waits=[("S", cexp)] + wwait(RE[3]))
                            RE[3].note_write(("V", cz))
                            pt_regs[psl].note_read(("V", cz))
                            lastZ = cz
                            wz = [("S", cexp), cvl]
                            if kcg == 0:
                                wz += wwait(RPS["A0"], RPS["A1"], RPS["B0"], RPS["B1"])
                            cpv = None
                            for dcl in range(8):
                                cpv = eT(lambda e, psl=psl, kcg=kcg, bsl=bsl, dcl=dcl:
                                         e.matmul(OUT2[dcl], vb_aps[bsl][:, dcl, :], pt_aps[psl][...],
                                                  start=(kcg == 0 and dcl % 2 == 0),
                                                  stop=(kcg == 63 and dcl % 2 == 1),
                                                  skip_group_check=True),
                                         waits=wz if dcl == 0 else (), inc=(dcl == 7))
                            pt_regs[psl].note_read(("T", cpv))
                            vb_regs[bsl].note_read(("T", cpv))
                    cpv63 = P.cnt["T"]
                    for bk in ("A0", "A1", "B0", "B1"):
                        RPS[bk].note_write(("T", cpv63))
                    czf = eT(lambda e: e.matmul(psD[0:1, 0:256], onescolf[...], zacc[...],
                                                start=True, stop=True, skip_group_check=True),
                             waits=[("V", lastZ)] + wwait(RPS["D"]), inc=True)
                    RPS["D"].note_write(("T", czf))
                    RE[3].note_read(("T", czf))
                    crz = eS(lambda e: e.copy(zrow[...], psD[0:1, 0:256]),
                             waits=[("T", czf)] + wwait(RSR[2]))
                    RSR[2].note_write(("S", crz))
                    RPS["D"].note_read(("S", crz))
                    cbx = eT(lambda e: e.matmul(psE[:, 0:256], onesrowf[...], zrow[...],
                                                start=True, stop=True),
                             waits=[("S", crz)] + wwait(RPS["E"]), inc=True)
                    RPS["E"].note_write(("T", cbx))
                    RSR[2].note_read(("T", cbx))
                    crc = eV(lambda e: e.reciprocal(e_slot[2][:, 0:256], psE[:, 0:256]),
                             waits=[("T", cbx)] + wwait(RE[2]))
                    RE[2].note_write(("V", crc))
                    RPS["E"].note_read(("V", crc))
                    cn = None
                    for dcl in range(8):
                        wn = [("T", cpv63), ("V", crc)]
                        if qt == 0:
                            wn += wwait(RA[h * 8 + dcl])
                        cn = eV(lambda e, h=h, dcl=dcl, qt=qt: e.tensor_tensor(
                            aA[:, h * 8 + dcl, qt * 256:(qt + 1) * 256],
                            OUT2[dcl], e_slot[2][:, 0:256], ALU.mult),
                            waits=wn)
                        RPS[O2RES[dcl]].note_read(("V", cn))
                        RE[2].note_read(("V", cn))
                        if qt == 3:
                            RA[h * 8 + dcl].note_write(("V", cn))
            for bk in ("C0", "C1"):
                RPS[bk].note_write(("T", P.cnt["T"]))

        if stages >= 10:
            emit_attn1()
            tap_sbuf("attn1", aA[:, :, :], rwait(*RA))

        if stages >= 11:
            emit_oproj(w1oT, relu_bias=b1o_s)

        ost_ap = eA[:, 0:2048]
        for tc in range(8):
            cVlast = None
            for g in range(4):
                bk = take_slot()
                bap = BANK_AP[bk]
                waits = wwait(RPS[bk]) + CW
                for dc in range(g * 4, g * 4 + 4):
                    waits += rwait(RX[dc])
                cT = None
                for j in range(4):
                    dc = g * 4 + j
                    cT = eT(lambda e, bap=bap, j=j, dc=dc, tc=tc:
                            e.transpose(bap[:, j * 128:(j + 1) * 128],
                                        xres[:, dc, tc * 128:(tc + 1) * 128], identf[...]),
                            waits=waits if j == 0 else (), inc=(j == 3))
                RPS[bk].note_write(("T", cT))
                for dc in range(g * 4, g * 4 + 4):
                    RX[dc].note_read(("T", cT))
                cVlast = eV(lambda e, bap=bap, g=g: e.tensor_copy(ost_ap[:, g * 512:(g + 1) * 512], bap),
                            waits=[("T", cT)] + (wwait(RE[0], RE[1]) if g == 0 else []))
                RPS[bk].note_read(("V", cVlast))
            cd = dY(lambda e, tc=tc: e.dma_start(out=y_out[tc * 128:(tc + 1) * 128, :],
                                                 in_=ost_ap[...]),
                    "Lout", waits=[("V", cVlast)])
            RE[0].note_write(("V", cVlast))
            RE[1].note_write(("V", cVlast))
            RE[0].note_read(cd)
            RE[1].note_read(cd)
        yWait([(ln, P.cnt[ln]) for ln in LANES_Y if P.cnt.get(ln)])

        with nc.Block() as block:
            def runner(ek):
                lastw = {}

                def go(eng):
                    for waits, fn, inc, by in P.ops[ek]:
                        for s, v in waits:
                            if v > lastw.get(s, -1):
                                eng.wait_ge(SEMS[s], v)
                                lastw[s] = v
                        ins = fn(eng)
                        if inc is not None and ins is not None:
                            ins.then_inc(SEMS[inc], by)
                return go

            block.tensor(runner("T"))
            block.scalar(runner("S"))
            block.vector(runner("V"))
            block.gpsimd(runner("G"))
            block.sync(runner("Y"))

    _NC_CACHE[key] = nc
    return nc


def kernel(**inputs):
    maps = _prep(inputs)
    nc = build_nc()
    res = run_bass_kernel_spmd(nc, maps, core_ids=list(range(NCORES)))
    parts = [res.results[c]["y"] for c in range(NCORES)]
    out = np.concatenate(parts, axis=0).reshape(1, SEQ, D).astype(np.float32)
    if TAPS:
        kernel.tap_results = [{k: v for k, v in res.results[c].items() if k.startswith("tap_")}
                              for c in range(NCORES)]
    return out


# version 03
# speedup vs baseline: 1.0227x; 1.0227x over previous
import numpy as np
import ml_dtypes
import concourse.bass as bass
import concourse.mybir as mybir
from concourse.bass_utils import run_bass_kernel_spmd
from concourse.replica_groups import maybe_share_collective_output_space
from contextlib import ExitStack

F32 = mybir.dt.float32
BF16 = mybir.dt.bfloat16
I8 = mybir.dt.int8
AF = mybir.ActivationFunctionType
ALU = mybir.AluOpType
AXX = mybir.AxisListType.X

NCORES = 8
T = 1024
D = 2048
DC = 16
SEQ = 8192
INV_HD0 = 0.08838834764831845
INV_HD1 = 0.03125
LN_EPS = 1e-5

STAGES = 99
TAPS = ()


class Res:
    __slots__ = ("w", "rs")

    def __init__(self):
        self.w = None
        self.rs = []

    def r_waits(self):
        return [self.w] if self.w else []

    def w_waits(self):
        out = list(self.rs)
        if self.w:
            out.append(self.w)
        return out

    def note_read(self, evt):
        if evt:
            self.rs.append(evt)

    def note_write(self, evt):
        self.w = evt
        self.rs = []


def rwait(*rr):
    out = []
    for r in rr:
        out += r.r_waits()
    return out


def wwait(*rr):
    out = []
    for r in rr:
        out += r.w_waits()
    return out


class Arena:
    def __init__(self):
        self.regions = []

    def region(self, s, e):
        r = Res()
        pend, keep = [], []
        for (s0, e0, r0) in self.regions:
            if s0 < e and s < e0:
                pend += r0.w_waits()
            else:
                keep.append((s0, e0, r0))
        r.rs = pend
        self.regions = keep + [(s, e, r)]
        return r


def dedup(waits):
    m = {}
    for w in waits:
        if w is None:
            continue
        s, v = w
        if s not in m or v > m[s]:
            m[s] = v
    return tuple(sorted(m.items()))


class Prog:
    def __init__(self):
        self.ops = {k: [] for k in "TSVGY"}
        self.cnt = {}

    def bump(self, s, by=1):
        self.cnt[s] = self.cnt.get(s, 0) + by
        return self.cnt[s]

    def emit(self, ek, fn, waits, inc, by):
        self.ops[ek].append((dedup(waits), fn, inc, by))


def _rope_tables():
    scale = (np.arange(0, 128, 2).astype(np.float32) / np.float32(128.0)).astype(np.float32)
    inv_freq = (np.float32(1.0) / (np.float32(10000.0) ** scale)).astype(np.float32)
    t = np.arange(SEQ, dtype=np.float32)
    ang = (t[:, None] * inv_freq[None, :]).astype(np.float32)
    c = np.cos(ang).astype(np.float32)
    s = np.sin(ang).astype(np.float32)
    C = np.empty((128, SEQ), np.float32)
    S2 = np.empty((128, SEQ), np.float32)
    for i in range(64):
        C[2 * i] = c[:, i]
        C[2 * i + 1] = c[:, i]
        S2[2 * i] = -s[:, i]
        S2[2 * i + 1] = s[:, i]
    return C, S2


def _col128(v, dtype=np.float32):
    v = np.asarray(v).astype(dtype)
    n = v.shape[0] // 128
    return np.ascontiguousarray(v.reshape(n, 128).T)


def _tile_go(wT, nsl, W):
    c = wT.shape[0] // 128
    return np.ascontiguousarray(wT.reshape(c, 128, nsl, W).transpose(2, 1, 0, 3))


def _tile_dn(wT, nsl, cc):
    O = wT.shape[1]
    return np.ascontiguousarray(wT.reshape(nsl, cc, 128, O).transpose(0, 2, 1, 3))


def _prep(inputs):
    f32 = np.float32
    bf16 = ml_dtypes.bfloat16
    x = np.asarray(inputs["L0_attn_x"])
    sh = {}
    sh["wqT"] = _tile_go(np.asarray(inputs["L0_attn_wq"]).T, 4, 512)
    sh["wkT"] = _tile_go(np.asarray(inputs["L0_attn_wk"]).T, 4, 512)
    sh["wvT"] = _tile_go(np.asarray(inputs["L0_attn_wv"]).T, 4, 512)
    sh["woT"] = _tile_go(np.asarray(inputs["L0_attn_wo"]).T, 4, 512)
    sh["wg8"] = _tile_go(np.asarray(inputs["L0_ffn_w_gate"]), 16, 512)
    sh["wu8"] = _tile_go(np.asarray(inputs["L0_ffn_w_up"]), 16, 512)
    sh["wd8"] = _tile_dn(np.asarray(inputs["L0_ffn_w_down"]), 16, 4)
    sh["sgate"] = _col128(inputs["L0_ffn_s_gate"])
    sh["sup"] = _col128(inputs["L0_ffn_s_up"])
    sh["sdown"] = _col128(inputs["L0_ffn_s_down"])
    for nm, key in (("w1qT", "L1_attn_Wq"), ("w1kT", "L1_attn_Wk"),
                    ("w1vT", "L1_attn_Wv"), ("w1oT", "L1_attn_Wo")):
        sh[nm] = _tile_go(np.asarray(inputs[key]).T, 4, 512)
    for nm, key in (("b1q", "L1_attn_bq"), ("b1k", "L1_attn_bk"),
                    ("b1v", "L1_attn_bv"), ("b1o", "L1_attn_bo")):
        sh[nm] = _col128(np.asarray(inputs[key]).astype(f32))
    sh["gw"] = np.ascontiguousarray(np.asarray(inputs["L1_ffn_gate_weight"]).T)
    sh["esb"] = np.asarray(inputs["L1_ffn_e_score_correction_bias"]).astype(f32).reshape(1, 16)
    mgT = np.asarray(inputs["L1_ffn_gate_w"]).transpose(0, 2, 1)
    muT = np.asarray(inputs["L1_ffn_up_w"]).transpose(0, 2, 1)
    mdT = np.asarray(inputs["L1_ffn_down_w"]).transpose(0, 2, 1)
    sh["mgT"] = np.ascontiguousarray(mgT.reshape(16, 16, 128, 4, 256).transpose(0, 3, 2, 1, 4))
    sh["muT"] = np.ascontiguousarray(muT.reshape(16, 16, 128, 4, 256).transpose(0, 3, 2, 1, 4))
    sh["mdT"] = np.ascontiguousarray(mdT.reshape(16, 8, 128, 8, 256).transpose(0, 3, 2, 1, 4))
    sh["shgT"] = _tile_go(np.asarray(inputs["L1_ffn_sh_gate_w"]).T, 4, 512)
    sh["shuT"] = _tile_go(np.asarray(inputs["L1_ffn_sh_up_w"]).T, 4, 512)
    sh["shdT"] = _tile_dn(np.asarray(inputs["L1_ffn_sh_down_w"]).T, 4, 4)
    sh["identb"] = np.eye(128, dtype=np.float32).astype(bf16)
    sh["identf"] = np.eye(128, dtype=np.float32)
    perm = np.zeros((128, 128), np.float32)
    for i in range(128):
        perm[i, i ^ 1] = 1.0
    sh["permf"] = perm
    sh["onescolb"] = np.ones((128, 1), np.float32).astype(bf16)
    sh["onescolf"] = np.ones((128, 1), np.float32)
    sh["onesrowf"] = np.ones((1, 128), np.float32)
    sh["neg20"] = np.full((128, 1), -20.0, np.float32)

    C, S2 = _rope_tables()
    maps = []
    for c in range(NCORES):
        m = dict(sh)
        m["x_tm"] = np.ascontiguousarray(x[0, c * T:(c + 1) * T, :]).astype(f32)
        m["ropec"] = np.ascontiguousarray(C[:, c * T:(c + 1) * T])
        m["ropes"] = np.ascontiguousarray(S2[:, c * T:(c + 1) * T])
        maps.append(m)
    return maps


_NC_CACHE = {}


def build_nc(stages=None, taps=None):
    stages = STAGES if stages is None else stages
    taps = TAPS if taps is None else taps
    key = (stages, tuple(taps))
    if key in _NC_CACHE:
        return _NC_CACHE[key]

    nc = bass.Bass("TRN2", target_bir_lowering=False)
    P = Prog()

    def din(name, shape, dt):
        return nc.dram_tensor(name, shape, dt, kind="ExternalInput").ap()

    x_tm = din("x_tm", [T, D], F32)
    wqT = din("wqT", [4, 128, 16, 512], F32)
    wkT = din("wkT", [4, 128, 16, 512], F32)
    wvT = din("wvT", [4, 128, 16, 512], F32)
    woT = din("woT", [4, 128, 16, 512], F32)
    wg8 = din("wg8", [16, 128, 16, 512], I8)
    wu8 = din("wu8", [16, 128, 16, 512], I8)
    wd8 = din("wd8", [16, 128, 4, 2048], I8)
    sgate = din("sgate", [128, 64], F32)
    sup = din("sup", [128, 64], F32)
    sdown = din("sdown", [128, 16], F32)
    w1qT = din("w1qT", [4, 128, 16, 512], BF16)
    w1kT = din("w1kT", [4, 128, 16, 512], BF16)
    w1vT = din("w1vT", [4, 128, 16, 512], BF16)
    w1oT = din("w1oT", [4, 128, 16, 512], BF16)
    b1q = din("b1q", [128, 16], F32)
    b1k = din("b1k", [128, 16], F32)
    b1v = din("b1v", [128, 16], F32)
    b1o = din("b1o", [128, 16], F32)
    gw = din("gw", [D, 16], F32)
    esb = din("esb", [1, 16], F32)
    mgT = din("mgT", [16, 4, 128, 16, 256], F32)
    muT = din("muT", [16, 4, 128, 16, 256], F32)
    mdT = din("mdT", [16, 8, 128, 8, 256], F32)
    shgT = din("shgT", [4, 128, 16, 512], F32)
    shuT = din("shuT", [4, 128, 16, 512], F32)
    shdT = din("shdT", [4, 128, 4, 2048], F32)
    ropec = din("ropec", [128, T], F32)
    ropes = din("ropes", [128, T], F32)
    identb_h = din("identb", [128, 128], BF16)
    identf_h = din("identf", [128, 128], F32)
    permf_h = din("permf", [128, 128], F32)
    onescolb_h = din("onescolb", [128, 1], BF16)
    onescolf_h = din("onescolf", [128, 1], F32)
    onesrowf_h = din("onesrowf", [1, 128], F32)
    neg20_h = din("neg20", [128, 1], F32)

    y_out = nc.dram_tensor("y", [T, D], F32, kind="ExternalOutput").ap()

    tap_outs = {}
    for tp in taps:
        if tp in ("xn", "xn1", "attn0", "attn1"):
            tap_outs[tp] = nc.dram_tensor("tap_" + tp, [128, DC, T], BF16, kind="ExternalOutput").ap()
        elif tp in ("x_ffn", "x_l0", "x_moe", "x_sh"):
            tap_outs[tp] = nc.dram_tensor("tap_" + tp, [128, DC, T], F32, kind="ExternalOutput").ap()
        elif tp in ("qsc", "qsc1"):
            tap_outs[tp] = nc.dram_tensor("tap_" + tp, [D, T], BF16, kind="ExternalOutput").ap()
        elif tp in ("kown", "vown", "kown1", "vown1"):
            tap_outs[tp] = nc.dram_tensor("tap_" + tp, [DC * 8 * 128, 128], BF16, kind="ExternalOutput").ap()
        elif tp == "wtm":
            tap_outs[tp] = nc.dram_tensor("tap_wtm", [128, 8, 16], F32, kind="ExternalOutput").ap()
        elif tp == "rsc":
            tap_outs[tp] = nc.dram_tensor("tap_rsc", [128, 8, 16], F32, kind="ExternalOutput").ap()
        elif tp == "esbb":
            tap_outs[tp] = nc.dram_tensor("tap_esbb", [128, 16], F32, kind="ExternalOutput").ap()

    kown0 = nc.dram_tensor("kown0", [DC * 8 * 128, 128], BF16, kind="Internal")
    vown0 = nc.dram_tensor("vown0", [DC * 8 * 128, 128], BF16, kind="Internal")
    kown1 = nc.dram_tensor("kown1", [DC * 8 * 128, 128], BF16, kind="Internal")
    vown1 = nc.dram_tensor("vown1", [DC * 8 * 128, 128], BF16, kind="Internal")
    qsc = nc.dram_tensor("qsc", [D, T], BF16, kind="Internal")
    qsc1 = nc.dram_tensor("qsc1", [D, T], BF16, kind="Internal")
    agspace = maybe_share_collective_output_space("AllGather", [list(range(NCORES))])
    KG0 = nc.dram_tensor("KG0", [NCORES * DC * 8 * 128, 128], BF16, kind="Internal", addr_space=agspace)
    VG0 = nc.dram_tensor("VG0", [NCORES * DC * 8 * 128, 128], BF16, kind="Internal", addr_space=agspace)
    KG1 = nc.dram_tensor("KG1", [NCORES * DC * 8 * 128, 128], BF16, kind="Internal", addr_space=agspace)
    VG1 = nc.dram_tensor("VG1", [NCORES * DC * 8 * 128, 128], BF16, kind="Internal", addr_space=agspace)

    def bview0(t_):
        return t_.ap().rearrange("(dc p tc) x -> dc p tc x", dc=DC, p=128, tc=8)

    def bview1(t_):
        return t_.ap().rearrange("(tc p dc) x -> dc p tc x", tc=8, p=128, dc=DC)

    def gview0(t_):
        return t_.ap().rearrange("(r dc p tc) x -> r dc p tc x", r=NCORES, dc=DC, p=128, tc=8)

    def gview1(t_):
        return t_.ap().rearrange("(r tc p dc) x -> r tc p dc x", r=NCORES, tc=8, p=128, dc=DC)

    kown0v = bview0(kown0)
    vown0v = bview0(vown0)
    kown1v = bview1(kown1)
    vown1v = bview1(vown1)
    KG0v = gview0(KG0)
    VG0v = gview0(VG0)
    KG1v = gview1(KG1)
    VG1v = gview1(VG1)
    qsc1v = qsc1.ap().rearrange("(c p) t -> p c t", p=128)

    with ExitStack() as stack:
        def sb(name, shape, dt):
            return stack.enter_context(nc.sbuf_tensor(name, shape, dt))

        def psmake(name, shape):
            return stack.enter_context(nc.psum_tensor(name, shape, F32))

        xres = sb("xres", [128, DC, T], F32)
        aA = sb("aA", [128, DC, T], BF16)
        aC = sb("aC", [128, 16384], BF16)
        aW = sb("aW", [128, 16384], BF16)
        eA = sb("eA", [128, 4096], F32)
        srow = sb("srow", [1, 2048], F32)
        identb = sb("identb_s", [128, 128], BF16)
        identf = sb("identf_s", [128, 128], F32)
        permf = sb("permf_s", [128, 128], F32)
        onescolb = sb("onescolb_s", [128, 1], BF16)
        onescolf = sb("onescolf_s", [128, 1], F32)
        onesrowf = sb("onesrowf_s", [1, 128], F32)
        neg20 = sb("neg20_s", [128, 1], F32)
        ropec_s = sb("ropec_s", [128, T], F32)
        ropes_s = sb("ropes_s", [128, T], F32)
        sgate_s = sb("sgate_s", [128, 64], F32)
        sup_s = sb("sup_s", [128, 64], F32)
        sdown_s = sb("sdown_s", [128, 16], F32)
        b1q_s = sb("b1q_s", [128, 16], F32)
        b1k_s = sb("b1k_s", [128, 16], F32)
        b1v_s = sb("b1v_s", [128, 16], F32)
        b1o_s = sb("b1o_s", [128, 16], F32)
        gwf = sb("gwf", [128, 16, 16], F32)
        esb_s = sb("esb_s", [1, 16], F32)
        esbb = sb("esbb", [128, 16], F32)
        r_sc = sb("r_sc", [128, 8, 16], F32)
        r_s4 = sb("r_s4", [128, 8, 16], F32)
        r_pr = sb("r_pr", [128, 8, 4, 6], F32)
        r_gt = sb("r_gt", [128, 8, 4], F32)
        r_m2 = sb("r_m2", [128, 8, 1], F32)
        r_gm = sb("r_gm", [128, 8, 16], F32)
        r_wk = sb("r_wk", [128, 8, 16], F32)
        r_ge = sb("r_ge", [128, 8, 16], F32)
        r_ti = sb("r_ti", [128, 8, 1], F32)
        r_sel = sb("r_sel", [128, 8, 16], F32)
        r_w = sb("r_w", [128, 8, 16], F32)

        psA = psmake("psA", [128, 1024])
        psB = psmake("psB", [128, 1024])
        psC = psmake("psC", [128, 1024])
        psD = psmake("psD", [128, 512])
        psE = psmake("psE", [128, 512])

        LANES_Y = ["Lcn", "Lx0", "Lx1", "Ls0", "Ls1", "Ls2", "Ls3", "Ls4", "Ls5",
                   "Lout", "Ltap"]
        LANES_G = ["Lg0", "Lg1", "Lg2", "Lg3", "Lg4", "Lg5"]
        SEMS = {}
        for nm in ["T", "S", "V", "G", "cc"] + LANES_Y + LANES_G:
            SEMS[nm] = stack.enter_context(nc.semaphore("sem" + nm))

        def eT(fn, waits=(), inc=False):
            c = P.bump("T") if inc else None
            P.emit("T", fn, waits, "T" if inc else None, 1)
            return c

        def eS(fn, waits=(), inc=True):
            c = P.bump("S") if inc else None
            P.emit("S", fn, waits, "S" if inc else None, 1)
            return c

        def eV(fn, waits=(), inc=True):
            c = P.bump("V") if inc else None
            P.emit("V", fn, waits, "V" if inc else None, 1)
            return c

        def eG(fn, waits=(), inc=True):
            c = P.bump("G") if inc else None
            P.emit("G", fn, waits, "G" if inc else None, 1)
            return c

        def dY(fn, lane, waits=()):
            c = P.bump(lane, 16)
            P.emit("Y", fn, waits, lane, 16)
            return (lane, c)

        def dG(fn, lane, waits=()):
            c = P.bump(lane, 16)
            P.emit("G", fn, waits, lane, 16)
            return (lane, c)

        def cG(fn, waits=()):
            c = P.bump("cc", 1)
            P.emit("G", fn, waits, "cc", 1)
            return c

        def yWait(waits):
            P.emit("Y", lambda e: None, waits, None, 1)

        RX = [Res() for _ in range(DC)]
        RA = [Res() for _ in range(DC)]
        RE = [Res() for _ in range(4)]
        RSR = [Res() for _ in range(4)]
        RPS = {k: Res() for k in ("A0", "A1", "B0", "B1", "C0", "C1", "D", "E")}
        arC = Arena()
        arW = Arena()
        r_kown0, r_vown0, r_kown1, r_vown1 = Res(), Res(), Res(), Res()
        r_qsc, r_qsc1 = Res(), Res()
        r_KG0, r_VG0, r_KG1, r_VG1 = Res(), Res(), Res(), Res()
        r_route = Res()
        r_wtm = Res()

        PAIR_AP = {"A": psA, "B": psB, "C": psC}

        def pair_res(p):
            return [RPS[p + "0"], RPS[p + "1"]]

        BANK_AP = {
            "A0": psA[:, 0:512], "A1": psA[:, 512:1024],
            "B0": psB[:, 0:512], "B1": psB[:, 512:1024],
            "C0": psC[:, 0:512], "C1": psC[:, 512:1024],
            "D": psD[:, :], "E": psE[:, :],
        }

        e_slot = [eA[:, i * 1024:(i + 1) * 1024] for i in range(4)]
        sr_slot = [srow[0:1, i * 512:(i + 1) * 512] for i in range(4)]

        consts = Res()
        for dst, src in ((identb, identb_h), (identf, identf_h), (permf, permf_h),
                         (onescolb, onescolb_h), (onescolf, onescolf_h),
                         (onesrowf, onesrowf_h), (neg20, neg20_h),
                         (ropec_s, ropec), (ropes_s, ropes), (sgate_s, sgate),
                         (sup_s, sup), (sdown_s, sdown), (b1q_s, b1q), (b1k_s, b1k),
                         (b1v_s, b1v), (b1o_s, b1o), (esb_s, esb)):
            dY(lambda e, d=dst, s=src: e.dma_start(out=d[...], in_=s[...]), "Lcn")
        cgw = dY(lambda e: e.dma_start(out=gwf[:, :, :], in_=gw.rearrange("(c p) n -> p c n", p=128)), "Lcn")
        consts.note_write(cgw)
        CW = consts.r_waits()

        ROT6 = ["A0", "A1", "B0", "B1", "C0", "C1"]
        rot_i = [0]

        def take_slot():
            k = ROT6[rot_i[0] % 6]
            rot_i[0] += 1
            return k

        xv = x_tm.rearrange("(tc p) d -> tc p d", p=128)
        stg = [eA[:, 0:2048], eA[:, 2048:4096]]
        for tc in range(8):
            sl = tc % 2
            ld = dY(lambda e, sl=sl, tc=tc: e.dma_start(out=stg[sl][...], in_=xv[tc]),
                    "Lx0" if sl == 0 else "Lx1",
                    waits=wwait(RE[2 * sl], RE[2 * sl + 1]))
            RE[2 * sl].note_write(ld)
            RE[2 * sl + 1].note_write(ld)
            for g in range(4):
                bk = take_slot()
                bap = BANK_AP[bk]
                waits = [ld] + wwait(RPS[bk]) + CW
                cT = None
                for j in range(4):
                    dc = g * 4 + j
                    cT = eT(lambda e, bap=bap, j=j, sl=sl, dc=dc:
                            e.transpose(bap[:, j * 128:(j + 1) * 128],
                                        stg[sl][:, dc * 128:(dc + 1) * 128], identf[...]),
                            waits=waits if j == 0 else (), inc=(j == 3))
                RPS[bk].note_write(("T", cT))
                RE[2 * sl].note_read(("T", cT))
                RE[2 * sl + 1].note_read(("T", cT))
                cV = eV(lambda e, bap=bap, g=g, tc=tc:
                        e.tensor_copy(xres[:, g * 4:(g + 1) * 4, tc * 128:(tc + 1) * 128],
                                      bap.rearrange("p (a b) -> p a b", a=4)),
                        waits=[("T", cT)] + wwait(*RX[g * 4:(g + 1) * 4]))
                RPS[bk].note_read(("V", cV))
                RE[2 * sl].note_read(("V", cV))
                RE[2 * sl + 1].note_read(("V", cV))
                for dc in range(g * 4, g * 4 + 4):
                    RX[dc].note_write(("V", cV))

        def emit_ln(layer):
            xb_reg = [arC.region(0, 1024), arC.region(1024, 2048)]
            sq_reg = [arC.region(2048, 3072), arC.region(3072, 4096)]
            xb_ap = [aC[:, 0:1024], aC[:, 1024:2048]]
            sq_ap = [aC[:, 2048:3072], aC[:, 3072:4096]]
            sum_banks = [("D", psD[0:1, :]), ("E", psE[0:1, :])]
            ssq_banks = [("C0", psC[0:1, 0:512]), ("C1", psC[0:1, 512:1024])]
            for dc in range(DC):
                sl = dc % 2
                c1 = eS(lambda e, sl=sl, dc=dc: e.copy(xb_ap[sl][...], xres[:, dc, :]),
                        waits=rwait(RX[dc]) + wwait(xb_reg[sl]))
                xb_reg[sl].note_write(("S", c1))
                RX[dc].note_read(("S", c1))
                c2 = eS(lambda e, sl=sl, dc=dc: e.activation(sq_ap[sl][...], xres[:, dc, :], AF.Square),
                        waits=rwait(RX[dc]) + wwait(sq_reg[sl]))
                sq_reg[sl].note_write(("S", c2))
                RX[dc].note_read(("S", c2))
                cT = None
                for th in range(2):
                    bk, bap = sum_banks[th]
                    w = [("S", c1)] + (wwait(RPS[bk]) if dc == 0 else [])
                    eT(lambda e, bap=bap, sl=sl, th=th, dc=dc:
                       e.matmul(bap, onescolb[...], xb_ap[sl][:, th * 512:(th + 1) * 512],
                                start=(dc == 0), stop=(dc == DC - 1)),
                       waits=w)
                    bk2, bap2 = ssq_banks[th]
                    w = [("S", c2)] + (wwait(RPS[bk2]) if dc == 0 else [])
                    cT = eT(lambda e, bap2=bap2, sl=sl, th=th, dc=dc:
                            e.matmul(bap2, onescolb[...], sq_ap[sl][:, th * 512:(th + 1) * 512],
                                     start=(dc == 0), stop=(dc == DC - 1)),
                            waits=w, inc=(th == 1))
                if dc == DC - 1:
                    for bk in ("D", "E", "C0", "C1"):
                        RPS[bk].note_write(("T", cT))
                xb_reg[sl].note_read(("T", cT))
                sq_reg[sl].note_read(("T", cT))
            cm0 = eV(lambda e: e.tensor_scalar(sr_slot[0][...], psD[0:1, :], 1.0 / D, None, ALU.mult),
                     waits=rwait(RPS["D"]) + wwait(RSR[0]))
            cm1 = eV(lambda e: e.tensor_scalar(sr_slot[1][...], psE[0:1, :], 1.0 / D, None, ALU.mult),
                     waits=rwait(RPS["E"]) + wwait(RSR[1]))
            cq = eV(lambda e: e.tensor_scalar(srow[0:1, 1024:2048], psC[0:1, 0:1024], 1.0 / D, None, ALU.mult),
                    waits=rwait(RPS["C0"], RPS["C1"]) + wwait(RSR[2], RSR[3]))
            ct0 = eV(lambda e: e.tensor_tensor(psD[0:1, :], sr_slot[0][...], sr_slot[0][...], ALU.mult),
                     waits=[("V", cm0)])
            ct1 = eV(lambda e: e.tensor_tensor(psE[0:1, :], sr_slot[1][...], sr_slot[1][...], ALU.mult),
                     waits=[("V", cm1)])
            cv0 = eV(lambda e: e.tensor_tensor(srow[0:1, 1024:1536], srow[0:1, 1024:1536], psD[0:1, :], ALU.subtract),
                     waits=[("V", ct0), ("V", cq)])
            cv1 = eV(lambda e: e.tensor_tensor(srow[0:1, 1536:2048], srow[0:1, 1536:2048], psE[0:1, :], ALU.subtract),
                     waits=[("V", ct1), ("V", cq)])
            ce = eV(lambda e: e.tensor_scalar(srow[0:1, 1024:2048], srow[0:1, 1024:2048], LN_EPS, None, ALU.add),
                    waits=[("V", cv0), ("V", cv1)])
            cr0 = eV(lambda e: e.reciprocal(psD[0:1, :], srow[0:1, 1024:1536]), waits=[("V", ce)])
            cr1 = eV(lambda e: e.reciprocal(psE[0:1, :], srow[0:1, 1536:2048]), waits=[("V", ce)])
            cs0 = eS(lambda e: e.activation(srow[0:1, 1024:1536], psD[0:1, :], AF.Sqrt), waits=[("V", cr0)])
            cs1 = eS(lambda e: e.activation(srow[0:1, 1536:2048], psE[0:1, :], AF.Sqrt), waits=[("V", cr1)])
            cmr = eV(lambda e: e.tensor_tensor(srow[0:1, 0:1024], srow[0:1, 0:1024], srow[0:1, 1024:2048], ALU.mult),
                     waits=[("S", cs0), ("S", cs1)])
            RPS["D"].note_write(("V", cr0))
            RPS["E"].note_write(("V", cr1))
            RPS["D"].note_read(("S", cs0))
            RPS["E"].note_read(("S", cs1))
            for r in (RSR[0], RSR[1]):
                r.note_write(("V", cmr))
            for r in (RSR[2], RSR[3]):
                r.note_write(("V", cmr))
            w_ab = [("S", cs1), ("V", cmr)] + wwait(RPS["A0"], RPS["A1"], RPS["B0"], RPS["B1"]) + CW
            cb2 = None
            for th in range(2):
                eT(lambda e, th=th: e.matmul(psA[:, th * 512:(th + 1) * 512], onesrowf[...],
                                             srow[0:1, 1024 + th * 512:1536 + th * 512],
                                             start=True, stop=True),
                   waits=w_ab if th == 0 else ())
                cb2 = eT(lambda e, th=th: e.matmul(psB[:, th * 512:(th + 1) * 512], onesrowf[...],
                                                   srow[0:1, th * 512:(th + 1) * 512],
                                                   start=True, stop=True),
                         inc=(th == 1))
            for k in ("A0", "A1", "B0", "B1"):
                RPS[k].note_write(("T", cb2))
            for r in RSR:
                r.note_read(("T", cb2))
            if layer == 1:
                cbe = eT(lambda e: e.matmul(psD[:, 0:16], onesrowf[...], esb_s[0:1, :], start=True, stop=True),
                         waits=wwait(RPS["D"]) + CW, inc=True)
                cse = eS(lambda e: e.copy(esbb[:, :], psD[:, 0:16]), waits=[("T", cbe)])
                RPS["D"].note_write(("T", cbe))
                RPS["D"].note_read(("S", cse))
                r_route.note_write(("S", cse))
            for dc in range(DC):
                tsl = dc % 2
                c1 = eV(lambda e, tsl=tsl, dc=dc: e.tensor_tensor(e_slot[tsl][...], xres[:, dc, :], psA[:, :], ALU.mult),
                        waits=rwait(RX[dc]) + [("T", cb2)] + wwait(RE[tsl]))
                RE[tsl].note_write(("V", c1))
                RX[dc].note_read(("V", c1))
                if layer == 0:
                    c2 = eV(lambda e, tsl=tsl, dc=dc: e.tensor_tensor(aA[:, dc, :], e_slot[tsl][...], psB[:, :], ALU.subtract),
                            waits=[("V", c1)] + wwait(RA[dc]))
                    RA[dc].note_write(("V", c2))
                    RE[tsl].note_read(("V", c2))
                else:
                    fsl = 2 + (dc % 2)
                    c2 = eV(lambda e, fsl=fsl, tsl=tsl: e.tensor_tensor(e_slot[fsl][...], e_slot[tsl][...], psB[:, :], ALU.subtract),
                            waits=[("V", c1)] + wwait(RE[fsl]))
                    RE[fsl].note_write(("V", c2))
                    RE[tsl].note_read(("V", c2))
                    c3 = eS(lambda e, fsl=fsl, dc=dc: e.copy(aA[:, dc, :], e_slot[fsl][...]),
                            waits=[("V", c2)] + wwait(RA[dc]))
                    RA[dc].note_write(("S", c3))
                    RE[fsl].note_read(("S", c3))
                    w0 = [("V", c2)] + CW
                    if dc == 0:
                        w0 += wwait(RPS["D"])
                    cr = None
                    for tc in range(8):
                        cr = eT(lambda e, fsl=fsl, tc=tc, dc=dc:
                                e.matmul(psD[:, tc * 16:(tc + 1) * 16],
                                         e_slot[fsl][:, tc * 128:(tc + 1) * 128],
                                         gwf[:, dc, :],
                                         start=(dc == 0 and tc == 0), stop=(dc == DC - 1),
                                         skip_group_check=True),
                                waits=w0 if tc == 0 else (), inc=(tc == 7))
                    RE[fsl].note_read(("T", cr))
                    if dc == DC - 1:
                        RPS["D"].note_write(("T", cr))
                for k in ("A0", "A1", "B0", "B1"):
                    RPS[k].note_read(("V", c2))

        if stages >= 1:
            emit_ln(0)

        def tap_sbuf(name, src_ap, waits):
            if name in taps:
                dY(lambda e, s=src_ap, o=tap_outs[name]: e.dma_start(out=o[...], in_=s), "Ltap", waits=waits)

        def tap_dram(name, dram, res):
            if name in taps:
                dY(lambda e, d=dram, o=tap_outs[name]: e.dma_start(out=o[...], in_=d.ap()),
                   "Ltap", waits=rwait(res))

        if stages >= 1:
            tap_sbuf("xn", aA[:, :, :], rwait(*RA))

        WSLOT_OFF = [0, 8192]

        def wslab_ap(off):
            return aW[:, off:off + 8192].rearrange("p (c o) -> p c o", c=16)

        arW_slots = [arW.region(0, 8192), arW.region(8192, 16384)]

        def emit_proj(w_hbm, epilogue):
            pair_rot = ["A", "B", "C"]
            pi = [0]
            for s in range(4):
                sslot = s % 2
                off = WSLOT_OFF[sslot]
                wreg = arW_slots[sslot]
                cl = dG(lambda e, s=s, off=off: e.dma_start(out=wslab_ap(off)[...],
                                                            in_=w_hbm[s]),
                        "Lg0" if sslot == 0 else "Lg1",
                        waits=wwait(wreg))
                wreg.note_write(cl)
                for j in range(4):
                    dc = s * 4 + j
                    pk = pair_rot[pi[0] % 3]
                    pi[0] += 1
                    pap = PAIR_AP[pk]
                    w = [cl] + wwait(*pair_res(pk)) + rwait(*RA)
                    cnt = None
                    for hc in range(DC):
                        for th in range(2):
                            cnt = eT(lambda e, pap=pap, off=off, hc=hc, j=j, th=th:
                                     e.matmul(pap[:, th * 512:(th + 1) * 512],
                                              wslab_ap(off)[:, hc, j * 128:(j + 1) * 128],
                                              aA[:, hc, th * 512:(th + 1) * 512],
                                              start=(hc == 0), stop=(hc == DC - 1)),
                                     waits=w if (hc == 0 and th == 0) else (),
                                     inc=(hc == DC - 1 and th == 1))
                    for r in pair_res(pk):
                        r.note_write(("T", cnt))
                    for r in RA:
                        r.note_read(("T", cnt))
                    wreg.note_read(("T", cnt))
                    epilogue(dc, pk, cnt)

        def kv_regions():
            return {
                "kst": ([arC.region(0, 1024), arC.region(1024, 2048)],
                        [aC[:, 0:1024], aC[:, 1024:2048]]),
                "vst": ([arC.region(2048, 3072), arC.region(3072, 4096)],
                        [aC[:, 2048:3072], aC[:, 3072:4096]]),
                "qst": ([arC.region(4096, 5120), arC.region(5120, 6144)],
                        [aC[:, 4096:5120], aC[:, 5120:6144]]),
                "vstg": ([arC.region(6144, 7168), arC.region(7168, 8192)],
                         [aC[:, 6144:7168].rearrange("p (a b) -> p a b", a=8),
                          aC[:, 7168:8192].rearrange("p (a b) -> p a b", a=8)]),
            }

        def rope_epilogue(dst_regs, dst_aps, store_fn):
            def ep(dc, pk, cnt):
                sl = dc % 2
                pap = PAIR_AP[pk]
                qf = e_slot[sl]
                c1 = eS(lambda e, qf=qf, pap=pap: e.copy(qf[...], pap[:, :]),
                        waits=[("T", cnt)] + wwait(RE[sl]))
                RE[sl].note_write(("S", c1))
                for r in pair_res(pk):
                    r.note_read(("S", c1))
                cp = None
                for th, bk in ((0, "D"), (1, "E")):
                    cp = eT(lambda e, qf=qf, th=th, bk=bk:
                            e.matmul(BANK_AP[bk], permf[...], qf[:, th * 512:(th + 1) * 512],
                                     start=True, stop=True),
                            waits=[("S", c1)] + wwait(RPS[bk]) + CW, inc=(th == 1))
                RPS["D"].note_write(("T", cp))
                RPS["E"].note_write(("T", cp))
                RE[sl].note_read(("T", cp))
                ct1a = eV(lambda e: e.tensor_tensor(e_slot[2][:, 0:512], psD[:, :], ropes_s[:, 0:512], ALU.mult),
                          waits=[("T", cp)] + wwait(RE[2]) + CW)
                ct1b = eV(lambda e: e.tensor_tensor(e_slot[2][:, 512:1024], psE[:, :], ropes_s[:, 512:1024], ALU.mult),
                          waits=[("T", cp)])
                RE[2].note_write(("V", ct1b))
                RPS["D"].note_read(("V", ct1a))
                RPS["E"].note_read(("V", ct1b))
                ct2 = eV(lambda e, qf=qf: e.tensor_tensor(e_slot[3][...], qf[...], ropec_s[:, :], ALU.mult),
                         waits=[("S", c1)] + wwait(RE[3]))
                RE[3].note_write(("V", ct2))
                RE[sl].note_read(("V", ct2))
                cfin = eV(lambda e, sl=sl: e.tensor_tensor(dst_aps[sl][...], e_slot[2][...], e_slot[3][...], ALU.add),
                          waits=[("V", ct1b), ("V", ct2)] + wwait(dst_regs[sl]))
                dst_regs[sl].note_write(("V", cfin))
                RE[2].note_read(("V", cfin))
                RE[3].note_read(("V", cfin))
                store_fn(dc, sl, ("V", cfin))
            return ep

        def blk_store(dst_regs, dst_aps, dram_view, dram_res):
            def store(dc, sl, evt):
                cdma = dY(lambda e, sl=sl, dc=dc: e.dma_start(
                    out=dram_view[dc],
                    in_=dst_aps[sl][...].rearrange("p (tc x) -> p tc x", tc=8)),
                    "Ls0" if sl == 0 else "Ls1",
                    waits=[evt])
                dst_regs[sl].note_read(cdma)
                dram_res.note_write(cdma)
            return store

        def row_store(dst_regs, dst_aps, qdram, dram_res):
            def store(dc, sl, evt):
                cdma = dY(lambda e, sl=sl, dc=dc: e.dma_start(
                    out=qdram[dc * 128:(dc + 1) * 128, :], in_=dst_aps[sl][...]),
                    "Ls4" if sl == 0 else "Ls5",
                    waits=[evt])
                dst_regs[sl].note_read(cdma)
                dram_res.note_write(cdma)
            return store

        def relu_epilogue(bias_col, dst_regs, dst_aps, store_fn):
            def ep(dc, pk, cnt):
                sl = dc % 2
                pap = PAIR_AP[pk]
                c1 = eS(lambda e, sl=sl, pap=pap, dc=dc:
                        e.activation(dst_aps[sl][...], pap[:, :], AF.Relu,
                                     bias=bias_col[:, dc:dc + 1], scale=1.0),
                        waits=[("T", cnt)] + wwait(dst_regs[sl]) + CW)
                dst_regs[sl].note_write(("S", c1))
                for r in pair_res(pk):
                    r.note_read(("S", c1))
                store_fn(dc, sl, ("S", c1))
            return ep

        def vtrans_epilogue(vst_regs, vst_aps, stg_regs, stg_aps, dram_view, dram_res,
                            relu_bias=None):
            def ep(dc, pk, cnt):
                sl = dc % 2
                pap = PAIR_AP[pk]
                if relu_bias is None:
                    c1 = eS(lambda e, sl=sl, pap=pap: e.copy(vst_aps[sl][...], pap[:, :]),
                            waits=[("T", cnt)] + wwait(vst_regs[sl]))
                else:
                    c1 = eS(lambda e, sl=sl, pap=pap, dc=dc:
                            e.activation(vst_aps[sl][...], pap[:, :], AF.Relu,
                                         bias=relu_bias[:, dc:dc + 1], scale=1.0),
                            waits=[("T", cnt)] + wwait(vst_regs[sl]) + CW)
                vst_regs[sl].note_write(("S", c1))
                for r in pair_res(pk):
                    r.note_read(("S", c1))
                ctr3 = ctr7 = None
                for tc in range(8):
                    bk = "D" if tc < 4 else "E"
                    w = [("S", c1)] + (wwait(RPS[bk]) if tc % 4 == 0 else []) + CW
                    cc_ = eT(lambda e, sl=sl, tc=tc, bk=bk:
                             e.matmul(BANK_AP[bk][:, (tc % 4) * 128:(tc % 4 + 1) * 128],
                                      vst_aps[sl][:, tc * 128:(tc + 1) * 128], identb[...],
                                      start=(tc % 4 == 0), stop=(tc % 4 == 3),
                                      skip_group_check=True),
                             waits=w, inc=(tc % 4 == 3))
                    if tc == 3:
                        ctr3 = cc_
                    if tc == 7:
                        ctr7 = cc_
                RPS["D"].note_write(("T", ctr3))
                RPS["E"].note_write(("T", ctr7))
                vst_regs[sl].note_read(("T", ctr7))
                ssl = dc % 2
                c2 = eS(lambda e, ssl=ssl: e.copy(stg_aps[ssl][:, 0:4, :],
                                                  psD[:, :].rearrange("p (a b) -> p a b", a=4)),
                        waits=[("T", ctr3)] + wwait(stg_regs[ssl]))
                c3 = eS(lambda e, ssl=ssl: e.copy(stg_aps[ssl][:, 4:8, :],
                                                  psE[:, :].rearrange("p (a b) -> p a b", a=4)),
                        waits=[("T", ctr7)])
                stg_regs[ssl].note_write(("S", c3))
                RPS["D"].note_read(("S", c2))
                RPS["E"].note_read(("S", c3))
                cdma = dY(lambda e, ssl=ssl, dc=dc: e.dma_start(
                    out=dram_view[dc], in_=stg_aps[ssl][...]),
                    "Ls2" if ssl == 0 else "Ls3",
                    waits=[("S", c3)])
                stg_regs[ssl].note_read(cdma)
                dram_res.note_write(cdma)
            return ep

        if stages >= 2:
            kv = kv_regions()
            emit_proj(wkT, rope_epilogue(kv["kst"][0], kv["kst"][1],
                                         blk_store(kv["kst"][0], kv["kst"][1], kown0v, r_kown0)))
            if stages >= 2.3:
                emit_proj(wvT, vtrans_epilogue(kv["vst"][0], kv["vst"][1],
                                               kv["vstg"][0], kv["vstg"][1], vown0v, r_vown0))
            if stages >= 2.6:
                emit_proj(wqT, rope_epilogue(kv["qst"][0], kv["qst"][1],
                                             row_store(kv["qst"][0], kv["qst"][1], qsc.ap(), r_qsc)))
            if stages >= 2.9:
                ck = cG(lambda e: e.collective_compute(
                    "AllGather", ALU.bypass, replica_groups=[list(range(NCORES))],
                    ins=[kown0.ap().opt()], outs=[KG0.ap().opt()]),
                    waits=rwait(r_kown0))
                r_KG0.note_write(("cc", ck))
                cv = cG(lambda e: e.collective_compute(
                    "AllGather", ALU.bypass, replica_groups=[list(range(NCORES))],
                    ins=[vown0.ap().opt()], outs=[VG0.ap().opt()]),
                    waits=rwait(r_vown0))
                r_VG0.note_write(("cc", cv))
            tap_dram("qsc", qsc, r_qsc)
            tap_dram("kown", kown0, r_kown0)
            tap_dram("vown", vown0, r_vown0)

        def emit_ffn(wg_h, wu_h, wd_h, inter, sg_col, su_col, sd_col):
            nslab = inter // 512
            fuse_reg = arC.region(0, 4096)
            fuse_ap = aC[:, 0:4096].rearrange("p (a b) -> p a b", a=4)
            wd_reg = arC.region(4096, 12288)
            wd_ap = aC[:, 4096:12288].rearrange("p (a b) -> p a b", a=4)
            gt_regs = [arC.region(12288, 12800), arC.region(12800, 13312)]
            gt_aps = [aC[:, 12288:12800], aC[:, 12800:13312]]
            ut_regs = [arC.region(13312, 13824), arC.region(13824, 14336)]
            ut_aps = [aC[:, 13312:13824], aC[:, 13824:14336]]
            wg_reg = arW_slots[0]
            wu_reg = arW_slots[1]
            for i in range(nslab):
                clg = dG(lambda e, i=i: e.dma_start(out=wslab_ap(0)[...],
                                                    in_=wg_h[i]),
                         "Lg0", waits=wwait(wg_reg))
                wg_reg.note_write(clg)
                clu = dG(lambda e, i=i: e.dma_start(out=wslab_ap(8192)[...],
                                                    in_=wu_h[i]),
                         "Lg1", waits=wwait(wu_reg))
                wu_reg.note_write(clu)
                cld = dG(lambda e, i=i: e.dma_start(out=wd_ap[...],
                                                    in_=wd_h[i]),
                         "Lg2", waits=wwait(wd_reg))
                wd_reg.note_write(cld)
                for oc in range(4):
                    og = i * 4 + oc
                    for th in range(2):
                        gsl = th
                        gk = take_slot()
                        w = [clg] + wwait(RPS[gk]) + rwait(*RA)
                        cg = None
                        for hc in range(DC):
                            cg = eT(lambda e, gk=gk, hc=hc, oc=oc, th=th:
                                    e.matmul(BANK_AP[gk], wslab_ap(0)[:, hc, oc * 128:(oc + 1) * 128],
                                             aA[:, hc, th * 512:(th + 1) * 512],
                                             start=(hc == 0), stop=(hc == DC - 1)),
                                    waits=w if hc == 0 else (), inc=(hc == DC - 1))
                        RPS[gk].note_write(("T", cg))
                        for r in RA:
                            r.note_read(("T", cg))
                        wg_reg.note_read(("T", cg))
                        if sg_col is not None:
                            cgs = eS(lambda e, gk=gk, gsl=gsl, og=og:
                                     e.activation(gt_aps[gsl][...], BANK_AP[gk], AF.Silu,
                                                  scale=sg_col[:, og:og + 1]),
                                     waits=[("T", cg)] + wwait(gt_regs[gsl]) + CW)
                        else:
                            cgs = eS(lambda e, gk=gk, gsl=gsl:
                                     e.activation(gt_aps[gsl][...], BANK_AP[gk], AF.Silu),
                                     waits=[("T", cg)] + wwait(gt_regs[gsl]))
                        gt_regs[gsl].note_write(("S", cgs))
                        RPS[gk].note_read(("S", cgs))
                        uk = take_slot()
                        w = [clu] + wwait(RPS[uk])
                        cu = None
                        for hc in range(DC):
                            cu = eT(lambda e, uk=uk, hc=hc, oc=oc, th=th:
                                    e.matmul(BANK_AP[uk], wslab_ap(8192)[:, hc, oc * 128:(oc + 1) * 128],
                                             aA[:, hc, th * 512:(th + 1) * 512],
                                             start=(hc == 0), stop=(hc == DC - 1)),
                                    waits=w if hc == 0 else (), inc=(hc == DC - 1))
                        RPS[uk].note_write(("T", cu))
                        for r in RA:
                            r.note_read(("T", cu))
                        wu_reg.note_read(("T", cu))
                        if su_col is not None:
                            cus = eV(lambda e, uk=uk, gsl=gsl, og=og:
                                     e.tensor_scalar(ut_aps[gsl][...], BANK_AP[uk],
                                                     su_col[:, og:og + 1], None, ALU.mult),
                                     waits=[("T", cu)] + wwait(ut_regs[gsl]) + CW)
                        else:
                            cus = eV(lambda e, uk=uk, gsl=gsl:
                                     e.tensor_copy(ut_aps[gsl][...], BANK_AP[uk]),
                                     waits=[("T", cu)] + wwait(ut_regs[gsl]))
                        ut_regs[gsl].note_write(("V", cus))
                        RPS[uk].note_read(("V", cus))
                        cf = eV(lambda e, gsl=gsl, oc=oc, th=th:
                                e.tensor_tensor(fuse_ap[:, oc, th * 512:(th + 1) * 512],
                                                gt_aps[gsl][...], ut_aps[gsl][...], ALU.mult),
                                waits=[("S", cgs), ("V", cus)] +
                                      (wwait(fuse_reg) if (oc == 0 and th == 0) else []))
                        gt_regs[gsl].note_read(("V", cf))
                        ut_regs[gsl].note_read(("V", cf))
                        if oc == 3 and th == 1:
                            fuse_reg.note_write(("V", cf))
                fuse_done = P.cnt["V"]
                for hc in range(DC):
                    for th in range(2):
                        dk = take_slot()
                        w = [cld, ("V", fuse_done)] + wwait(RPS[dk])
                        cd = None
                        for oc in range(4):
                            cd = eT(lambda e, dk=dk, oc=oc, hc=hc, th=th:
                                    e.matmul(BANK_AP[dk], wd_ap[:, oc, hc * 128:(hc + 1) * 128],
                                             fuse_ap[:, oc, th * 512:(th + 1) * 512],
                                             start=(oc == 0), stop=(oc == 3)),
                                    waits=w if oc == 0 else (), inc=(oc == 3))
                        RPS[dk].note_write(("T", cd))
                        wd_reg.note_read(("T", cd))
                        fuse_reg.note_read(("T", cd))
                        if sd_col is not None:
                            ca = eV(lambda e, dk=dk, hc=hc, th=th:
                                    e.scalar_tensor_tensor(xres[:, hc, th * 512:(th + 1) * 512],
                                                           BANK_AP[dk], sd_col[:, hc:hc + 1],
                                                           xres[:, hc, th * 512:(th + 1) * 512],
                                                           ALU.mult, ALU.add),
                                    waits=[("T", cd)] + wwait(RX[hc]) + CW)
                        else:
                            ca = eV(lambda e, dk=dk, hc=hc, th=th:
                                    e.tensor_tensor(xres[:, hc, th * 512:(th + 1) * 512],
                                                    BANK_AP[dk],
                                                    xres[:, hc, th * 512:(th + 1) * 512], ALU.add),
                                    waits=[("T", cd)] + wwait(RX[hc]))
                        RX[hc].note_write(("V", ca))
                        RPS[dk].note_read(("V", ca))

        if stages >= 3:
            emit_ffn(wg8, wu8, wd8, 8192, sgate_s, sup_s, sdown_s)
            tap_sbuf("x_ffn", xres[:, :, :], rwait(*RX))

        def emit_attn0():
            qh_regs = [arC.region(0, 1024), arC.region(1024, 2048)]
            qh_aps = [aC[:, 0:1024], aC[:, 1024:2048]]
            ks_regs = [arC.region(2048 + i * 1024, 3072 + i * 1024) for i in range(3)]
            ks_aps = [aC[:, 2048 + i * 1024:3072 + i * 1024].rearrange("p (a b) -> p a b", a=8)
                      for i in range(3)]
            vs_regs = [arC.region(5120 + i * 1024, 6144 + i * 1024) for i in range(3)]
            vs_aps = [aC[:, 5120 + i * 1024:6144 + i * 1024].rearrange("p (a b) -> p a b", a=8)
                      for i in range(3)]
            pt_regs = [arC.region(8192 + i * 1024, 9216 + i * 1024) for i in range(4)]
            pt_aps = [aC[:, 8192 + i * 1024:9216 + i * 1024] for i in range(4)]
            STP = {0: (psA, pair_res("A")), 1: (psB, pair_res("B"))}
            for h in range(16):
                qsl = h % 2
                cq = dY(lambda e, h=h, qsl=qsl: e.dma_start(out=qh_aps[qsl][...],
                                                            in_=qsc.ap()[h * 128:(h + 1) * 128, :]),
                        "Lx0" if qsl == 0 else "Lx1",
                        waits=wwait(qh_regs[qsl]) + rwait(r_qsc))
                qh_regs[qsl].note_write(cq)
                exp_cnt = {}
                for r in range(8):
                    ssl = r % 3
                    ck = dY(lambda e, r=r, h=h, ssl=ssl: e.dma_start(
                        out=ks_aps[ssl][...], in_=KG0v[r, h]),
                        "Ls0" if r % 2 == 0 else "Ls1",
                        waits=wwait(ks_regs[ssl]) + rwait(r_KG0))
                    ks_regs[ssl].note_write(ck)
                    cvl = dY(lambda e, r=r, h=h, ssl=ssl: e.dma_start(
                        out=vs_aps[ssl][...], in_=VG0v[r, h]),
                        "Ls2" if r % 2 == 0 else "Ls3",
                        waits=wwait(vs_regs[ssl]) + rwait(r_VG0))
                    vs_regs[ssl].note_write(cvl)
                    for kc in range(8):
                        kcg = r * 8 + kc
                        stp, stres = STP[kcg % 2]
                        w = [ck, cq] + wwait(*stres)
                        eT(lambda e, stp=stp, ssl=ssl, kc=kc, qsl=qsl:
                           e.matmul(stp[:, 0:512], ks_aps[ssl][:, kc, :], qh_aps[qsl][:, 0:512],
                                    start=True, stop=True),
                           waits=w)
                        cst = eT(lambda e, stp=stp, ssl=ssl, kc=kc, qsl=qsl:
                                 e.matmul(stp[:, 512:1024], ks_aps[ssl][:, kc, :], qh_aps[qsl][:, 512:1024],
                                          start=True, stop=True), inc=True)
                        for rr in stres:
                            rr.note_write(("T", cst))
                        ks_regs[ssl].note_read(("T", cst))
                        qh_regs[qsl].note_read(("T", cst))
                        psl = kcg % 4
                        cexp = eS(lambda e, stp=stp, psl=psl:
                                  e.activation(pt_aps[psl][...], stp[:, :], AF.Exp,
                                               bias=neg20[:, :], scale=INV_HD0),
                                  waits=[("T", cst)] + wwait(pt_regs[psl]) + CW)
                        exp_cnt[kcg] = cexp
                        pt_regs[psl].note_write(("S", cexp))
                        for rr in stres:
                            rr.note_read(("S", cexp))
                        wz = [("S", cexp), cvl]
                        if kcg == 0:
                            wz += wwait(RPS["D"], RPS["E"], RPS["C0"], RPS["C1"])
                        eT(lambda e, psl=psl, kcg=kcg: e.matmul(psD[0:1, :], onescolb[...],
                                                                pt_aps[psl][:, 0:512],
                                                                start=(kcg == 0), stop=(kcg == 63)),
                           waits=wz)
                        eT(lambda e, psl=psl, kcg=kcg: e.matmul(psE[0:1, :], onescolb[...],
                                                                pt_aps[psl][:, 512:1024],
                                                                start=(kcg == 0), stop=(kcg == 63)))
                        eT(lambda e, psl=psl, kcg=kcg, ssl=ssl, kc=kc:
                           e.matmul(psC[:, 0:512], vs_aps[ssl][:, kc, :], pt_aps[psl][:, 0:512],
                                    start=(kcg == 0), stop=(kcg == 63)))
                        cpv = eT(lambda e, psl=psl, kcg=kcg, ssl=ssl, kc=kc:
                                 e.matmul(psC[:, 512:1024], vs_aps[ssl][:, kc, :], pt_aps[psl][:, 512:1024],
                                          start=(kcg == 0), stop=(kcg == 63)), inc=True)
                        pt_regs[psl].note_read(("T", cpv))
                        vs_regs[ssl].note_read(("T", cpv))
                cpv63 = P.cnt["T"]
                for bk in ("D", "E", "C0", "C1"):
                    RPS[bk].note_write(("T", cpv63))
                cr0 = eV(lambda e: e.reciprocal(sr_slot[0][...], psD[0:1, :]),
                         waits=[("T", cpv63)] + wwait(RSR[0]))
                cr1 = eV(lambda e: e.reciprocal(sr_slot[1][...], psE[0:1, :]),
                         waits=[("T", cpv63)] + wwait(RSR[1]))
                RSR[0].note_write(("V", cr0))
                RSR[1].note_write(("V", cr1))
                RPS["D"].note_read(("V", cr0))
                RPS["E"].note_read(("V", cr1))
                cb = None
                wb = [("V", cr1), ("S", exp_cnt[63])] + wwait(RPS["A0"], RPS["A1"])
                for th in range(2):
                    cb = eT(lambda e, th=th: e.matmul(psA[:, th * 512:(th + 1) * 512], onesrowf[...],
                                                      sr_slot[th][...], start=True, stop=True),
                            waits=wb if th == 0 else (), inc=(th == 1))
                RPS["A0"].note_write(("T", cb))
                RPS["A1"].note_write(("T", cb))
                RSR[0].note_read(("T", cb))
                RSR[1].note_read(("T", cb))
                crc = eS(lambda e: e.copy(e_slot[2][...], psA[:, :]),
                         waits=[("T", cb)] + wwait(RE[2]))
                RE[2].note_write(("S", crc))
                RPS["A0"].note_read(("S", crc))
                RPS["A1"].note_read(("S", crc))
                cn = eV(lambda e, h=h: e.tensor_tensor(aA[:, h, :], psC[:, :], e_slot[2][...], ALU.mult),
                        waits=[("T", cpv63), ("S", crc)] + wwait(RA[h]))
                RA[h].note_write(("V", cn))
                RE[2].note_read(("V", cn))
                RPS["C0"].note_read(("V", cn))
                RPS["C1"].note_read(("V", cn))

        if stages >= 4:
            emit_attn0()
            tap_sbuf("attn0", aA[:, :, :], rwait(*RA))

        def emit_oproj(w_hbm, relu_bias=None):
            def ep(dc, pk, cnt):
                pap = PAIR_AP[pk]
                if relu_bias is None:
                    ca = eV(lambda e, pap=pap, dc=dc: e.tensor_tensor(
                        xres[:, dc, :], pap[:, :], xres[:, dc, :], ALU.add),
                        waits=[("T", cnt)] + wwait(RX[dc]))
                    RX[dc].note_write(("V", ca))
                    for r in pair_res(pk):
                        r.note_read(("V", ca))
                else:
                    esl = dc % 2
                    cs = eS(lambda e, pap=pap, dc=dc, esl=esl: e.activation(
                        e_slot[esl][...], pap[:, :], AF.Relu,
                        bias=relu_bias[:, dc:dc + 1], scale=1.0),
                        waits=[("T", cnt)] + wwait(RE[esl]) + CW)
                    RE[esl].note_write(("S", cs))
                    for r in pair_res(pk):
                        r.note_read(("S", cs))
                    ca = eV(lambda e, dc=dc, esl=esl: e.tensor_tensor(
                        xres[:, dc, :], e_slot[esl][...], xres[:, dc, :], ALU.add),
                        waits=[("S", cs)] + wwait(RX[dc]))
                    RX[dc].note_write(("V", ca))
                    RE[esl].note_read(("V", ca))
            emit_proj(w_hbm, ep)

        if stages >= 5:
            emit_oproj(woT)
            tap_sbuf("x_l0", xres[:, :, :], rwait(*RX))

        if stages >= 6:
            emit_ln(1)
            rw0 = rwait(RPS["D"]) + rwait(r_route)
            c1 = eS(lambda e: e.activation(r_sc[:, :, :],
                                           psD[:, 0:128].rearrange("p (a b) -> p a b", a=8),
                                           AF.Sigmoid), waits=rw0)
            RPS["D"].note_read(("S", c1))
            c2 = eV(lambda e: e.tensor_tensor(r_s4[:, :, :], r_sc[:, :, :],
                                              esbb[:, :].unsqueeze(1).broadcast_to((128, 8, 16)),
                                              ALU.add),
                    waits=[("S", c1)] + rwait(r_route))
            s4g = r_s4[:, :, :].rearrange("p a (g e) -> p a g e", g=4)
            c3 = eV(lambda e: e.tensor_tensor(r_pr[:, :, :, 0:3], s4g[:, :, :, 0:3], s4g[:, :, :, 1:4], ALU.add),
                    waits=[("V", c2)])
            c4 = eV(lambda e: e.tensor_tensor(r_pr[:, :, :, 3:5], s4g[:, :, :, 0:2], s4g[:, :, :, 2:4], ALU.add),
                    waits=[("V", c3)])
            c5 = eV(lambda e: e.tensor_tensor(r_pr[:, :, :, 5:6], s4g[:, :, :, 0:1], s4g[:, :, :, 3:4], ALU.add),
                    waits=[("V", c4)])
            c6 = eV(lambda e: e.tensor_reduce(r_gt[:, :, :], r_pr[:, :, :, :], axis=AXX, op=ALU.max),
                    waits=[("V", c5)])
            c7 = eV(lambda e: e.tensor_tensor(r_pr[:, :, 0, 0:3], r_gt[:, :, 0:3], r_gt[:, :, 1:4], ALU.min),
                    waits=[("V", c6)])
            c8 = eV(lambda e: e.tensor_tensor(r_pr[:, :, 0, 3:5], r_gt[:, :, 0:2], r_gt[:, :, 2:4], ALU.min),
                    waits=[("V", c7)])
            c9 = eV(lambda e: e.tensor_tensor(r_pr[:, :, 0, 5:6], r_gt[:, :, 0:1], r_gt[:, :, 3:4], ALU.min),
                    waits=[("V", c8)])
            c10 = eV(lambda e: e.tensor_reduce(r_m2[:, :, :], r_pr[:, :, 0, :], axis=AXX, op=ALU.max),
                     waits=[("V", c9)])
            c11 = eV(lambda e: e.tensor_tensor(r_gt[:, :, :], r_gt[:, :, :],
                                               r_m2[:, :, :].broadcast_to((128, 8, 4)), ALU.is_ge),
                     waits=[("V", c10)])
            c12 = eV(lambda e: e.tensor_copy(r_gm[:, :, :].rearrange("p a (g e) -> p a g e", g=4),
                                             r_gt[:, :, :].unsqueeze(3).broadcast_to((128, 8, 4, 4))),
                     waits=[("V", c11)])
            c13a = eV(lambda e: e.tensor_tensor(r_wk[:, :, :], r_s4[:, :, :], r_gm[:, :, :], ALU.mult),
                      waits=[("V", c12)])
            c13b = eV(lambda e: e.tensor_scalar(r_gm[:, :, :], r_gm[:, :, :], -1.0, None, ALU.add),
                      waits=[("V", c13a)])
            c14 = eV(lambda e: e.scalar_tensor_tensor(r_gm[:, :, :], r_gm[:, :, :], 1e30,
                                                      r_wk[:, :, :], ALU.mult, ALU.add),
                     waits=[("V", c13b)])
            prev = eV(lambda e: e.tensor_copy(r_wk[:, :, :], r_gm[:, :, :]), waits=[("V", c14)])
            for it in range(3):
                ct = eV(lambda e: e.tensor_reduce(r_ti[:, :, :], r_wk[:, :, :], axis=AXX, op=ALU.max),
                        waits=[("V", prev)])
                cg_ = eV(lambda e: e.tensor_tensor(r_ge[:, :, :], r_wk[:, :, :],
                                                   r_ti[:, :, :].broadcast_to((128, 8, 16)), ALU.is_ge),
                         waits=[("V", ct)])
                prev = eV(lambda e: e.scalar_tensor_tensor(r_wk[:, :, :], r_ge[:, :, :], -1e30,
                                                           r_wk[:, :, :], ALU.mult, ALU.add),
                          waits=[("V", cg_)])
            cti = eV(lambda e: e.tensor_reduce(r_ti[:, :, :], r_wk[:, :, :], axis=AXX, op=ALU.max),
                     waits=[("V", prev)])
            csel = eV(lambda e: e.tensor_tensor(r_sel[:, :, :], r_gm[:, :, :],
                                                r_ti[:, :, :].broadcast_to((128, 8, 16)), ALU.is_ge),
                      waits=[("V", cti)])
            cwr = eV(lambda e: e.tensor_tensor(r_w[:, :, :], r_sc[:, :, :], r_sel[:, :, :], ALU.mult),
                     waits=[("V", csel)])
            czz = eV(lambda e: e.tensor_reduce(r_ti[:, :, :], r_w[:, :, :], axis=AXX, op=ALU.add),
                     waits=[("V", cwr)])
            cz2 = eV(lambda e: e.tensor_scalar(r_ti[:, :, :], r_ti[:, :, :], 1e-20, None, ALU.add),
                     waits=[("V", czz)])
            cz3 = eV(lambda e: e.reciprocal(r_m2[:, :, :], r_ti[:, :, :]), waits=[("V", cz2)])
            cz4 = eV(lambda e: e.tensor_tensor(r_w[:, :, :], r_w[:, :, :],
                                               r_m2[:, :, :].broadcast_to((128, 8, 16)), ALU.mult),
                     waits=[("V", cz3)])
            cz5 = eV(lambda e: e.tensor_scalar(r_w[:, :, :], r_w[:, :, :], 2.5, None, ALU.mult),
                     waits=[("V", cz4)])
            r_wtm.note_write(("V", cz5))
            if "wtm" in taps:
                dY(lambda e: e.dma_start(out=tap_outs["wtm"][...], in_=r_w[:, :, :]),
                   "Ltap", waits=rwait(r_wtm))
            if "rsc" in taps:
                dY(lambda e: e.dma_start(out=tap_outs["rsc"][...], in_=r_sc[:, :, :]),
                   "Ltap", waits=rwait(r_wtm))
            if "esbb" in taps:
                dY(lambda e: e.dma_start(out=tap_outs["esbb"][...], in_=esbb[:, :]),
                   "Ltap", waits=rwait(r_wtm))
            tap_sbuf("xn1", aA[:, :, :], rwait(*RA))

        if stages >= 7:
            kv = kv_regions()
            emit_proj(w1kT, relu_epilogue(b1k_s, kv["kst"][0], kv["kst"][1],
                                          blk_store(kv["kst"][0], kv["kst"][1], kown1v, r_kown1)))
            emit_proj(w1vT, vtrans_epilogue(kv["vst"][0], kv["vst"][1],
                                            kv["vstg"][0], kv["vstg"][1], vown1v, r_vown1,
                                            relu_bias=b1v_s))
            emit_proj(w1qT, relu_epilogue(b1q_s, kv["qst"][0], kv["qst"][1],
                                          row_store(kv["qst"][0], kv["qst"][1], qsc1.ap(), r_qsc1)))
            ck = cG(lambda e: e.collective_compute(
                "AllGather", ALU.bypass, replica_groups=[list(range(NCORES))],
                ins=[kown1.ap().opt()], outs=[KG1.ap().opt()]),
                waits=rwait(r_kown1))
            r_KG1.note_write(("cc", ck))
            cv = cG(lambda e: e.collective_compute(
                "AllGather", ALU.bypass, replica_groups=[list(range(NCORES))],
                ins=[vown1.ap().opt()], outs=[VG1.ap().opt()]),
                waits=rwait(r_vown1))
            r_VG1.note_write(("cc", cv))
            tap_dram("qsc1", qsc1, r_qsc1)
            tap_dram("kown1", kown1, r_kown1)
            tap_dram("vown1", vown1, r_vown1)

        def emit_moe():
            fuse_reg = arC.region(0, 8192)
            fuse_ap = aC[:, 0:8192].rearrange("p (a b) -> p a b", a=8)
            gt_regs = [arC.region(8192, 8704), arC.region(8704, 9216)]
            gt_aps = [aC[:, 8192:8704], aC[:, 8704:9216]]
            ut_regs = [arC.region(9216, 9728), arC.region(9728, 10240)]
            ut_aps = [aC[:, 9216:9728], aC[:, 9728:10240]]
            wg_regs = [arW.region(0, 4096), arW.region(4096, 8192)]
            wg_off = [0, 4096]
            wu_regs = [arW.region(8192, 12288), arW.region(12288, 16384)]
            wu_off = [8192, 12288]
            dn_regs = [arC.region(10240, 12288), arC.region(12288, 14336)]
            dn_off = [10240, 12288]

            def wpiece(off):
                return aW[:, off:off + 4096].rearrange("p (c o) -> p c o", c=16)

            def dpiece(off):
                return aC[:, off:off + 2048].rearrange("p (c o) -> p c o", c=8)

            for n in range(16):
                wex = rwait(r_wtm) + wwait(RPS["D"], RPS["E"]) + CW
                cx = None
                for tc in range(8):
                    bk = "D" if tc < 4 else "E"
                    cx = eT(lambda e, tc=tc, n=n, bk=bk:
                            e.matmul(BANK_AP[bk][0:1, (tc % 4) * 128:(tc % 4 + 1) * 128],
                                     r_w[:, tc, n:n + 1], identf[...],
                                     start=(tc % 4 == 0), stop=(tc % 4 == 3),
                                     skip_group_check=True),
                            waits=wex if tc == 0 else (), inc=(tc == 7))
                RPS["D"].note_write(("T", cx))
                RPS["E"].note_write(("T", cx))
                r_wtm.note_read(("T", cx))
                cw1 = eS(lambda e: e.copy(sr_slot[0][...], psD[0:1, :]),
                         waits=[("T", cx)] + wwait(RSR[0]))
                cw2 = eS(lambda e: e.copy(sr_slot[1][...], psE[0:1, :]),
                         waits=[("T", cx)] + wwait(RSR[1]))
                RSR[0].note_write(("S", cw1))
                RSR[1].note_write(("S", cw2))
                RPS["D"].note_read(("S", cw1))
                RPS["E"].note_read(("S", cw2))
                cb = None
                wbc = [("S", cw1), ("S", cw2)]
                for th in range(2):
                    bk = "D" if th == 0 else "E"
                    cb = eT(lambda e, th=th, bk=bk: e.matmul(BANK_AP[bk], onesrowf[...],
                                                             sr_slot[th][...], start=True, stop=True),
                            waits=wbc if th == 0 else (), inc=(th == 1))
                RPS["D"].note_write(("T", cb))
                RPS["E"].note_write(("T", cb))
                RSR[0].note_read(("T", cb))
                RSR[1].note_read(("T", cb))
                cbw = eS(lambda e: e.copy(e_slot[3][:, 0:512], psD[:, :]),
                         waits=[("T", cb)] + wwait(RE[3]))
                cbw2 = eS(lambda e: e.copy(e_slot[3][:, 512:1024], psE[:, :]))
                RE[3].note_write(("S", cbw2))
                RPS["D"].note_read(("S", cbw))
                RPS["E"].note_read(("S", cbw2))
                for fq in range(4):
                    gsl = fq % 2
                    clg = dG(lambda e, n=n, fq=fq, gsl=gsl: e.dma_start(
                        out=wpiece(wg_off[gsl])[...], in_=mgT[n, fq]),
                        "Lg0" if gsl == 0 else "Lg1",
                        waits=wwait(wg_regs[gsl]))
                    wg_regs[gsl].note_write(clg)
                    clu = dG(lambda e, n=n, fq=fq, gsl=gsl: e.dma_start(
                        out=wpiece(wu_off[gsl])[...], in_=muT[n, fq]),
                        "Lg2" if gsl == 0 else "Lg3",
                        waits=wwait(wu_regs[gsl]))
                    wu_regs[gsl].note_write(clu)
                    for fc in range(2):
                        fcg = fq * 2 + fc
                        for th in range(2):
                            tsl = th
                            gk = take_slot()
                            w = [clg] + wwait(RPS[gk]) + rwait(*RA)
                            cg2 = None
                            for hc in range(DC):
                                cg2 = eT(lambda e, gk=gk, gsl=gsl, hc=hc, fc=fc, th=th:
                                         e.matmul(BANK_AP[gk], wpiece(wg_off[gsl])[:, hc, fc * 128:(fc + 1) * 128],
                                                  aA[:, hc, th * 512:(th + 1) * 512],
                                                  start=(hc == 0), stop=(hc == DC - 1)),
                                         waits=w if hc == 0 else (), inc=(hc == DC - 1))
                            RPS[gk].note_write(("T", cg2))
                            for r in RA:
                                r.note_read(("T", cg2))
                            wg_regs[gsl].note_read(("T", cg2))
                            cgs = eS(lambda e, gk=gk, tsl=tsl: e.activation(gt_aps[tsl][...], BANK_AP[gk], AF.Silu),
                                     waits=[("T", cg2)] + wwait(gt_regs[tsl]))
                            gt_regs[tsl].note_write(("S", cgs))
                            RPS[gk].note_read(("S", cgs))
                            uk = take_slot()
                            w = [clu] + wwait(RPS[uk])
                            cu2 = None
                            for hc in range(DC):
                                cu2 = eT(lambda e, uk=uk, gsl=gsl, hc=hc, fc=fc, th=th:
                                         e.matmul(BANK_AP[uk], wpiece(wu_off[gsl])[:, hc, fc * 128:(fc + 1) * 128],
                                                  aA[:, hc, th * 512:(th + 1) * 512],
                                                  start=(hc == 0), stop=(hc == DC - 1)),
                                         waits=w if hc == 0 else (), inc=(hc == DC - 1))
                            RPS[uk].note_write(("T", cu2))
                            for r in RA:
                                r.note_read(("T", cu2))
                            wu_regs[gsl].note_read(("T", cu2))
                            cus = eV(lambda e, uk=uk, tsl=tsl, th=th:
                                     e.tensor_tensor(ut_aps[tsl][...], BANK_AP[uk],
                                                     e_slot[3][:, th * 512:(th + 1) * 512], ALU.mult),
                                     waits=[("T", cu2), ("S", cbw2)] + wwait(ut_regs[tsl]))
                            ut_regs[tsl].note_write(("V", cus))
                            RPS[uk].note_read(("V", cus))
                            RE[3].note_read(("V", cus))
                            cf = eV(lambda e, tsl=tsl, fcg=fcg, th=th:
                                    e.tensor_tensor(fuse_ap[:, fcg, th * 512:(th + 1) * 512],
                                                    gt_aps[tsl][...], ut_aps[tsl][...], ALU.mult),
                                    waits=[("S", cgs), ("V", cus)] +
                                          (wwait(fuse_reg) if (fcg == 0 and th == 0) else []))
                            gt_regs[tsl].note_read(("V", cf))
                            ut_regs[tsl].note_read(("V", cf))
                            if fcg == 7 and th == 1:
                                fuse_reg.note_write(("V", cf))
                fuse_done = P.cnt["V"]
                for hs in range(8):
                    dsl = hs % 2
                    cld = dG(lambda e, n=n, hs=hs, dsl=dsl: e.dma_start(
                        out=dpiece(dn_off[dsl])[...], in_=mdT[n, hs]),
                        "Lg4" if dsl == 0 else "Lg5",
                        waits=wwait(dn_regs[dsl]))
                    dn_regs[dsl].note_write(cld)
                    for hcl in range(2):
                        hc = hs * 2 + hcl
                        for th in range(2):
                            dk = take_slot()
                            w = [cld, ("V", fuse_done)] + wwait(RPS[dk])
                            cd = None
                            for fc in range(8):
                                cd = eT(lambda e, dk=dk, dsl=dsl, fc=fc, hcl=hcl, th=th:
                                        e.matmul(BANK_AP[dk], dpiece(dn_off[dsl])[:, fc, hcl * 128:(hcl + 1) * 128],
                                                 fuse_ap[:, fc, th * 512:(th + 1) * 512],
                                                 start=(fc == 0), stop=(fc == 7)),
                                        waits=w if fc == 0 else (), inc=(fc == 7))
                            RPS[dk].note_write(("T", cd))
                            dn_regs[dsl].note_read(("T", cd))
                            fuse_reg.note_read(("T", cd))
                            ca = eV(lambda e, dk=dk, hc=hc, th=th:
                                    e.tensor_tensor(xres[:, hc, th * 512:(th + 1) * 512],
                                                    BANK_AP[dk],
                                                    xres[:, hc, th * 512:(th + 1) * 512], ALU.add),
                                    waits=[("T", cd)] + wwait(RX[hc]))
                            RX[hc].note_write(("V", ca))
                            RPS[dk].note_read(("V", ca))

        if stages >= 8:
            emit_moe()
            tap_sbuf("x_moe", xres[:, :, :], rwait(*RX))

        if stages >= 9:
            emit_ffn(shgT, shuT, shdT, 2048, None, None, None)
            tap_sbuf("x_sh", xres[:, :, :], rwait(*RX))

        def emit_attn1():
            q2_reg = arC.region(0, 8192)
            q2_ap = aC[:, 0:8192].rearrange("p (a b) -> p a b", a=8)
            kb_regs = [arC.region(8192 + i * 1024, 9216 + i * 1024) for i in range(3)]
            kb_aps = [aC[:, 8192 + i * 1024:9216 + i * 1024].rearrange("p (a b) -> p a b", a=8)
                      for i in range(3)]
            vb_regs = [arC.region(11264 + i * 1024, 12288 + i * 1024) for i in range(3)]
            vb_aps = [aC[:, 11264 + i * 1024:12288 + i * 1024].rearrange("p (a b) -> p a b", a=8)
                      for i in range(3)]
            pt_regs = [arC.region(14336 + i * 256, 14592 + i * 256) for i in range(4)]
            pt_aps = [aC[:, 14336 + i * 256:14592 + i * 256] for i in range(4)]
            OUT2 = {0: psA[:, 0:256], 1: psA[:, 256:512], 2: psA[:, 512:768], 3: psA[:, 768:1024],
                    4: psB[:, 0:256], 5: psB[:, 256:512], 6: psB[:, 512:768], 7: psB[:, 768:1024]}
            O2RES = {0: "A0", 1: "A0", 2: "A1", 3: "A1", 4: "B0", 5: "B0", 6: "B1", 7: "B1"}
            SC4 = [psC[:, i * 256:(i + 1) * 256] for i in range(4)]
            RQ = [Res() for _ in range(4)]
            for i, bk in ((0, "C0"), (1, "C0"), (2, "C1"), (3, "C1")):
                RQ[i].rs = list(RPS[bk].w_waits())
            zacc = e_slot[3][:, 0:256]
            zrow = srow[0:1, 1024:1280]
            for h in range(2):
                cq = dY(lambda e, h=h: e.dma_start(
                    out=q2_ap[...], in_=qsc1v[:, h * 8:(h + 1) * 8, :]),
                    "Lx0", waits=wwait(q2_reg) + rwait(r_qsc1))
                q2_reg.note_write(cq)
                for qt in range(4):
                    lastZ = None
                    for r in range(8):
                        for tc in range(8):
                            kcg = r * 8 + tc
                            bsl = kcg % 3
                            ck = dY(lambda e, r=r, h=h, tc=tc, bsl=bsl: e.dma_start(
                                out=kb_aps[bsl][...],
                                in_=KG1v[r, tc, :, h * 8:(h + 1) * 8, :]),
                                "Ls0" if kcg % 2 == 0 else "Ls1",
                                waits=wwait(kb_regs[bsl]) + rwait(r_KG1))
                            kb_regs[bsl].note_write(ck)
                            cvl = dG(lambda e, r=r, h=h, tc=tc, bsl=bsl: e.dma_start(
                                out=vb_aps[bsl][...],
                                in_=VG1v[r, tc, :, h * 8:(h + 1) * 8, :]),
                                "Lg4" if kcg % 2 == 0 else "Lg5",
                                waits=wwait(vb_regs[bsl]) + rwait(r_VG1))
                            vb_regs[bsl].note_write(cvl)
                            sq = kcg % 4
                            w = [ck, cq] + wwait(RQ[sq])
                            cst = None
                            for dcl in range(8):
                                cst = eT(lambda e, sq=sq, bsl=bsl, dcl=dcl, qt=qt:
                                         e.matmul(SC4[sq], kb_aps[bsl][:, dcl, :],
                                                  q2_ap[:, dcl, qt * 256:(qt + 1) * 256],
                                                  start=(dcl == 0), stop=(dcl == 7),
                                                  skip_group_check=True),
                                         waits=w if dcl == 0 else (), inc=(dcl == 7))
                            RQ[sq].note_write(("T", cst))
                            kb_regs[bsl].note_read(("T", cst))
                            q2_reg.note_read(("T", cst))
                            psl = kcg % 4
                            cexp = eS(lambda e, sq=sq, psl=psl:
                                      e.activation(pt_aps[psl][...], SC4[sq], AF.Exp,
                                                   bias=neg20[:, :], scale=INV_HD1),
                                      waits=[("T", cst)] + wwait(pt_regs[psl]) + CW)
                            pt_regs[psl].note_write(("S", cexp))
                            RQ[sq].note_read(("S", cexp))
                            if kcg == 0:
                                cz = eV(lambda e, psl=psl: e.tensor_copy(zacc[...], pt_aps[psl][...]),
                                        waits=[("S", cexp)] + wwait(RE[3]))
                            else:
                                cz = eV(lambda e, psl=psl: e.tensor_tensor(
                                    zacc[...], zacc[...], pt_aps[psl][...], ALU.add),
                                    waits=[("S", cexp)] + wwait(RE[3]))
                            RE[3].note_write(("V", cz))
                            pt_regs[psl].note_read(("V", cz))
                            lastZ = cz
                            wz = [("S", cexp), cvl]
                            if kcg == 0:
                                wz += wwait(RPS["A0"], RPS["A1"], RPS["B0"], RPS["B1"])
                            cpv = None
                            for dcl in range(8):
                                cpv = eT(lambda e, psl=psl, kcg=kcg, bsl=bsl, dcl=dcl:
                                         e.matmul(OUT2[dcl], vb_aps[bsl][:, dcl, :], pt_aps[psl][...],
                                                  start=(kcg == 0 and dcl % 2 == 0),
                                                  stop=(kcg == 63 and dcl % 2 == 1),
                                                  skip_group_check=True),
                                         waits=wz if dcl == 0 else (), inc=(dcl == 7))
                            pt_regs[psl].note_read(("T", cpv))
                            vb_regs[bsl].note_read(("T", cpv))
                    cpv63 = P.cnt["T"]
                    for bk in ("A0", "A1", "B0", "B1"):
                        RPS[bk].note_write(("T", cpv63))
                    czf = eT(lambda e: e.matmul(psD[0:1, 0:256], onescolf[...], zacc[...],
                                                start=True, stop=True, skip_group_check=True),
                             waits=[("V", lastZ)] + wwait(RPS["D"]), inc=True)
                    RPS["D"].note_write(("T", czf))
                    RE[3].note_read(("T", czf))
                    crz = eS(lambda e: e.copy(zrow[...], psD[0:1, 0:256]),
                             waits=[("T", czf)] + wwait(RSR[2]))
                    RSR[2].note_write(("S", crz))
                    RPS["D"].note_read(("S", crz))
                    cbx = eT(lambda e: e.matmul(psE[:, 0:256], onesrowf[...], zrow[...],
                                                start=True, stop=True),
                             waits=[("S", crz)] + wwait(RPS["E"]), inc=True)
                    RPS["E"].note_write(("T", cbx))
                    RSR[2].note_read(("T", cbx))
                    crc = eV(lambda e: e.reciprocal(e_slot[2][:, 0:256], psE[:, 0:256]),
                             waits=[("T", cbx)] + wwait(RE[2]))
                    RE[2].note_write(("V", crc))
                    RPS["E"].note_read(("V", crc))
                    cn = None
                    for dcl in range(8):
                        wn = [("T", cpv63), ("V", crc)]
                        if qt == 0:
                            wn += wwait(RA[h * 8 + dcl])
                        cn = eV(lambda e, h=h, dcl=dcl, qt=qt: e.tensor_tensor(
                            aA[:, h * 8 + dcl, qt * 256:(qt + 1) * 256],
                            OUT2[dcl], e_slot[2][:, 0:256], ALU.mult),
                            waits=wn)
                        RPS[O2RES[dcl]].note_read(("V", cn))
                        RE[2].note_read(("V", cn))
                        if qt == 3:
                            RA[h * 8 + dcl].note_write(("V", cn))
            for bk in ("C0", "C1"):
                RPS[bk].note_write(("T", P.cnt["T"]))

        if stages >= 10:
            emit_attn1()
            tap_sbuf("attn1", aA[:, :, :], rwait(*RA))

        if stages >= 11:
            emit_oproj(w1oT, relu_bias=b1o_s)

        ost_ap = eA[:, 0:2048]
        for tc in range(8):
            cVlast = None
            for g in range(4):
                bk = take_slot()
                bap = BANK_AP[bk]
                waits = wwait(RPS[bk]) + CW
                for dc in range(g * 4, g * 4 + 4):
                    waits += rwait(RX[dc])
                cT = None
                for j in range(4):
                    dc = g * 4 + j
                    cT = eT(lambda e, bap=bap, j=j, dc=dc, tc=tc:
                            e.transpose(bap[:, j * 128:(j + 1) * 128],
                                        xres[:, dc, tc * 128:(tc + 1) * 128], identf[...]),
                            waits=waits if j == 0 else (), inc=(j == 3))
                RPS[bk].note_write(("T", cT))
                for dc in range(g * 4, g * 4 + 4):
                    RX[dc].note_read(("T", cT))
                cVlast = eV(lambda e, bap=bap, g=g: e.tensor_copy(ost_ap[:, g * 512:(g + 1) * 512], bap),
                            waits=[("T", cT)] + (wwait(RE[0], RE[1]) if g == 0 else []))
                RPS[bk].note_read(("V", cVlast))
            cd = dY(lambda e, tc=tc: e.dma_start(out=y_out[tc * 128:(tc + 1) * 128, :],
                                                 in_=ost_ap[...]),
                    "Lout", waits=[("V", cVlast)])
            RE[0].note_write(("V", cVlast))
            RE[1].note_write(("V", cVlast))
            RE[0].note_read(cd)
            RE[1].note_read(cd)
        yWait([(ln, P.cnt[ln]) for ln in LANES_Y if P.cnt.get(ln)])

        with nc.Block() as block:
            def runner(ek):
                lastw = {}

                def go(eng):
                    for waits, fn, inc, by in P.ops[ek]:
                        for s, v in waits:
                            if v > lastw.get(s, -1):
                                eng.wait_ge(SEMS[s], v)
                                lastw[s] = v
                        ins = fn(eng)
                        if inc is not None and ins is not None:
                            ins.then_inc(SEMS[inc], by)
                return go

            block.tensor(runner("T"))
            block.scalar(runner("S"))
            block.vector(runner("V"))
            block.gpsimd(runner("G"))
            block.sync(runner("Y"))

    _NC_CACHE[key] = nc
    return nc


def kernel(**inputs):
    maps = _prep(inputs)
    nc = build_nc()
    res = run_bass_kernel_spmd(nc, maps, core_ids=list(range(NCORES)))
    parts = [res.results[c]["y"] for c in range(NCORES)]
    out = np.concatenate(parts, axis=0).reshape(1, SEQ, D).astype(np.float32)
    if TAPS:
        kernel.tap_results = [{k: v for k, v in res.results[c].items() if k.startswith("tap_")}
                              for c in range(NCORES)]
    return out


# version 06
# speedup vs baseline: 1.1685x; 1.1426x over previous
import numpy as np
import ml_dtypes
import concourse.bass as bass
import concourse.mybir as mybir
from concourse.bass_utils import run_bass_kernel_spmd
from concourse.replica_groups import maybe_share_collective_output_space
from contextlib import ExitStack

F32 = mybir.dt.float32
BF16 = mybir.dt.bfloat16
I8 = mybir.dt.int8
AF = mybir.ActivationFunctionType
ALU = mybir.AluOpType
AXX = mybir.AxisListType.X

NCORES = 8
T = 1024
D = 2048
DC = 16
SEQ = 8192
INV_HD0 = 0.08838834764831845
INV_HD1 = 0.03125
LN_EPS = 1e-5

STAGES = 99
TAPS = ()


class Res:
    __slots__ = ("w", "rs")

    def __init__(self):
        self.w = None
        self.rs = []

    def r_waits(self):
        return [self.w] if self.w else []

    def w_waits(self):
        out = list(self.rs)
        if self.w:
            out.append(self.w)
        return out

    def note_read(self, evt):
        if evt:
            self.rs.append(evt)

    def note_write(self, evt):
        self.w = evt
        self.rs = []


def rwait(*rr):
    out = []
    for r in rr:
        out += r.r_waits()
    return out


def wwait(*rr):
    out = []
    for r in rr:
        out += r.w_waits()
    return out


class Arena:
    def __init__(self):
        self.regions = []

    def region(self, s, e):
        r = Res()
        pend, keep = [], []
        for (s0, e0, r0) in self.regions:
            if s0 < e and s < e0:
                pend += r0.w_waits()
            else:
                keep.append((s0, e0, r0))
        r.rs = pend
        self.regions = keep + [(s, e, r)]
        return r


def dedup(waits):
    m = {}
    for w in waits:
        if w is None:
            continue
        s, v = w
        if s not in m or v > m[s]:
            m[s] = v
    return tuple(sorted(m.items()))


class Prog:
    def __init__(self):
        self.ops = {k: [] for k in "TSVGY"}
        self.cnt = {}

    def bump(self, s, by=1):
        self.cnt[s] = self.cnt.get(s, 0) + by
        return self.cnt[s]

    def emit(self, ek, fn, waits, inc, by):
        self.ops[ek].append((dedup(waits), fn, inc, by))


def _rope_tables():
    scale = (np.arange(0, 128, 2).astype(np.float32) / np.float32(128.0)).astype(np.float32)
    inv_freq = (np.float32(1.0) / (np.float32(10000.0) ** scale)).astype(np.float32)
    t = np.arange(SEQ, dtype=np.float32)
    ang = (t[:, None] * inv_freq[None, :]).astype(np.float32)
    c = np.cos(ang).astype(np.float32)
    s = np.sin(ang).astype(np.float32)
    C = np.empty((128, SEQ), np.float32)
    S2 = np.empty((128, SEQ), np.float32)
    for i in range(64):
        C[2 * i] = c[:, i]
        C[2 * i + 1] = c[:, i]
        S2[2 * i] = -s[:, i]
        S2[2 * i + 1] = s[:, i]
    return C, S2


def _col128(v, dtype=np.float32):
    v = np.asarray(v).astype(dtype)
    n = v.shape[0] // 128
    return np.ascontiguousarray(v.reshape(n, 128).T)


def _tile_go(wT, nsl, W):
    c = wT.shape[0] // 128
    return np.ascontiguousarray(wT.reshape(c, 128, nsl, W).transpose(2, 1, 0, 3))


def _tile_dn(wT, nsl, cc):
    O = wT.shape[1]
    return np.ascontiguousarray(wT.reshape(nsl, cc, 128, O).transpose(0, 2, 1, 3))


def _prep(inputs):
    f32 = np.float32
    bf16 = ml_dtypes.bfloat16
    x = np.asarray(inputs["L0_attn_x"])
    sh = {}
    sh["wqT"] = _tile_go(np.asarray(inputs["L0_attn_wq"]).T, 4, 512)
    sh["wkT"] = _tile_go(np.asarray(inputs["L0_attn_wk"]).T, 4, 512)
    sh["wvT"] = _tile_go(np.asarray(inputs["L0_attn_wv"]).T, 4, 512)
    sh["woT"] = _tile_go(np.asarray(inputs["L0_attn_wo"]).T, 4, 512)
    sh["wg8"] = _tile_go(np.asarray(inputs["L0_ffn_w_gate"]), 16, 512)
    sh["wu8"] = _tile_go(np.asarray(inputs["L0_ffn_w_up"]), 16, 512)
    sh["wd8"] = _tile_dn(np.asarray(inputs["L0_ffn_w_down"]), 16, 4)
    sh["sgate"] = _col128(inputs["L0_ffn_s_gate"])
    sh["sup"] = _col128(inputs["L0_ffn_s_up"])
    sh["sdown"] = _col128(inputs["L0_ffn_s_down"])
    for nm, key in (("w1qT", "L1_attn_Wq"), ("w1kT", "L1_attn_Wk"),
                    ("w1vT", "L1_attn_Wv"), ("w1oT", "L1_attn_Wo")):
        sh[nm] = _tile_go(np.asarray(inputs[key]).T, 4, 512)
    for nm, key in (("b1q", "L1_attn_bq"), ("b1k", "L1_attn_bk"),
                    ("b1v", "L1_attn_bv"), ("b1o", "L1_attn_bo")):
        sh[nm] = _col128(np.asarray(inputs[key]).astype(f32))
    sh["gw"] = np.ascontiguousarray(np.asarray(inputs["L1_ffn_gate_weight"]).T)
    sh["esb"] = np.asarray(inputs["L1_ffn_e_score_correction_bias"]).astype(f32).reshape(1, 16)
    mgT = np.asarray(inputs["L1_ffn_gate_w"]).transpose(0, 2, 1)
    muT = np.asarray(inputs["L1_ffn_up_w"]).transpose(0, 2, 1)
    mdT = np.asarray(inputs["L1_ffn_down_w"]).transpose(0, 2, 1)
    sh["mgT"] = np.ascontiguousarray(mgT.reshape(16, 16, 128, 4, 256).transpose(0, 3, 2, 1, 4))
    sh["muT"] = np.ascontiguousarray(muT.reshape(16, 16, 128, 4, 256).transpose(0, 3, 2, 1, 4))
    sh["mdT"] = np.ascontiguousarray(mdT.reshape(16, 8, 128, 8, 256).transpose(0, 3, 2, 1, 4))
    sh["shgT"] = _tile_go(np.asarray(inputs["L1_ffn_sh_gate_w"]).T, 4, 512)
    sh["shuT"] = _tile_go(np.asarray(inputs["L1_ffn_sh_up_w"]).T, 4, 512)
    sh["shdT"] = _tile_dn(np.asarray(inputs["L1_ffn_sh_down_w"]).T, 4, 4)
    sh["identb"] = np.eye(128, dtype=np.float32).astype(bf16)
    sh["identf"] = np.eye(128, dtype=np.float32)
    perm = np.zeros((128, 128), np.float32)
    for i in range(128):
        perm[i, i ^ 1] = 1.0
    sh["permf"] = perm
    sh["onescolb"] = np.ones((128, 1), np.float32).astype(bf16)
    sh["onescolf"] = np.ones((128, 1), np.float32)
    sh["onesrowf"] = np.ones((1, 128), np.float32)
    sh["neg20"] = np.full((128, 1), -20.0, np.float32)

    C, S2 = _rope_tables()
    maps = []
    for c in range(NCORES):
        m = dict(sh)
        m["x_tm"] = np.ascontiguousarray(x[0, c * T:(c + 1) * T, :]).astype(f32)
        m["ropec"] = np.ascontiguousarray(C[:, c * T:(c + 1) * T])
        m["ropes"] = np.ascontiguousarray(S2[:, c * T:(c + 1) * T])
        maps.append(m)
    return maps


_NC_CACHE = {}


def build_nc(stages=None, taps=None):
    stages = STAGES if stages is None else stages
    taps = TAPS if taps is None else taps
    key = (stages, tuple(taps))
    if key in _NC_CACHE:
        return _NC_CACHE[key]

    nc = bass.Bass("TRN2", target_bir_lowering=False)
    P = Prog()

    def din(name, shape, dt):
        return nc.dram_tensor(name, shape, dt, kind="ExternalInput").ap()

    x_tm = din("x_tm", [T, D], F32)
    wqT = din("wqT", [4, 128, 16, 512], F32)
    wkT = din("wkT", [4, 128, 16, 512], F32)
    wvT = din("wvT", [4, 128, 16, 512], F32)
    woT = din("woT", [4, 128, 16, 512], F32)
    wg8 = din("wg8", [16, 128, 16, 512], I8)
    wu8 = din("wu8", [16, 128, 16, 512], I8)
    wd8 = din("wd8", [16, 128, 4, 2048], I8)
    sgate = din("sgate", [128, 64], F32)
    sup = din("sup", [128, 64], F32)
    sdown = din("sdown", [128, 16], F32)
    w1qT = din("w1qT", [4, 128, 16, 512], BF16)
    w1kT = din("w1kT", [4, 128, 16, 512], BF16)
    w1vT = din("w1vT", [4, 128, 16, 512], BF16)
    w1oT = din("w1oT", [4, 128, 16, 512], BF16)
    b1q = din("b1q", [128, 16], F32)
    b1k = din("b1k", [128, 16], F32)
    b1v = din("b1v", [128, 16], F32)
    b1o = din("b1o", [128, 16], F32)
    gw = din("gw", [D, 16], F32)
    esb = din("esb", [1, 16], F32)
    mgT = din("mgT", [16, 4, 128, 16, 256], F32)
    muT = din("muT", [16, 4, 128, 16, 256], F32)
    mdT = din("mdT", [16, 8, 128, 8, 256], F32)
    shgT = din("shgT", [4, 128, 16, 512], F32)
    shuT = din("shuT", [4, 128, 16, 512], F32)
    shdT = din("shdT", [4, 128, 4, 2048], F32)
    ropec = din("ropec", [128, T], F32)
    ropes = din("ropes", [128, T], F32)
    identb_h = din("identb", [128, 128], BF16)
    identf_h = din("identf", [128, 128], F32)
    permf_h = din("permf", [128, 128], F32)
    onescolb_h = din("onescolb", [128, 1], BF16)
    onescolf_h = din("onescolf", [128, 1], F32)
    onesrowf_h = din("onesrowf", [1, 128], F32)
    neg20_h = din("neg20", [128, 1], F32)

    y_out = nc.dram_tensor("y", [T, D], F32, kind="ExternalOutput").ap()

    tap_outs = {}
    for tp in taps:
        if tp in ("xn", "xn1", "attn0", "attn1"):
            tap_outs[tp] = nc.dram_tensor("tap_" + tp, [128, DC, T], BF16, kind="ExternalOutput").ap()
        elif tp in ("x_ffn", "x_l0", "x_moe", "x_sh"):
            tap_outs[tp] = nc.dram_tensor("tap_" + tp, [128, DC, T], F32, kind="ExternalOutput").ap()
        elif tp in ("qsc", "qsc1"):
            tap_outs[tp] = nc.dram_tensor("tap_" + tp, [D, T], BF16, kind="ExternalOutput").ap()
        elif tp in ("kown", "vown", "kown1", "vown1"):
            tap_outs[tp] = nc.dram_tensor("tap_" + tp, [DC * 8 * 128, 128], BF16, kind="ExternalOutput").ap()
        elif tp == "wtm":
            tap_outs[tp] = nc.dram_tensor("tap_wtm", [128, 8, 16], F32, kind="ExternalOutput").ap()
        elif tp == "rsc":
            tap_outs[tp] = nc.dram_tensor("tap_rsc", [128, 8, 16], F32, kind="ExternalOutput").ap()
        elif tp == "esbb":
            tap_outs[tp] = nc.dram_tensor("tap_esbb", [128, 16], F32, kind="ExternalOutput").ap()

    kown0 = nc.dram_tensor("kown0", [DC * 8 * 128, 128], BF16, kind="Internal")
    vown0 = nc.dram_tensor("vown0", [DC * 8 * 128, 128], BF16, kind="Internal")
    kown1 = nc.dram_tensor("kown1", [DC * 8 * 128, 128], BF16, kind="Internal")
    vown1 = nc.dram_tensor("vown1", [DC * 8 * 128, 128], BF16, kind="Internal")
    qsc = nc.dram_tensor("qsc", [D, T], BF16, kind="Internal")
    qsc1 = nc.dram_tensor("qsc1", [D, T], BF16, kind="Internal")
    agspace = maybe_share_collective_output_space("AllGather", [list(range(NCORES))])
    KG0 = nc.dram_tensor("KG0", [NCORES * DC * 8 * 128, 128], BF16, kind="Internal", addr_space=agspace)
    VG0 = nc.dram_tensor("VG0", [NCORES * DC * 8 * 128, 128], BF16, kind="Internal", addr_space=agspace)
    KG1 = nc.dram_tensor("KG1", [NCORES * DC * 8 * 128, 128], BF16, kind="Internal", addr_space=agspace)
    VG1 = nc.dram_tensor("VG1", [NCORES * DC * 8 * 128, 128], BF16, kind="Internal", addr_space=agspace)

    def bview0(t_):
        return t_.ap().rearrange("(dc p tc) x -> dc p tc x", dc=DC, p=128, tc=8)

    def bview1(t_):
        return t_.ap().rearrange("(tc p dc) x -> dc p tc x", tc=8, p=128, dc=DC)

    def gview0(t_):
        return t_.ap().rearrange("(r dc p tc) x -> r dc p tc x", r=NCORES, dc=DC, p=128, tc=8)

    def gview1(t_):
        return t_.ap().rearrange("(r tc p dc) x -> r tc p dc x", r=NCORES, tc=8, p=128, dc=DC)

    kown0v = bview0(kown0)
    vown0v = bview0(vown0)
    kown1v = bview1(kown1)
    vown1v = bview1(vown1)
    KG0v = gview0(KG0)
    VG0v = gview0(VG0)
    KG1v = gview1(KG1)
    VG1v = gview1(VG1)
    qsc1v = qsc1.ap().rearrange("(c p) t -> p c t", p=128)

    with ExitStack() as stack:
        def sb(name, shape, dt):
            return stack.enter_context(nc.sbuf_tensor(name, shape, dt))

        def psmake(name, shape):
            return stack.enter_context(nc.psum_tensor(name, shape, F32))

        xres = sb("xres", [128, DC, T], F32)
        aA = sb("aA", [128, DC, T], BF16)
        aC = sb("aC", [128, 16384], BF16)
        aW = sb("aW", [128, 16384], BF16)
        eA = sb("eA", [128, 4096], F32)
        srow = sb("srow", [1, 2048], F32)
        identb = sb("identb_s", [128, 128], BF16)
        identf = sb("identf_s", [128, 128], F32)
        permf = sb("permf_s", [128, 128], F32)
        onescolb = sb("onescolb_s", [128, 1], BF16)
        onescolf = sb("onescolf_s", [128, 1], F32)
        onesrowf = sb("onesrowf_s", [1, 128], F32)
        neg20 = sb("neg20_s", [128, 1], F32)
        ropec_s = sb("ropec_s", [128, T], F32)
        ropes_s = sb("ropes_s", [128, T], F32)
        sgate_s = sb("sgate_s", [128, 64], F32)
        sup_s = sb("sup_s", [128, 64], F32)
        sdown_s = sb("sdown_s", [128, 16], F32)
        b1q_s = sb("b1q_s", [128, 16], F32)
        b1k_s = sb("b1k_s", [128, 16], F32)
        b1v_s = sb("b1v_s", [128, 16], F32)
        b1o_s = sb("b1o_s", [128, 16], F32)
        gwf = sb("gwf", [128, 16, 16], F32)
        esb_s = sb("esb_s", [1, 16], F32)
        esbb = sb("esbb", [128, 16], F32)
        r_sc = sb("r_sc", [128, 8, 16], F32)
        r_s4 = sb("r_s4", [128, 8, 16], F32)
        r_pr = sb("r_pr", [128, 8, 4, 6], F32)
        r_gt = sb("r_gt", [128, 8, 4], F32)
        r_m2 = sb("r_m2", [128, 8, 1], F32)
        r_gm = sb("r_gm", [128, 8, 16], F32)
        r_wk = sb("r_wk", [128, 8, 16], F32)
        r_ge = sb("r_ge", [128, 8, 16], F32)
        r_ti = sb("r_ti", [128, 8, 1], F32)
        r_sel = sb("r_sel", [128, 8, 16], F32)
        r_w = sb("r_w", [128, 8, 16], F32)

        psA = psmake("psA", [128, 1024])
        psB = psmake("psB", [128, 1024])
        psC = psmake("psC", [128, 1024])
        psD = psmake("psD", [128, 512])
        psE = psmake("psE", [128, 512])

        LANES_Y = ["Lcn", "Lx0", "Lx1", "Ls0", "Ls1", "Ls2", "Ls3", "Ls4", "Ls5",
                   "Lout", "Ltap"]
        LANES_G = ["Lg0", "Lg1", "Lg2", "Lg3", "Lg4", "Lg5"]
        SEMS = {}
        for nm in ["T", "S", "V", "G", "cc"] + LANES_Y + LANES_G:
            SEMS[nm] = stack.enter_context(nc.semaphore("sem" + nm))

        def eT(fn, waits=(), inc=False):
            c = P.bump("T") if inc else None
            P.emit("T", fn, waits, "T" if inc else None, 1)
            return c

        def eS(fn, waits=(), inc=True):
            c = P.bump("S") if inc else None
            P.emit("S", fn, waits, "S" if inc else None, 1)
            return c

        def eV(fn, waits=(), inc=True):
            c = P.bump("V") if inc else None
            P.emit("V", fn, waits, "V" if inc else None, 1)
            return c

        def eG(fn, waits=(), inc=True):
            c = P.bump("G") if inc else None
            P.emit("G", fn, waits, "G" if inc else None, 1)
            return c

        def dY(fn, lane, waits=()):
            c = P.bump(lane, 16)
            P.emit("Y", fn, waits, lane, 16)
            return (lane, c)

        def dG(fn, lane, waits=()):
            c = P.bump(lane, 16)
            P.emit("G", fn, waits, lane, 16)
            return (lane, c)

        def cG(fn, waits=()):
            c = P.bump("cc", 1)
            P.emit("G", fn, waits, "cc", 1)
            return c

        def yWait(waits):
            P.emit("Y", lambda e: None, waits, None, 1)

        RX = [Res() for _ in range(DC)]
        RA = [Res() for _ in range(DC)]
        RE = [Res() for _ in range(4)]
        RSR = [Res() for _ in range(4)]
        RPS = {k: Res() for k in ("A0", "A1", "B0", "B1", "C0", "C1", "D", "E")}
        arC = Arena()
        arW = Arena()
        r_kown0, r_vown0, r_kown1, r_vown1 = Res(), Res(), Res(), Res()
        r_qsc, r_qsc1 = Res(), Res()
        r_KG0, r_VG0, r_KG1, r_VG1 = Res(), Res(), Res(), Res()
        r_route = Res()
        r_wtm = Res()

        PAIR_AP = {"A": psA, "B": psB, "C": psC}

        def pair_res(p):
            return [RPS[p + "0"], RPS[p + "1"]]

        BANK_AP = {
            "A0": psA[:, 0:512], "A1": psA[:, 512:1024],
            "B0": psB[:, 0:512], "B1": psB[:, 512:1024],
            "C0": psC[:, 0:512], "C1": psC[:, 512:1024],
            "D": psD[:, :], "E": psE[:, :],
        }

        e_slot = [eA[:, i * 1024:(i + 1) * 1024] for i in range(4)]
        sr_slot = [srow[0:1, i * 512:(i + 1) * 512] for i in range(4)]

        consts = Res()
        for dst, src in ((identb, identb_h), (identf, identf_h), (permf, permf_h),
                         (onescolb, onescolb_h), (onescolf, onescolf_h),
                         (onesrowf, onesrowf_h), (neg20, neg20_h),
                         (ropec_s, ropec), (ropes_s, ropes), (sgate_s, sgate),
                         (sup_s, sup), (sdown_s, sdown), (b1q_s, b1q), (b1k_s, b1k),
                         (b1v_s, b1v), (b1o_s, b1o), (esb_s, esb)):
            dY(lambda e, d=dst, s=src: e.dma_start(out=d[...], in_=s[...]), "Lcn")
        cgw = dY(lambda e: e.dma_start(out=gwf[:, :, :], in_=gw.rearrange("(c p) n -> p c n", p=128)), "Lcn")
        consts.note_write(cgw)
        CW = consts.r_waits()

        ROT6 = ["A0", "A1", "B0", "B1", "C0", "C1"]
        rot_i = [0]

        def take_slot():
            k = ROT6[rot_i[0] % 6]
            rot_i[0] += 1
            return k

        xv = x_tm.rearrange("(tc p) d -> tc p d", p=128)
        stg = [eA[:, 0:2048], eA[:, 2048:4096]]
        for tc in range(8):
            sl = tc % 2
            ld = dY(lambda e, sl=sl, tc=tc: e.dma_start(out=stg[sl][...], in_=xv[tc]),
                    "Lx0" if sl == 0 else "Lx1",
                    waits=wwait(RE[2 * sl], RE[2 * sl + 1]))
            RE[2 * sl].note_write(ld)
            RE[2 * sl + 1].note_write(ld)
            for g in range(4):
                bk = take_slot()
                bap = BANK_AP[bk]
                waits = [ld] + wwait(RPS[bk]) + CW
                cT = None
                for j in range(4):
                    dc = g * 4 + j
                    cT = eT(lambda e, bap=bap, j=j, sl=sl, dc=dc:
                            e.transpose(bap[:, j * 128:(j + 1) * 128],
                                        stg[sl][:, dc * 128:(dc + 1) * 128], identf[...]),
                            waits=waits if j == 0 else (), inc=(j == 3))
                RPS[bk].note_write(("T", cT))
                RE[2 * sl].note_read(("T", cT))
                RE[2 * sl + 1].note_read(("T", cT))
                cV = eV(lambda e, bap=bap, g=g, tc=tc:
                        e.tensor_copy(xres[:, g * 4:(g + 1) * 4, tc * 128:(tc + 1) * 128],
                                      bap.rearrange("p (a b) -> p a b", a=4)),
                        waits=[("T", cT)] + wwait(*RX[g * 4:(g + 1) * 4]))
                RPS[bk].note_read(("V", cV))
                RE[2 * sl].note_read(("V", cV))
                RE[2 * sl + 1].note_read(("V", cV))
                for dc in range(g * 4, g * 4 + 4):
                    RX[dc].note_write(("V", cV))

        def emit_ln(layer):
            xb_reg = [arC.region(0, 1024), arC.region(1024, 2048)]
            sq_reg = [arC.region(2048, 3072), arC.region(3072, 4096)]
            xb_ap = [aC[:, 0:1024], aC[:, 1024:2048]]
            sq_ap = [aC[:, 2048:3072], aC[:, 3072:4096]]
            sum_banks = [("D", psD[0:1, :]), ("E", psE[0:1, :])]
            ssq_banks = [("C0", psC[0:1, 0:512]), ("C1", psC[0:1, 512:1024])]
            for dc in range(DC):
                sl = dc % 2
                c1 = eS(lambda e, sl=sl, dc=dc: e.copy(xb_ap[sl][...], xres[:, dc, :]),
                        waits=rwait(RX[dc]) + wwait(xb_reg[sl]))
                xb_reg[sl].note_write(("S", c1))
                RX[dc].note_read(("S", c1))
                c2 = eS(lambda e, sl=sl, dc=dc: e.activation(sq_ap[sl][...], xres[:, dc, :], AF.Square),
                        waits=rwait(RX[dc]) + wwait(sq_reg[sl]))
                sq_reg[sl].note_write(("S", c2))
                RX[dc].note_read(("S", c2))
                cT = None
                for th in range(2):
                    bk, bap = sum_banks[th]
                    w = [("S", c1)] + (wwait(RPS[bk]) if dc == 0 else [])
                    eT(lambda e, bap=bap, sl=sl, th=th, dc=dc:
                       e.matmul(bap, onescolb[...], xb_ap[sl][:, th * 512:(th + 1) * 512],
                                start=(dc == 0), stop=(dc == DC - 1)),
                       waits=w)
                    bk2, bap2 = ssq_banks[th]
                    w = [("S", c2)] + (wwait(RPS[bk2]) if dc == 0 else [])
                    cT = eT(lambda e, bap2=bap2, sl=sl, th=th, dc=dc:
                            e.matmul(bap2, onescolb[...], sq_ap[sl][:, th * 512:(th + 1) * 512],
                                     start=(dc == 0), stop=(dc == DC - 1)),
                            waits=w, inc=(th == 1))
                if dc == DC - 1:
                    for bk in ("D", "E", "C0", "C1"):
                        RPS[bk].note_write(("T", cT))
                xb_reg[sl].note_read(("T", cT))
                sq_reg[sl].note_read(("T", cT))
            cm0 = eV(lambda e: e.tensor_scalar(sr_slot[0][...], psD[0:1, :], 1.0 / D, None, ALU.mult),
                     waits=rwait(RPS["D"]) + wwait(RSR[0]))
            cm1 = eV(lambda e: e.tensor_scalar(sr_slot[1][...], psE[0:1, :], 1.0 / D, None, ALU.mult),
                     waits=rwait(RPS["E"]) + wwait(RSR[1]))
            cq = eV(lambda e: e.tensor_scalar(srow[0:1, 1024:2048], psC[0:1, 0:1024], 1.0 / D, None, ALU.mult),
                    waits=rwait(RPS["C0"], RPS["C1"]) + wwait(RSR[2], RSR[3]))
            ct0 = eV(lambda e: e.tensor_tensor(psD[0:1, :], sr_slot[0][...], sr_slot[0][...], ALU.mult),
                     waits=[("V", cm0)])
            ct1 = eV(lambda e: e.tensor_tensor(psE[0:1, :], sr_slot[1][...], sr_slot[1][...], ALU.mult),
                     waits=[("V", cm1)])
            cv0 = eV(lambda e: e.tensor_tensor(srow[0:1, 1024:1536], srow[0:1, 1024:1536], psD[0:1, :], ALU.subtract),
                     waits=[("V", ct0), ("V", cq)])
            cv1 = eV(lambda e: e.tensor_tensor(srow[0:1, 1536:2048], srow[0:1, 1536:2048], psE[0:1, :], ALU.subtract),
                     waits=[("V", ct1), ("V", cq)])
            ce = eV(lambda e: e.tensor_scalar(srow[0:1, 1024:2048], srow[0:1, 1024:2048], LN_EPS, None, ALU.add),
                    waits=[("V", cv0), ("V", cv1)])
            cr0 = eV(lambda e: e.reciprocal(psD[0:1, :], srow[0:1, 1024:1536]), waits=[("V", ce)])
            cr1 = eV(lambda e: e.reciprocal(psE[0:1, :], srow[0:1, 1536:2048]), waits=[("V", ce)])
            cs0 = eS(lambda e: e.activation(srow[0:1, 1024:1536], psD[0:1, :], AF.Sqrt), waits=[("V", cr0)])
            cs1 = eS(lambda e: e.activation(srow[0:1, 1536:2048], psE[0:1, :], AF.Sqrt), waits=[("V", cr1)])
            cmr = eV(lambda e: e.tensor_tensor(srow[0:1, 0:1024], srow[0:1, 0:1024], srow[0:1, 1024:2048], ALU.mult),
                     waits=[("S", cs0), ("S", cs1)])
            RPS["D"].note_write(("V", cr0))
            RPS["E"].note_write(("V", cr1))
            RPS["D"].note_read(("S", cs0))
            RPS["E"].note_read(("S", cs1))
            for r in (RSR[0], RSR[1]):
                r.note_write(("V", cmr))
            for r in (RSR[2], RSR[3]):
                r.note_write(("V", cmr))
            w_ab = [("S", cs1), ("V", cmr)] + wwait(RPS["A0"], RPS["A1"], RPS["B0"], RPS["B1"]) + CW
            cb2 = None
            for th in range(2):
                eT(lambda e, th=th: e.matmul(psA[:, th * 512:(th + 1) * 512], onesrowf[...],
                                             srow[0:1, 1024 + th * 512:1536 + th * 512],
                                             start=True, stop=True),
                   waits=w_ab if th == 0 else ())
                cb2 = eT(lambda e, th=th: e.matmul(psB[:, th * 512:(th + 1) * 512], onesrowf[...],
                                                   srow[0:1, th * 512:(th + 1) * 512],
                                                   start=True, stop=True),
                         inc=(th == 1))
            for k in ("A0", "A1", "B0", "B1"):
                RPS[k].note_write(("T", cb2))
            for r in RSR:
                r.note_read(("T", cb2))
            if layer == 1:
                cbe = eT(lambda e: e.matmul(psD[:, 0:16], onesrowf[...], esb_s[0:1, :], start=True, stop=True),
                         waits=wwait(RPS["D"]) + CW, inc=True)
                cse = eS(lambda e: e.copy(esbb[:, :], psD[:, 0:16]), waits=[("T", cbe)])
                RPS["D"].note_write(("T", cbe))
                RPS["D"].note_read(("S", cse))
                r_route.note_write(("S", cse))
            for dc in range(DC):
                tsl = dc % 2
                c1 = eV(lambda e, tsl=tsl, dc=dc: e.tensor_tensor(e_slot[tsl][...], xres[:, dc, :], psA[:, :], ALU.mult),
                        waits=rwait(RX[dc]) + [("T", cb2)] + wwait(RE[tsl]))
                RE[tsl].note_write(("V", c1))
                RX[dc].note_read(("V", c1))
                if layer == 0:
                    c2 = eV(lambda e, tsl=tsl, dc=dc: e.tensor_tensor(aA[:, dc, :], e_slot[tsl][...], psB[:, :], ALU.subtract),
                            waits=[("V", c1)] + wwait(RA[dc]))
                    RA[dc].note_write(("V", c2))
                    RE[tsl].note_read(("V", c2))
                else:
                    fsl = 2 + (dc % 2)
                    c2 = eV(lambda e, fsl=fsl, tsl=tsl: e.tensor_tensor(e_slot[fsl][...], e_slot[tsl][...], psB[:, :], ALU.subtract),
                            waits=[("V", c1)] + wwait(RE[fsl]))
                    RE[fsl].note_write(("V", c2))
                    RE[tsl].note_read(("V", c2))
                    c3 = eS(lambda e, fsl=fsl, dc=dc: e.copy(aA[:, dc, :], e_slot[fsl][...]),
                            waits=[("V", c2)] + wwait(RA[dc]))
                    RA[dc].note_write(("S", c3))
                    RE[fsl].note_read(("S", c3))
                    w0 = [("V", c2)] + CW
                    if dc == 0:
                        w0 += wwait(RPS["D"])
                    cr = None
                    for tc in range(8):
                        cr = eT(lambda e, fsl=fsl, tc=tc, dc=dc:
                                e.matmul(psD[:, tc * 16:(tc + 1) * 16],
                                         e_slot[fsl][:, tc * 128:(tc + 1) * 128],
                                         gwf[:, dc, :],
                                         start=(dc == 0 and tc == 0), stop=(dc == DC - 1),
                                         skip_group_check=True),
                                waits=w0 if tc == 0 else (), inc=(tc == 7))
                    RE[fsl].note_read(("T", cr))
                    if dc == DC - 1:
                        RPS["D"].note_write(("T", cr))
                for k in ("A0", "A1", "B0", "B1"):
                    RPS[k].note_read(("V", c2))

        if stages >= 1:
            emit_ln(0)

        def tap_sbuf(name, src_ap, waits):
            if name in taps:
                dY(lambda e, s=src_ap, o=tap_outs[name]: e.dma_start(out=o[...], in_=s), "Ltap", waits=waits)

        def tap_dram(name, dram, res):
            if name in taps:
                dY(lambda e, d=dram, o=tap_outs[name]: e.dma_start(out=o[...], in_=d.ap()),
                   "Ltap", waits=rwait(res))

        if stages >= 1:
            tap_sbuf("xn", aA[:, :, :], rwait(*RA))

        WSLOT_OFF = [0, 8192]

        def wslab_ap(off):
            return aW[:, off:off + 8192].rearrange("p (c o) -> p c o", c=16)

        arW_slots = [arW.region(0, 8192), arW.region(8192, 16384)]

        def emit_proj(w_hbm, epilogue):
            pair_rot = ["A", "B", "C"]
            pi = [0]
            for s in range(4):
                sslot = s % 2
                off = WSLOT_OFF[sslot]
                wreg = arW_slots[sslot]
                cl = dG(lambda e, s=s, off=off: e.dma_start(out=wslab_ap(off)[...],
                                                            in_=w_hbm[s]),
                        "Lg0" if sslot == 0 else "Lg1",
                        waits=wwait(wreg))
                wreg.note_write(cl)
                for j in range(4):
                    dc = s * 4 + j
                    pk = pair_rot[pi[0] % 3]
                    pi[0] += 1
                    pap = PAIR_AP[pk]
                    w = [cl] + wwait(*pair_res(pk)) + rwait(*RA)
                    cnt = None
                    for hc in range(DC):
                        for th in range(2):
                            cnt = eT(lambda e, pap=pap, off=off, hc=hc, j=j, th=th:
                                     e.matmul(pap[:, th * 512:(th + 1) * 512],
                                              wslab_ap(off)[:, hc, j * 128:(j + 1) * 128],
                                              aA[:, hc, th * 512:(th + 1) * 512],
                                              start=(hc == 0), stop=(hc == DC - 1)),
                                     waits=w if (hc == 0 and th == 0) else (),
                                     inc=(hc == DC - 1 and th == 1))
                    for r in pair_res(pk):
                        r.note_write(("T", cnt))
                    for r in RA:
                        r.note_read(("T", cnt))
                    wreg.note_read(("T", cnt))
                    epilogue(dc, pk, cnt)

        def kv_regions():
            return {
                "kst": ([arC.region(0, 1024), arC.region(1024, 2048)],
                        [aC[:, 0:1024], aC[:, 1024:2048]]),
                "vst": ([arC.region(2048, 3072), arC.region(3072, 4096)],
                        [aC[:, 2048:3072], aC[:, 3072:4096]]),
                "qst": ([arC.region(4096, 5120), arC.region(5120, 6144)],
                        [aC[:, 4096:5120], aC[:, 5120:6144]]),
                "vstg": ([arC.region(6144, 7168), arC.region(7168, 8192)],
                         [aC[:, 6144:7168].rearrange("p (a b) -> p a b", a=8),
                          aC[:, 7168:8192].rearrange("p (a b) -> p a b", a=8)]),
            }

        def rope_epilogue(dst_regs, dst_aps, store_fn):
            def ep(dc, pk, cnt):
                sl = dc % 2
                pap = PAIR_AP[pk]
                qf = e_slot[sl]
                c1 = eS(lambda e, qf=qf, pap=pap: e.copy(qf[...], pap[:, :]),
                        waits=[("T", cnt)] + wwait(RE[sl]))
                RE[sl].note_write(("S", c1))
                for r in pair_res(pk):
                    r.note_read(("S", c1))
                cp = None
                for th, bk in ((0, "D"), (1, "E")):
                    cp = eT(lambda e, qf=qf, th=th, bk=bk:
                            e.matmul(BANK_AP[bk], permf[...], qf[:, th * 512:(th + 1) * 512],
                                     start=True, stop=True),
                            waits=[("S", c1)] + wwait(RPS[bk]) + CW, inc=(th == 1))
                RPS["D"].note_write(("T", cp))
                RPS["E"].note_write(("T", cp))
                RE[sl].note_read(("T", cp))
                ct1a = eV(lambda e: e.tensor_tensor(e_slot[2][:, 0:512], psD[:, :], ropes_s[:, 0:512], ALU.mult),
                          waits=[("T", cp)] + wwait(RE[2]) + CW)
                ct1b = eV(lambda e: e.tensor_tensor(e_slot[2][:, 512:1024], psE[:, :], ropes_s[:, 512:1024], ALU.mult),
                          waits=[("T", cp)])
                RE[2].note_write(("V", ct1b))
                RPS["D"].note_read(("V", ct1a))
                RPS["E"].note_read(("V", ct1b))
                ct2 = eV(lambda e, qf=qf: e.tensor_tensor(e_slot[3][...], qf[...], ropec_s[:, :], ALU.mult),
                         waits=[("S", c1)] + wwait(RE[3]))
                RE[3].note_write(("V", ct2))
                RE[sl].note_read(("V", ct2))
                cfin = eV(lambda e, sl=sl: e.tensor_tensor(dst_aps[sl][...], e_slot[2][...], e_slot[3][...], ALU.add),
                          waits=[("V", ct1b), ("V", ct2)] + wwait(dst_regs[sl]))
                dst_regs[sl].note_write(("V", cfin))
                RE[2].note_read(("V", cfin))
                RE[3].note_read(("V", cfin))
                store_fn(dc, sl, ("V", cfin))
            return ep

        def blk_store(dst_regs, dst_aps, dram_view, dram_res):
            def store(dc, sl, evt):
                cdma = dY(lambda e, sl=sl, dc=dc: e.dma_start(
                    out=dram_view[dc],
                    in_=dst_aps[sl][...].rearrange("p (tc x) -> p tc x", tc=8)),
                    "Ls0" if sl == 0 else "Ls1",
                    waits=[evt])
                dst_regs[sl].note_read(cdma)
                dram_res.note_write(cdma)
            return store

        def row_store(dst_regs, dst_aps, qdram, dram_res):
            def store(dc, sl, evt):
                cdma = dY(lambda e, sl=sl, dc=dc: e.dma_start(
                    out=qdram[dc * 128:(dc + 1) * 128, :], in_=dst_aps[sl][...]),
                    "Ls4" if sl == 0 else "Ls5",
                    waits=[evt])
                dst_regs[sl].note_read(cdma)
                dram_res.note_write(cdma)
            return store

        def relu_epilogue(bias_col, dst_regs, dst_aps, store_fn):
            def ep(dc, pk, cnt):
                sl = dc % 2
                pap = PAIR_AP[pk]
                c1 = eS(lambda e, sl=sl, pap=pap, dc=dc:
                        e.activation(dst_aps[sl][...], pap[:, :], AF.Relu,
                                     bias=bias_col[:, dc:dc + 1], scale=1.0),
                        waits=[("T", cnt)] + wwait(dst_regs[sl]) + CW)
                dst_regs[sl].note_write(("S", c1))
                for r in pair_res(pk):
                    r.note_read(("S", c1))
                store_fn(dc, sl, ("S", c1))
            return ep

        def vtrans_epilogue(vst_regs, vst_aps, stg_regs, stg_aps, dram_view, dram_res,
                            relu_bias=None):
            def ep(dc, pk, cnt):
                sl = dc % 2
                pap = PAIR_AP[pk]
                if relu_bias is None:
                    c1 = eS(lambda e, sl=sl, pap=pap: e.copy(vst_aps[sl][...], pap[:, :]),
                            waits=[("T", cnt)] + wwait(vst_regs[sl]))
                else:
                    c1 = eS(lambda e, sl=sl, pap=pap, dc=dc:
                            e.activation(vst_aps[sl][...], pap[:, :], AF.Relu,
                                         bias=relu_bias[:, dc:dc + 1], scale=1.0),
                            waits=[("T", cnt)] + wwait(vst_regs[sl]) + CW)
                vst_regs[sl].note_write(("S", c1))
                for r in pair_res(pk):
                    r.note_read(("S", c1))
                ctr3 = ctr7 = None
                for tc in range(8):
                    bk = "D" if tc < 4 else "E"
                    w = [("S", c1)] + (wwait(RPS[bk]) if tc % 4 == 0 else []) + CW
                    cc_ = eT(lambda e, sl=sl, tc=tc, bk=bk:
                             e.matmul(BANK_AP[bk][:, (tc % 4) * 128:(tc % 4 + 1) * 128],
                                      vst_aps[sl][:, tc * 128:(tc + 1) * 128], identb[...],
                                      start=(tc % 4 == 0), stop=(tc % 4 == 3),
                                      skip_group_check=True),
                             waits=w, inc=(tc % 4 == 3))
                    if tc == 3:
                        ctr3 = cc_
                    if tc == 7:
                        ctr7 = cc_
                RPS["D"].note_write(("T", ctr3))
                RPS["E"].note_write(("T", ctr7))
                vst_regs[sl].note_read(("T", ctr7))
                ssl = dc % 2
                c2 = eS(lambda e, ssl=ssl: e.copy(stg_aps[ssl][:, 0:4, :],
                                                  psD[:, :].rearrange("p (a b) -> p a b", a=4)),
                        waits=[("T", ctr3)] + wwait(stg_regs[ssl]))
                c3 = eS(lambda e, ssl=ssl: e.copy(stg_aps[ssl][:, 4:8, :],
                                                  psE[:, :].rearrange("p (a b) -> p a b", a=4)),
                        waits=[("T", ctr7)])
                stg_regs[ssl].note_write(("S", c3))
                RPS["D"].note_read(("S", c2))
                RPS["E"].note_read(("S", c3))
                cdma = dY(lambda e, ssl=ssl, dc=dc: e.dma_start(
                    out=dram_view[dc], in_=stg_aps[ssl][...]),
                    "Ls2" if ssl == 0 else "Ls3",
                    waits=[("S", c3)])
                stg_regs[ssl].note_read(cdma)
                dram_res.note_write(cdma)
            return ep

        if stages >= 2:
            kv = kv_regions()
            emit_proj(wkT, rope_epilogue(kv["kst"][0], kv["kst"][1],
                                         blk_store(kv["kst"][0], kv["kst"][1], kown0v, r_kown0)))
            if stages >= 2.3:
                emit_proj(wvT, vtrans_epilogue(kv["vst"][0], kv["vst"][1],
                                               kv["vstg"][0], kv["vstg"][1], vown0v, r_vown0))
            if stages >= 2.6:
                emit_proj(wqT, rope_epilogue(kv["qst"][0], kv["qst"][1],
                                             row_store(kv["qst"][0], kv["qst"][1], qsc.ap(), r_qsc)))
            if stages >= 2.9:
                ck = cG(lambda e: e.collective_compute(
                    "AllGather", ALU.bypass, replica_groups=[list(range(NCORES))],
                    ins=[kown0.ap().opt()], outs=[KG0.ap().opt()]),
                    waits=rwait(r_kown0))
                r_KG0.note_write(("cc", ck))
                cv = cG(lambda e: e.collective_compute(
                    "AllGather", ALU.bypass, replica_groups=[list(range(NCORES))],
                    ins=[vown0.ap().opt()], outs=[VG0.ap().opt()]),
                    waits=rwait(r_vown0))
                r_VG0.note_write(("cc", cv))
            tap_dram("qsc", qsc, r_qsc)
            tap_dram("kown", kown0, r_kown0)
            tap_dram("vown", vown0, r_vown0)

        def emit_ffn(wg_h, wu_h, wd_h, inter, sg_col, su_col, sd_col):
            nslab = inter // 512
            fuse_reg = arC.region(0, 4096)
            fuse_ap = aC[:, 0:4096].rearrange("p (a b) -> p a b", a=4)
            wd_reg = arC.region(4096, 12288)
            wd_ap = aC[:, 4096:12288].rearrange("p (a b) -> p a b", a=4)
            gt_regs = [arC.region(12288, 12800), arC.region(12800, 13312)]
            gt_aps = [aC[:, 12288:12800], aC[:, 12800:13312]]
            ut_regs = [arC.region(13312, 13824), arC.region(13824, 14336)]
            ut_aps = [aC[:, 13312:13824], aC[:, 13824:14336]]
            wg_reg = arW_slots[0]
            wu_reg = arW_slots[1]
            for i in range(nslab):
                clg = dG(lambda e, i=i: e.dma_start(out=wslab_ap(0)[...],
                                                    in_=wg_h[i]),
                         "Lg0", waits=wwait(wg_reg))
                wg_reg.note_write(clg)
                clu = dG(lambda e, i=i: e.dma_start(out=wslab_ap(8192)[...],
                                                    in_=wu_h[i]),
                         "Lg1", waits=wwait(wu_reg))
                wu_reg.note_write(clu)
                cld = dG(lambda e, i=i: e.dma_start(out=wd_ap[...],
                                                    in_=wd_h[i]),
                         "Lg2", waits=wwait(wd_reg))
                wd_reg.note_write(cld)
                for oc in range(4):
                    og = i * 4 + oc
                    for th in range(2):
                        gsl = th
                        gk = take_slot()
                        w = [clg] + wwait(RPS[gk]) + rwait(*RA)
                        cg = None
                        for hc in range(DC):
                            cg = eT(lambda e, gk=gk, hc=hc, oc=oc, th=th:
                                    e.matmul(BANK_AP[gk], wslab_ap(0)[:, hc, oc * 128:(oc + 1) * 128],
                                             aA[:, hc, th * 512:(th + 1) * 512],
                                             start=(hc == 0), stop=(hc == DC - 1)),
                                    waits=w if hc == 0 else (), inc=(hc == DC - 1))
                        RPS[gk].note_write(("T", cg))
                        for r in RA:
                            r.note_read(("T", cg))
                        wg_reg.note_read(("T", cg))
                        if sg_col is not None:
                            cgs = eS(lambda e, gk=gk, gsl=gsl, og=og:
                                     e.activation(gt_aps[gsl][...], BANK_AP[gk], AF.Silu,
                                                  scale=sg_col[:, og:og + 1]),
                                     waits=[("T", cg)] + wwait(gt_regs[gsl]) + CW)
                        else:
                            cgs = eS(lambda e, gk=gk, gsl=gsl:
                                     e.activation(gt_aps[gsl][...], BANK_AP[gk], AF.Silu),
                                     waits=[("T", cg)] + wwait(gt_regs[gsl]))
                        gt_regs[gsl].note_write(("S", cgs))
                        RPS[gk].note_read(("S", cgs))
                        uk = take_slot()
                        w = [clu] + wwait(RPS[uk])
                        cu = None
                        for hc in range(DC):
                            cu = eT(lambda e, uk=uk, hc=hc, oc=oc, th=th:
                                    e.matmul(BANK_AP[uk], wslab_ap(8192)[:, hc, oc * 128:(oc + 1) * 128],
                                             aA[:, hc, th * 512:(th + 1) * 512],
                                             start=(hc == 0), stop=(hc == DC - 1)),
                                    waits=w if hc == 0 else (), inc=(hc == DC - 1))
                        RPS[uk].note_write(("T", cu))
                        for r in RA:
                            r.note_read(("T", cu))
                        wu_reg.note_read(("T", cu))
                        if su_col is not None:
                            cus = eV(lambda e, uk=uk, gsl=gsl, og=og:
                                     e.tensor_scalar(ut_aps[gsl][...], BANK_AP[uk],
                                                     su_col[:, og:og + 1], None, ALU.mult),
                                     waits=[("T", cu)] + wwait(ut_regs[gsl]) + CW)
                        else:
                            cus = eV(lambda e, uk=uk, gsl=gsl:
                                     e.tensor_copy(ut_aps[gsl][...], BANK_AP[uk]),
                                     waits=[("T", cu)] + wwait(ut_regs[gsl]))
                        ut_regs[gsl].note_write(("V", cus))
                        RPS[uk].note_read(("V", cus))
                        cf = eV(lambda e, gsl=gsl, oc=oc, th=th:
                                e.tensor_tensor(fuse_ap[:, oc, th * 512:(th + 1) * 512],
                                                gt_aps[gsl][...], ut_aps[gsl][...], ALU.mult),
                                waits=[("S", cgs), ("V", cus)] +
                                      (wwait(fuse_reg) if (oc == 0 and th == 0) else []))
                        gt_regs[gsl].note_read(("V", cf))
                        ut_regs[gsl].note_read(("V", cf))
                        if oc == 3 and th == 1:
                            fuse_reg.note_write(("V", cf))
                fuse_done = P.cnt["V"]
                for hc in range(DC):
                    for th in range(2):
                        dk = take_slot()
                        w = [cld, ("V", fuse_done)] + wwait(RPS[dk])
                        cd = None
                        for oc in range(4):
                            cd = eT(lambda e, dk=dk, oc=oc, hc=hc, th=th:
                                    e.matmul(BANK_AP[dk], wd_ap[:, oc, hc * 128:(hc + 1) * 128],
                                             fuse_ap[:, oc, th * 512:(th + 1) * 512],
                                             start=(oc == 0), stop=(oc == 3)),
                                    waits=w if oc == 0 else (), inc=(oc == 3))
                        RPS[dk].note_write(("T", cd))
                        wd_reg.note_read(("T", cd))
                        fuse_reg.note_read(("T", cd))
                        if sd_col is not None:
                            ca = eV(lambda e, dk=dk, hc=hc, th=th:
                                    e.scalar_tensor_tensor(xres[:, hc, th * 512:(th + 1) * 512],
                                                           BANK_AP[dk], sd_col[:, hc:hc + 1],
                                                           xres[:, hc, th * 512:(th + 1) * 512],
                                                           ALU.mult, ALU.add),
                                    waits=[("T", cd)] + wwait(RX[hc]) + CW)
                        else:
                            ca = eV(lambda e, dk=dk, hc=hc, th=th:
                                    e.tensor_tensor(xres[:, hc, th * 512:(th + 1) * 512],
                                                    BANK_AP[dk],
                                                    xres[:, hc, th * 512:(th + 1) * 512], ALU.add),
                                    waits=[("T", cd)] + wwait(RX[hc]))
                        RX[hc].note_write(("V", ca))
                        RPS[dk].note_read(("V", ca))

        if stages >= 3:
            emit_ffn(wg8, wu8, wd8, 8192, sgate_s, sup_s, sdown_s)
            tap_sbuf("x_ffn", xres[:, :, :], rwait(*RX))

        def emit_attn0():
            qh_regs = [arC.region(0, 1024), arC.region(1024, 2048)]
            qh_aps = [aC[:, 0:1024], aC[:, 1024:2048]]
            ks_regs = [arC.region(2048 + i * 1024, 3072 + i * 1024) for i in range(3)]
            ks_aps = [aC[:, 2048 + i * 1024:3072 + i * 1024].rearrange("p (a b) -> p a b", a=8)
                      for i in range(3)]
            vs_regs = [arC.region(5120 + i * 1024, 6144 + i * 1024) for i in range(3)]
            vs_aps = [aC[:, 5120 + i * 1024:6144 + i * 1024].rearrange("p (a b) -> p a b", a=8)
                      for i in range(3)]
            pt_regs = [arC.region(8192 + i * 1024, 9216 + i * 1024) for i in range(4)]
            pt_aps = [aC[:, 8192 + i * 1024:9216 + i * 1024] for i in range(4)]
            STP = {0: (psA, pair_res("A")), 1: (psB, pair_res("B"))}
            for h in range(16):
                qsl = h % 2
                cq = dY(lambda e, h=h, qsl=qsl: e.dma_start(out=qh_aps[qsl][...],
                                                            in_=qsc.ap()[h * 128:(h + 1) * 128, :]),
                        "Lx0" if qsl == 0 else "Lx1",
                        waits=wwait(qh_regs[qsl]) + rwait(r_qsc))
                qh_regs[qsl].note_write(cq)
                exp_cnt = {}
                pend = None

                def zpv(pend):
                    psl, ssl, kc, cvl, cexp, kcg = pend
                    wz = [("S", cexp)]
                    if kcg == 0:
                        wz += wwait(RPS["D"], RPS["E"], RPS["C0"], RPS["C1"])
                    eT(lambda e, psl=psl, kcg=kcg: e.matmul(psD[0:1, :], onescolb[...],
                                                            pt_aps[psl][:, 0:512],
                                                            start=(kcg == 0), stop=(kcg == 63)),
                       waits=wz)
                    eT(lambda e, psl=psl, kcg=kcg: e.matmul(psE[0:1, :], onescolb[...],
                                                            pt_aps[psl][:, 512:1024],
                                                            start=(kcg == 0), stop=(kcg == 63)))
                    eT(lambda e, psl=psl, kcg=kcg, ssl=ssl, kc=kc:
                       e.matmul(psC[:, 0:512], vs_aps[ssl][:, kc, :], pt_aps[psl][:, 0:512],
                                start=(kcg == 0), stop=(kcg == 63)),
                       waits=[cvl])
                    cpv = eT(lambda e, psl=psl, kcg=kcg, ssl=ssl, kc=kc:
                             e.matmul(psC[:, 512:1024], vs_aps[ssl][:, kc, :], pt_aps[psl][:, 512:1024],
                                      start=(kcg == 0), stop=(kcg == 63)), inc=True)
                    pt_regs[psl].note_read(("T", cpv))
                    vs_regs[ssl].note_read(("T", cpv))

                for r in range(8):
                    ssl = r % 3
                    ck = dY(lambda e, r=r, h=h, ssl=ssl: e.dma_start(
                        out=ks_aps[ssl][...], in_=KG0v[r, h]),
                        "Ls0" if r % 2 == 0 else "Ls1",
                        waits=wwait(ks_regs[ssl]) + rwait(r_KG0))
                    ks_regs[ssl].note_write(ck)
                    cvl = dY(lambda e, r=r, h=h, ssl=ssl: e.dma_start(
                        out=vs_aps[ssl][...], in_=VG0v[r, h]),
                        "Ls2" if r % 2 == 0 else "Ls3",
                        waits=wwait(vs_regs[ssl]) + rwait(r_VG0))
                    vs_regs[ssl].note_write(cvl)
                    for kc in range(8):
                        kcg = r * 8 + kc
                        stp, stres = STP[kcg % 2]
                        w = [ck, cq] + wwait(*stres)
                        eT(lambda e, stp=stp, ssl=ssl, kc=kc, qsl=qsl:
                           e.matmul(stp[:, 0:512], ks_aps[ssl][:, kc, :], qh_aps[qsl][:, 0:512],
                                    start=True, stop=True),
                           waits=w)
                        cst = eT(lambda e, stp=stp, ssl=ssl, kc=kc, qsl=qsl:
                                 e.matmul(stp[:, 512:1024], ks_aps[ssl][:, kc, :], qh_aps[qsl][:, 512:1024],
                                          start=True, stop=True), inc=True)
                        for rr in stres:
                            rr.note_write(("T", cst))
                        ks_regs[ssl].note_read(("T", cst))
                        qh_regs[qsl].note_read(("T", cst))
                        psl = kcg % 4
                        cexp = eS(lambda e, stp=stp, psl=psl:
                                  e.activation(pt_aps[psl][...], stp[:, :], AF.Exp,
                                               bias=neg20[:, :], scale=INV_HD0),
                                  waits=[("T", cst)] + wwait(pt_regs[psl]) + CW)
                        exp_cnt[kcg] = cexp
                        pt_regs[psl].note_write(("S", cexp))
                        for rr in stres:
                            rr.note_read(("S", cexp))
                        if pend is not None:
                            zpv(pend)
                        pend = (psl, ssl, kc, cvl, cexp, kcg)
                zpv(pend)
                pend = None
                cpv63 = P.cnt["T"]
                for bk in ("D", "E", "C0", "C1"):
                    RPS[bk].note_write(("T", cpv63))
                cr0 = eV(lambda e: e.reciprocal(sr_slot[0][...], psD[0:1, :]),
                         waits=[("T", cpv63)] + wwait(RSR[0]))
                cr1 = eV(lambda e: e.reciprocal(sr_slot[1][...], psE[0:1, :]),
                         waits=[("T", cpv63)] + wwait(RSR[1]))
                RSR[0].note_write(("V", cr0))
                RSR[1].note_write(("V", cr1))
                RPS["D"].note_read(("V", cr0))
                RPS["E"].note_read(("V", cr1))
                cb = None
                wb = [("V", cr1), ("S", exp_cnt[63])] + wwait(RPS["A0"], RPS["A1"])
                for th in range(2):
                    cb = eT(lambda e, th=th: e.matmul(psA[:, th * 512:(th + 1) * 512], onesrowf[...],
                                                      sr_slot[th][...], start=True, stop=True),
                            waits=wb if th == 0 else (), inc=(th == 1))
                RPS["A0"].note_write(("T", cb))
                RPS["A1"].note_write(("T", cb))
                RSR[0].note_read(("T", cb))
                RSR[1].note_read(("T", cb))
                crc = eS(lambda e: e.copy(e_slot[2][...], psA[:, :]),
                         waits=[("T", cb)] + wwait(RE[2]))
                RE[2].note_write(("S", crc))
                RPS["A0"].note_read(("S", crc))
                RPS["A1"].note_read(("S", crc))
                cn = eV(lambda e, h=h: e.tensor_tensor(aA[:, h, :], psC[:, :], e_slot[2][...], ALU.mult),
                        waits=[("T", cpv63), ("S", crc)] + wwait(RA[h]))
                RA[h].note_write(("V", cn))
                RE[2].note_read(("V", cn))
                RPS["C0"].note_read(("V", cn))
                RPS["C1"].note_read(("V", cn))

        if stages >= 4:
            emit_attn0()
            tap_sbuf("attn0", aA[:, :, :], rwait(*RA))

        def emit_oproj(w_hbm, relu_bias=None):
            def ep(dc, pk, cnt):
                pap = PAIR_AP[pk]
                if relu_bias is None:
                    ca = eV(lambda e, pap=pap, dc=dc: e.tensor_tensor(
                        xres[:, dc, :], pap[:, :], xres[:, dc, :], ALU.add),
                        waits=[("T", cnt)] + wwait(RX[dc]))
                    RX[dc].note_write(("V", ca))
                    for r in pair_res(pk):
                        r.note_read(("V", ca))
                else:
                    esl = dc % 2
                    cs = eS(lambda e, pap=pap, dc=dc, esl=esl: e.activation(
                        e_slot[esl][...], pap[:, :], AF.Relu,
                        bias=relu_bias[:, dc:dc + 1], scale=1.0),
                        waits=[("T", cnt)] + wwait(RE[esl]) + CW)
                    RE[esl].note_write(("S", cs))
                    for r in pair_res(pk):
                        r.note_read(("S", cs))
                    ca = eV(lambda e, dc=dc, esl=esl: e.tensor_tensor(
                        xres[:, dc, :], e_slot[esl][...], xres[:, dc, :], ALU.add),
                        waits=[("S", cs)] + wwait(RX[dc]))
                    RX[dc].note_write(("V", ca))
                    RE[esl].note_read(("V", ca))
            emit_proj(w_hbm, ep)

        if stages >= 5:
            emit_oproj(woT)
            tap_sbuf("x_l0", xres[:, :, :], rwait(*RX))

        if stages >= 6:
            emit_ln(1)
            rw0 = rwait(RPS["D"]) + rwait(r_route)
            c1 = eS(lambda e: e.activation(r_sc[:, :, :],
                                           psD[:, 0:128].rearrange("p (a b) -> p a b", a=8),
                                           AF.Sigmoid), waits=rw0)
            RPS["D"].note_read(("S", c1))
            c2 = eV(lambda e: e.tensor_tensor(r_s4[:, :, :], r_sc[:, :, :],
                                              esbb[:, :].unsqueeze(1).broadcast_to((128, 8, 16)),
                                              ALU.add),
                    waits=[("S", c1)] + rwait(r_route))
            s4g = r_s4[:, :, :].rearrange("p a (g e) -> p a g e", g=4)
            c3 = eV(lambda e: e.tensor_tensor(r_pr[:, :, :, 0:3], s4g[:, :, :, 0:3], s4g[:, :, :, 1:4], ALU.add),
                    waits=[("V", c2)])
            c4 = eV(lambda e: e.tensor_tensor(r_pr[:, :, :, 3:5], s4g[:, :, :, 0:2], s4g[:, :, :, 2:4], ALU.add),
                    waits=[("V", c3)])
            c5 = eV(lambda e: e.tensor_tensor(r_pr[:, :, :, 5:6], s4g[:, :, :, 0:1], s4g[:, :, :, 3:4], ALU.add),
                    waits=[("V", c4)])
            c6 = eV(lambda e: e.tensor_reduce(r_gt[:, :, :], r_pr[:, :, :, :], axis=AXX, op=ALU.max),
                    waits=[("V", c5)])
            c7 = eV(lambda e: e.tensor_tensor(r_pr[:, :, 0, 0:3], r_gt[:, :, 0:3], r_gt[:, :, 1:4], ALU.min),
                    waits=[("V", c6)])
            c8 = eV(lambda e: e.tensor_tensor(r_pr[:, :, 0, 3:5], r_gt[:, :, 0:2], r_gt[:, :, 2:4], ALU.min),
                    waits=[("V", c7)])
            c9 = eV(lambda e: e.tensor_tensor(r_pr[:, :, 0, 5:6], r_gt[:, :, 0:1], r_gt[:, :, 3:4], ALU.min),
                    waits=[("V", c8)])
            c10 = eV(lambda e: e.tensor_reduce(r_m2[:, :, :], r_pr[:, :, 0, :], axis=AXX, op=ALU.max),
                     waits=[("V", c9)])
            c11 = eV(lambda e: e.tensor_tensor(r_gt[:, :, :], r_gt[:, :, :],
                                               r_m2[:, :, :].broadcast_to((128, 8, 4)), ALU.is_ge),
                     waits=[("V", c10)])
            c12 = eV(lambda e: e.tensor_copy(r_gm[:, :, :].rearrange("p a (g e) -> p a g e", g=4),
                                             r_gt[:, :, :].unsqueeze(3).broadcast_to((128, 8, 4, 4))),
                     waits=[("V", c11)])
            c13a = eV(lambda e: e.tensor_tensor(r_wk[:, :, :], r_s4[:, :, :], r_gm[:, :, :], ALU.mult),
                      waits=[("V", c12)])
            c13b = eV(lambda e: e.tensor_scalar(r_gm[:, :, :], r_gm[:, :, :], -1.0, None, ALU.add),
                      waits=[("V", c13a)])
            c14 = eV(lambda e: e.scalar_tensor_tensor(r_gm[:, :, :], r_gm[:, :, :], 1e30,
                                                      r_wk[:, :, :], ALU.mult, ALU.add),
                     waits=[("V", c13b)])
            prev = eV(lambda e: e.tensor_copy(r_wk[:, :, :], r_gm[:, :, :]), waits=[("V", c14)])
            for it in range(3):
                ct = eV(lambda e: e.tensor_reduce(r_ti[:, :, :], r_wk[:, :, :], axis=AXX, op=ALU.max),
                        waits=[("V", prev)])
                cg_ = eV(lambda e: e.tensor_tensor(r_ge[:, :, :], r_wk[:, :, :],
                                                   r_ti[:, :, :].broadcast_to((128, 8, 16)), ALU.is_ge),
                         waits=[("V", ct)])
                prev = eV(lambda e: e.scalar_tensor_tensor(r_wk[:, :, :], r_ge[:, :, :], -1e30,
                                                           r_wk[:, :, :], ALU.mult, ALU.add),
                          waits=[("V", cg_)])
            cti = eV(lambda e: e.tensor_reduce(r_ti[:, :, :], r_wk[:, :, :], axis=AXX, op=ALU.max),
                     waits=[("V", prev)])
            csel = eV(lambda e: e.tensor_tensor(r_sel[:, :, :], r_gm[:, :, :],
                                                r_ti[:, :, :].broadcast_to((128, 8, 16)), ALU.is_ge),
                      waits=[("V", cti)])
            cwr = eV(lambda e: e.tensor_tensor(r_w[:, :, :], r_sc[:, :, :], r_sel[:, :, :], ALU.mult),
                     waits=[("V", csel)])
            czz = eV(lambda e: e.tensor_reduce(r_ti[:, :, :], r_w[:, :, :], axis=AXX, op=ALU.add),
                     waits=[("V", cwr)])
            cz2 = eV(lambda e: e.tensor_scalar(r_ti[:, :, :], r_ti[:, :, :], 1e-20, None, ALU.add),
                     waits=[("V", czz)])
            cz3 = eV(lambda e: e.reciprocal(r_m2[:, :, :], r_ti[:, :, :]), waits=[("V", cz2)])
            cz4 = eV(lambda e: e.tensor_tensor(r_w[:, :, :], r_w[:, :, :],
                                               r_m2[:, :, :].broadcast_to((128, 8, 16)), ALU.mult),
                     waits=[("V", cz3)])
            cz5 = eV(lambda e: e.tensor_scalar(r_w[:, :, :], r_w[:, :, :], 2.5, None, ALU.mult),
                     waits=[("V", cz4)])
            r_wtm.note_write(("V", cz5))
            if "wtm" in taps:
                dY(lambda e: e.dma_start(out=tap_outs["wtm"][...], in_=r_w[:, :, :]),
                   "Ltap", waits=rwait(r_wtm))
            if "rsc" in taps:
                dY(lambda e: e.dma_start(out=tap_outs["rsc"][...], in_=r_sc[:, :, :]),
                   "Ltap", waits=rwait(r_wtm))
            if "esbb" in taps:
                dY(lambda e: e.dma_start(out=tap_outs["esbb"][...], in_=esbb[:, :]),
                   "Ltap", waits=rwait(r_wtm))
            tap_sbuf("xn1", aA[:, :, :], rwait(*RA))

        if stages >= 7:
            kv = kv_regions()
            emit_proj(w1kT, relu_epilogue(b1k_s, kv["kst"][0], kv["kst"][1],
                                          blk_store(kv["kst"][0], kv["kst"][1], kown1v, r_kown1)))
            emit_proj(w1vT, vtrans_epilogue(kv["vst"][0], kv["vst"][1],
                                            kv["vstg"][0], kv["vstg"][1], vown1v, r_vown1,
                                            relu_bias=b1v_s))
            emit_proj(w1qT, relu_epilogue(b1q_s, kv["qst"][0], kv["qst"][1],
                                          row_store(kv["qst"][0], kv["qst"][1], qsc1.ap(), r_qsc1)))
            ck = cG(lambda e: e.collective_compute(
                "AllGather", ALU.bypass, replica_groups=[list(range(NCORES))],
                ins=[kown1.ap().opt()], outs=[KG1.ap().opt()]),
                waits=rwait(r_kown1))
            r_KG1.note_write(("cc", ck))
            cv = cG(lambda e: e.collective_compute(
                "AllGather", ALU.bypass, replica_groups=[list(range(NCORES))],
                ins=[vown1.ap().opt()], outs=[VG1.ap().opt()]),
                waits=rwait(r_vown1))
            r_VG1.note_write(("cc", cv))
            tap_dram("qsc1", qsc1, r_qsc1)
            tap_dram("kown1", kown1, r_kown1)
            tap_dram("vown1", vown1, r_vown1)

        def emit_moe():
            fuse_reg = arC.region(0, 8192)
            fuse_ap = aC[:, 0:8192].rearrange("p (a b) -> p a b", a=8)
            gt_regs = [arC.region(8192, 8704), arC.region(8704, 9216)]
            gt_aps = [aC[:, 8192:8704], aC[:, 8704:9216]]
            ut_regs = [arC.region(9216, 9728), arC.region(9728, 10240)]
            ut_aps = [aC[:, 9216:9728], aC[:, 9728:10240]]
            wg_regs = [arW.region(0, 4096), arW.region(4096, 8192)]
            wg_off = [0, 4096]
            wu_regs = [arW.region(8192, 12288), arW.region(12288, 16384)]
            wu_off = [8192, 12288]
            dn_regs = [arC.region(10240, 12288), arC.region(12288, 14336)]
            dn_off = [10240, 12288]

            def wpiece(off):
                return aW[:, off:off + 4096].rearrange("p (c o) -> p c o", c=16)

            def dpiece(off):
                return aC[:, off:off + 2048].rearrange("p (c o) -> p c o", c=8)

            for n in range(16):
                wex = rwait(r_wtm) + wwait(RPS["D"], RPS["E"]) + CW
                cx = None
                for tc in range(8):
                    bk = "D" if tc < 4 else "E"
                    cx = eT(lambda e, tc=tc, n=n, bk=bk:
                            e.matmul(BANK_AP[bk][0:1, (tc % 4) * 128:(tc % 4 + 1) * 128],
                                     r_w[:, tc, n:n + 1], identf[...],
                                     start=(tc % 4 == 0), stop=(tc % 4 == 3),
                                     skip_group_check=True),
                            waits=wex if tc == 0 else (), inc=(tc == 7))
                RPS["D"].note_write(("T", cx))
                RPS["E"].note_write(("T", cx))
                r_wtm.note_read(("T", cx))
                cw1 = eS(lambda e: e.copy(sr_slot[0][...], psD[0:1, :]),
                         waits=[("T", cx)] + wwait(RSR[0]))
                cw2 = eS(lambda e: e.copy(sr_slot[1][...], psE[0:1, :]),
                         waits=[("T", cx)] + wwait(RSR[1]))
                RSR[0].note_write(("S", cw1))
                RSR[1].note_write(("S", cw2))
                RPS["D"].note_read(("S", cw1))
                RPS["E"].note_read(("S", cw2))
                cb = None
                wbc = [("S", cw1), ("S", cw2)]
                for th in range(2):
                    bk = "D" if th == 0 else "E"
                    cb = eT(lambda e, th=th, bk=bk: e.matmul(BANK_AP[bk], onesrowf[...],
                                                             sr_slot[th][...], start=True, stop=True),
                            waits=wbc if th == 0 else (), inc=(th == 1))
                RPS["D"].note_write(("T", cb))
                RPS["E"].note_write(("T", cb))
                RSR[0].note_read(("T", cb))
                RSR[1].note_read(("T", cb))
                cbw = eS(lambda e: e.copy(e_slot[3][:, 0:512], psD[:, :]),
                         waits=[("T", cb)] + wwait(RE[3]))
                cbw2 = eS(lambda e: e.copy(e_slot[3][:, 512:1024], psE[:, :]))
                RE[3].note_write(("S", cbw2))
                RPS["D"].note_read(("S", cbw))
                RPS["E"].note_read(("S", cbw2))
                for fq in range(4):
                    gsl = fq % 2
                    clg = dG(lambda e, n=n, fq=fq, gsl=gsl: e.dma_start(
                        out=wpiece(wg_off[gsl])[...], in_=mgT[n, fq]),
                        "Lg0" if gsl == 0 else "Lg1",
                        waits=wwait(wg_regs[gsl]))
                    wg_regs[gsl].note_write(clg)
                    clu = dG(lambda e, n=n, fq=fq, gsl=gsl: e.dma_start(
                        out=wpiece(wu_off[gsl])[...], in_=muT[n, fq]),
                        "Lg2" if gsl == 0 else "Lg3",
                        waits=wwait(wu_regs[gsl]))
                    wu_regs[gsl].note_write(clu)
                    for fc in range(2):
                        fcg = fq * 2 + fc
                        for th in range(2):
                            tsl = th
                            gk = take_slot()
                            w = [clg] + wwait(RPS[gk]) + rwait(*RA)
                            cg2 = None
                            for hc in range(DC):
                                cg2 = eT(lambda e, gk=gk, gsl=gsl, hc=hc, fc=fc, th=th:
                                         e.matmul(BANK_AP[gk], wpiece(wg_off[gsl])[:, hc, fc * 128:(fc + 1) * 128],
                                                  aA[:, hc, th * 512:(th + 1) * 512],
                                                  start=(hc == 0), stop=(hc == DC - 1)),
                                         waits=w if hc == 0 else (), inc=(hc == DC - 1))
                            RPS[gk].note_write(("T", cg2))
                            for r in RA:
                                r.note_read(("T", cg2))
                            wg_regs[gsl].note_read(("T", cg2))
                            cgs = eS(lambda e, gk=gk, tsl=tsl: e.activation(gt_aps[tsl][...], BANK_AP[gk], AF.Silu),
                                     waits=[("T", cg2)] + wwait(gt_regs[tsl]))
                            gt_regs[tsl].note_write(("S", cgs))
                            RPS[gk].note_read(("S", cgs))
                            uk = take_slot()
                            w = [clu] + wwait(RPS[uk])
                            cu2 = None
                            for hc in range(DC):
                                cu2 = eT(lambda e, uk=uk, gsl=gsl, hc=hc, fc=fc, th=th:
                                         e.matmul(BANK_AP[uk], wpiece(wu_off[gsl])[:, hc, fc * 128:(fc + 1) * 128],
                                                  aA[:, hc, th * 512:(th + 1) * 512],
                                                  start=(hc == 0), stop=(hc == DC - 1)),
                                         waits=w if hc == 0 else (), inc=(hc == DC - 1))
                            RPS[uk].note_write(("T", cu2))
                            for r in RA:
                                r.note_read(("T", cu2))
                            wu_regs[gsl].note_read(("T", cu2))
                            cus = eV(lambda e, uk=uk, tsl=tsl, th=th:
                                     e.tensor_tensor(ut_aps[tsl][...], BANK_AP[uk],
                                                     e_slot[3][:, th * 512:(th + 1) * 512], ALU.mult),
                                     waits=[("T", cu2), ("S", cbw2)] + wwait(ut_regs[tsl]))
                            ut_regs[tsl].note_write(("V", cus))
                            RPS[uk].note_read(("V", cus))
                            RE[3].note_read(("V", cus))
                            cf = eV(lambda e, tsl=tsl, fcg=fcg, th=th:
                                    e.tensor_tensor(fuse_ap[:, fcg, th * 512:(th + 1) * 512],
                                                    gt_aps[tsl][...], ut_aps[tsl][...], ALU.mult),
                                    waits=[("S", cgs), ("V", cus)] +
                                          (wwait(fuse_reg) if (fcg == 0 and th == 0) else []))
                            gt_regs[tsl].note_read(("V", cf))
                            ut_regs[tsl].note_read(("V", cf))
                            if fcg == 7 and th == 1:
                                fuse_reg.note_write(("V", cf))
                fuse_done = P.cnt["V"]
                for hs in range(8):
                    dsl = hs % 2
                    cld = dG(lambda e, n=n, hs=hs, dsl=dsl: e.dma_start(
                        out=dpiece(dn_off[dsl])[...], in_=mdT[n, hs]),
                        "Lg4" if dsl == 0 else "Lg5",
                        waits=wwait(dn_regs[dsl]))
                    dn_regs[dsl].note_write(cld)
                    for hcl in range(2):
                        hc = hs * 2 + hcl
                        for th in range(2):
                            dk = take_slot()
                            w = [cld, ("V", fuse_done)] + wwait(RPS[dk])
                            cd = None
                            for fc in range(8):
                                cd = eT(lambda e, dk=dk, dsl=dsl, fc=fc, hcl=hcl, th=th:
                                        e.matmul(BANK_AP[dk], dpiece(dn_off[dsl])[:, fc, hcl * 128:(hcl + 1) * 128],
                                                 fuse_ap[:, fc, th * 512:(th + 1) * 512],
                                                 start=(fc == 0), stop=(fc == 7)),
                                        waits=w if fc == 0 else (), inc=(fc == 7))
                            RPS[dk].note_write(("T", cd))
                            dn_regs[dsl].note_read(("T", cd))
                            fuse_reg.note_read(("T", cd))
                            ca = eV(lambda e, dk=dk, hc=hc, th=th:
                                    e.tensor_tensor(xres[:, hc, th * 512:(th + 1) * 512],
                                                    BANK_AP[dk],
                                                    xres[:, hc, th * 512:(th + 1) * 512], ALU.add),
                                    waits=[("T", cd)] + wwait(RX[hc]))
                            RX[hc].note_write(("V", ca))
                            RPS[dk].note_read(("V", ca))

        if stages >= 8:
            emit_moe()
            tap_sbuf("x_moe", xres[:, :, :], rwait(*RX))

        if stages >= 9:
            emit_ffn(shgT, shuT, shdT, 2048, None, None, None)
            tap_sbuf("x_sh", xres[:, :, :], rwait(*RX))

        def emit_attn1():
            q2_reg = arC.region(0, 8192)
            q2_ap = aC[:, 0:8192].rearrange("p (a b) -> p a b", a=8)
            kb_regs = [arC.region(8192 + i * 1024, 9216 + i * 1024) for i in range(3)]
            kb_aps = [aC[:, 8192 + i * 1024:9216 + i * 1024].rearrange("p (a b) -> p a b", a=8)
                      for i in range(3)]
            vb_regs = [arC.region(11264 + i * 1024, 12288 + i * 1024) for i in range(3)]
            vb_aps = [aC[:, 11264 + i * 1024:12288 + i * 1024].rearrange("p (a b) -> p a b", a=8)
                      for i in range(3)]
            pt_regs = [arC.region(14336 + i * 256, 14592 + i * 256) for i in range(4)]
            pt_aps = [aC[:, 14336 + i * 256:14592 + i * 256] for i in range(4)]
            OUT2 = {0: psA[:, 0:256], 1: psA[:, 256:512], 2: psA[:, 512:768], 3: psA[:, 768:1024],
                    4: psB[:, 0:256], 5: psB[:, 256:512], 6: psB[:, 512:768], 7: psB[:, 768:1024]}
            O2RES = {0: "A0", 1: "A0", 2: "A1", 3: "A1", 4: "B0", 5: "B0", 6: "B1", 7: "B1"}
            SC4 = [psC[:, i * 256:(i + 1) * 256] for i in range(4)]
            RQ = [Res() for _ in range(4)]
            for i, bk in ((0, "C0"), (1, "C0"), (2, "C1"), (3, "C1")):
                RQ[i].rs = list(RPS[bk].w_waits())
            zacc = e_slot[3][:, 0:256]
            zrow = srow[0:1, 1024:1280]
            for h in range(2):
                cq = dY(lambda e, h=h: e.dma_start(
                    out=q2_ap[...], in_=qsc1v[:, h * 8:(h + 1) * 8, :]),
                    "Lx0", waits=wwait(q2_reg) + rwait(r_qsc1))
                q2_reg.note_write(cq)
                for qt in range(4):
                    lastZ = None
                    for r in range(8):
                        for tc in range(8):
                            kcg = r * 8 + tc
                            bsl = kcg % 3
                            ck = dY(lambda e, r=r, h=h, tc=tc, bsl=bsl: e.dma_start(
                                out=kb_aps[bsl][...],
                                in_=KG1v[r, tc, :, h * 8:(h + 1) * 8, :]),
                                "Ls0" if kcg % 2 == 0 else "Ls1",
                                waits=wwait(kb_regs[bsl]) + rwait(r_KG1))
                            kb_regs[bsl].note_write(ck)
                            cvl = dG(lambda e, r=r, h=h, tc=tc, bsl=bsl: e.dma_start(
                                out=vb_aps[bsl][...],
                                in_=VG1v[r, tc, :, h * 8:(h + 1) * 8, :]),
                                "Lg4" if kcg % 2 == 0 else "Lg5",
                                waits=wwait(vb_regs[bsl]) + rwait(r_VG1))
                            vb_regs[bsl].note_write(cvl)
                            sq = kcg % 4
                            w = [ck, cq] + wwait(RQ[sq])
                            cst = None
                            for dcl in range(8):
                                cst = eT(lambda e, sq=sq, bsl=bsl, dcl=dcl, qt=qt:
                                         e.matmul(SC4[sq], kb_aps[bsl][:, dcl, :],
                                                  q2_ap[:, dcl, qt * 256:(qt + 1) * 256],
                                                  start=(dcl == 0), stop=(dcl == 7),
                                                  skip_group_check=True),
                                         waits=w if dcl == 0 else (), inc=(dcl == 7))
                            RQ[sq].note_write(("T", cst))
                            kb_regs[bsl].note_read(("T", cst))
                            q2_reg.note_read(("T", cst))
                            psl = kcg % 4
                            cexp = eS(lambda e, sq=sq, psl=psl:
                                      e.activation(pt_aps[psl][...], SC4[sq], AF.Exp,
                                                   bias=neg20[:, :], scale=INV_HD1),
                                      waits=[("T", cst)] + wwait(pt_regs[psl]) + CW)
                            pt_regs[psl].note_write(("S", cexp))
                            RQ[sq].note_read(("S", cexp))
                            if kcg == 0:
                                cz = eV(lambda e, psl=psl: e.tensor_copy(zacc[...], pt_aps[psl][...]),
                                        waits=[("S", cexp)] + wwait(RE[3]))
                            else:
                                cz = eV(lambda e, psl=psl: e.tensor_tensor(
                                    zacc[...], zacc[...], pt_aps[psl][...], ALU.add),
                                    waits=[("S", cexp)] + wwait(RE[3]))
                            RE[3].note_write(("V", cz))
                            pt_regs[psl].note_read(("V", cz))
                            lastZ = cz
                            wz = [("S", cexp), cvl]
                            if kcg == 0:
                                wz += wwait(RPS["A0"], RPS["A1"], RPS["B0"], RPS["B1"])
                            cpv = None
                            for dcl in range(8):
                                cpv = eT(lambda e, psl=psl, kcg=kcg, bsl=bsl, dcl=dcl:
                                         e.matmul(OUT2[dcl], vb_aps[bsl][:, dcl, :], pt_aps[psl][...],
                                                  start=(kcg == 0 and dcl % 2 == 0),
                                                  stop=(kcg == 63 and dcl % 2 == 1),
                                                  skip_group_check=True),
                                         waits=wz if dcl == 0 else (), inc=(dcl == 7))
                            pt_regs[psl].note_read(("T", cpv))
                            vb_regs[bsl].note_read(("T", cpv))
                    cpv63 = P.cnt["T"]
                    for bk in ("A0", "A1", "B0", "B1"):
                        RPS[bk].note_write(("T", cpv63))
                    czf = eT(lambda e: e.matmul(psD[0:1, 0:256], onescolf[...], zacc[...],
                                                start=True, stop=True, skip_group_check=True),
                             waits=[("V", lastZ)] + wwait(RPS["D"]), inc=True)
                    RPS["D"].note_write(("T", czf))
                    RE[3].note_read(("T", czf))
                    crz = eS(lambda e: e.copy(zrow[...], psD[0:1, 0:256]),
                             waits=[("T", czf)] + wwait(RSR[2]))
                    RSR[2].note_write(("S", crz))
                    RPS["D"].note_read(("S", crz))
                    cbx = eT(lambda e: e.matmul(psE[:, 0:256], onesrowf[...], zrow[...],
                                                start=True, stop=True),
                             waits=[("S", crz)] + wwait(RPS["E"]), inc=True)
                    RPS["E"].note_write(("T", cbx))
                    RSR[2].note_read(("T", cbx))
                    crc = eV(lambda e: e.reciprocal(e_slot[2][:, 0:256], psE[:, 0:256]),
                             waits=[("T", cbx)] + wwait(RE[2]))
                    RE[2].note_write(("V", crc))
                    RPS["E"].note_read(("V", crc))
                    cn = None
                    for dcl in range(8):
                        wn = [("T", cpv63), ("V", crc)]
                        if qt == 0:
                            wn += wwait(RA[h * 8 + dcl])
                        cn = eV(lambda e, h=h, dcl=dcl, qt=qt: e.tensor_tensor(
                            aA[:, h * 8 + dcl, qt * 256:(qt + 1) * 256],
                            OUT2[dcl], e_slot[2][:, 0:256], ALU.mult),
                            waits=wn)
                        RPS[O2RES[dcl]].note_read(("V", cn))
                        RE[2].note_read(("V", cn))
                        if qt == 3:
                            RA[h * 8 + dcl].note_write(("V", cn))
            for bk in ("C0", "C1"):
                RPS[bk].note_write(("T", P.cnt["T"]))

        if stages >= 10:
            emit_attn1()
            tap_sbuf("attn1", aA[:, :, :], rwait(*RA))

        if stages >= 11:
            emit_oproj(w1oT, relu_bias=b1o_s)

        ost_ap = eA[:, 0:2048]
        for tc in range(8):
            cVlast = None
            for g in range(4):
                bk = take_slot()
                bap = BANK_AP[bk]
                waits = wwait(RPS[bk]) + CW
                for dc in range(g * 4, g * 4 + 4):
                    waits += rwait(RX[dc])
                cT = None
                for j in range(4):
                    dc = g * 4 + j
                    cT = eT(lambda e, bap=bap, j=j, dc=dc, tc=tc:
                            e.transpose(bap[:, j * 128:(j + 1) * 128],
                                        xres[:, dc, tc * 128:(tc + 1) * 128], identf[...]),
                            waits=waits if j == 0 else (), inc=(j == 3))
                RPS[bk].note_write(("T", cT))
                for dc in range(g * 4, g * 4 + 4):
                    RX[dc].note_read(("T", cT))
                cVlast = eV(lambda e, bap=bap, g=g: e.tensor_copy(ost_ap[:, g * 512:(g + 1) * 512], bap),
                            waits=[("T", cT)] + (wwait(RE[0], RE[1]) if g == 0 else []))
                RPS[bk].note_read(("V", cVlast))
            cd = dY(lambda e, tc=tc: e.dma_start(out=y_out[tc * 128:(tc + 1) * 128, :],
                                                 in_=ost_ap[...]),
                    "Lout", waits=[("V", cVlast)])
            RE[0].note_write(("V", cVlast))
            RE[1].note_write(("V", cVlast))
            RE[0].note_read(cd)
            RE[1].note_read(cd)
        yWait([(ln, P.cnt[ln]) for ln in LANES_Y if P.cnt.get(ln)])

        with nc.Block() as block:
            def runner(ek):
                lastw = {}

                def go(eng):
                    for waits, fn, inc, by in P.ops[ek]:
                        for s, v in waits:
                            if v > lastw.get(s, -1):
                                eng.wait_ge(SEMS[s], v)
                                lastw[s] = v
                        ins = fn(eng)
                        if inc is not None and ins is not None:
                            ins.then_inc(SEMS[inc], by)
                return go

            block.tensor(runner("T"))
            block.scalar(runner("S"))
            block.vector(runner("V"))
            block.gpsimd(runner("G"))
            block.sync(runner("Y"))

    _NC_CACHE[key] = nc
    return nc


def kernel(**inputs):
    maps = _prep(inputs)
    nc = build_nc()
    res = run_bass_kernel_spmd(nc, maps, core_ids=list(range(NCORES)))
    parts = [res.results[c]["y"] for c in range(NCORES)]
    out = np.concatenate(parts, axis=0).reshape(1, SEQ, D).astype(np.float32)
    if TAPS:
        kernel.tap_results = [{k: v for k, v in res.results[c].items() if k.startswith("tap_")}
                              for c in range(NCORES)]
    return out
